# version 03
# baseline (speedup 1.0000x reference)
import numpy as np
from contextlib import ExitStack

import concourse.bass as bass
import concourse.mybir as mybir
from concourse.bass_utils import run_bass_kernel_spmd

F32 = mybir.dt.float32
BF16 = mybir.dt.bfloat16
AF = mybir.ActivationFunctionType
OP = mybir.AluOpType
AF_COPY = getattr(AF, "Copy", None)
if AF_COPY is None:
    AF_COPY = getattr(AF, "Identity")

ENGINES = ("tensor", "vector", "scalar", "sync", "gpsimd")
ENG_SEM = {"tensor": "mt", "vector": "mv", "scalar": "ms", "sync": "dsy", "gpsimd": "dgp"}
COMPUTE = ("tensor", "vector", "scalar")


class Buf:
    __slots__ = ("name", "ready", "readers")

    def __init__(self, name=""):
        self.name = name
        self.ready = None
        self.readers = {}


def bufs(n, name):
    return [Buf(f"{name}{i}") for i in range(n)]


class Em:
    def __init__(self):
        self.ops = {e: [] for e in ENGINES}
        self.cnt = {s: 0 for s in ENG_SEM.values()}
        self.cnt_sbuf = {s: 0 for s in ENG_SEM.values()}
        self.waited = {e: {} for e in ENGINES}
        self.pfence = {e: None for e in ENGINES}

    def emit(self, eng, build, reads=(), writes=(), inc=False, record=True,
             dram_only=False, tag=""):
        own = ENG_SEM[eng]
        waits = {}

        def need(sem, val, why):
            if sem == own and eng in COMPUTE:
                return
            if val is None:
                raise RuntimeError(f"un-inc'd dependency: {why} [{tag}] on {eng}")
            if self.waited[eng].get(sem, 0) >= val:
                return
            if waits.get(sem, 0) < val:
                waits[sem] = val

        if eng in ("sync", "gpsimd"):
            need(own, self.cnt[own], "dma-serialize")
        pf = self.pfence[eng]
        if pf:
            self.pfence[eng] = None
            for s, v in pf.items():
                need(s, v, "fence")
        for b in reads:
            r = b.ready
            if r is None:
                continue
            if r[0] == "eng":
                if r[1] != eng:
                    raise RuntimeError(f"cross-engine read of un-inc'd {b.name} [{tag}]")
            else:
                need(r[0], r[1], f"read {b.name}")
        for b in writes:
            for s, v in b.readers.items():
                if s == own and eng in COMPUTE:
                    continue
                need(s, v, f"WAR {b.name}")
            r = b.ready
            if r is not None:
                if r[0] == "eng":
                    if r[1] != eng:
                        raise RuntimeError(f"WAW cross-engine un-inc'd {b.name} [{tag}]")
                else:
                    need(r[0], r[1], f"WAW {b.name}")
        increc = None
        if inc:
            amt = (16 if eng in ("sync", "gpsimd") else 1) if inc is True else int(inc)
            self.cnt[own] += amt
            if not dram_only:
                self.cnt_sbuf[own] = self.cnt[own]
            increc = (own, amt)
        self.ops[eng].append((tuple(sorted(waits.items())), build, increc, tag))
        for s, v in waits.items():
            self.waited[eng][s] = v
        mark = (own, self.cnt[own]) if inc else ("eng", eng)
        for b in writes:
            b.ready = mark
            b.readers = {}
        if record:
            for b in reads:
                b.readers[own] = self.cnt[own] if inc else None

    def note_readers(self, rbufs, eng):
        own = ENG_SEM[eng]
        for b in rbufs:
            b.readers[own] = self.cnt[own]

    def fence(self):
        snap = {s: v for s, v in self.cnt_sbuf.items() if v > 0}
        for e in ENGINES:
            pf = self.pfence[e] or {}
            keys = set(pf) | set(snap)
            self.pfence[e] = {s: max(pf.get(s, 0), snap.get(s, 0)) for s in keys}

    def replay(self, nc, blk, sems):
        for eng in ENGINES:
            ops = self.ops[eng]

            def make(ops=ops):
                def body(e):
                    for waits, build, increc, tag in ops:
                        for s, v in waits:
                            e.wait_ge(sems[s], v)
                        ins = build(e) if build is not None else None
                        if increc is not None:
                            ins.then_inc(sems[increc[0]], increc[1])
                return body
            getattr(blk, eng)(make())


T = 1024
S = 8192
D = 4096
DH = 1024
F = 11008
NT = 8
NFC = 86
FHC = 43
C_SHIFT = -12.0
SCALE = 1.0 / 32.0


def build_program(dbg=False):
    nc = bass.Bass("TRN2", target_bir_lowering=False)

    d_x = nc.dram_tensor("x", [T, D], F32, kind="ExternalInput").ap()
    d_wq = nc.dram_tensor("wq", [24 * 128, D], F32, kind="ExternalInput").ap()
    d_bqkv = nc.dram_tensor("bqkv", [128, 24], F32, kind="ExternalInput").ap()
    d_wo = nc.dram_tensor("wo", [DH, D], F32, kind="ExternalInput").ap()
    d_bo = nc.dram_tensor("bo", [128, D], F32, kind="ExternalInput").ap()
    d_wg = nc.dram_tensor("wg", [128, NFC * D], F32, kind="ExternalInput").ap()
    d_wu = nc.dram_tensor("wu", [128, NFC * D], F32, kind="ExternalInput").ap()
    d_wd = nc.dram_tensor("wd", [8 * 128, NFC * 512], F32, kind="ExternalInput").ap()
    d_id = nc.dram_tensor("ident", [128, 128], F32, kind="ExternalInput").ap()
    d_out = nc.dram_tensor("out", [T, D], F32, kind="ExternalOutput").ap()

    xk = "ExternalOutput" if dbg else "Internal"
    d_x2 = nc.dram_tensor("x2d", [T, D], BF16, kind=xk).ap()
    d_y0 = nc.dram_tensor("y0d", [T, D], BF16, kind=xk).ap()
    d_wub = nc.dram_tensor("wub", [128, NFC * D], BF16, kind="Internal").ap()
    d_wdb = nc.dram_tensor("wdb", [8 * 128, NFC * 512], BF16, kind="Internal").ap()
    d_wob = nc.dram_tensor("wob", [DH, D], BF16, kind="Internal").ap()
    d_cik = nc.dram_tensor("cc_in_k", [DH, T], BF16, kind="Internal").ap()
    d_civ = nc.dram_tensor("cc_in_v", [T, DH], BF16, kind="Internal").ap()
    d_cok = nc.dram_tensor("cc_out_k", [8 * DH, T], BF16, addr_space="Shared", kind="Internal").ap()
    d_cov = nc.dram_tensor("cc_out_v", [S, DH], BF16, addr_space="Shared", kind="Internal").ap()
    if dbg:
        d_dbg = {
            "d_qfm": nc.dram_tensor("d_qfm", [128, 8 * T], BF16, kind="ExternalOutput").ap(),
            "d_kfm": nc.dram_tensor("d_kfm", [128, 8 * T], BF16, kind="ExternalOutput").ap(),
            "d_vtm": nc.dram_tensor("d_vtm", [128, 8 * T], BF16, kind="ExternalOutput").ap(),
            "d_osb": nc.dram_tensor("d_osb", [128, 8 * T], F32, kind="ExternalOutput").ap(),
            "d_den": nc.dram_tensor("d_den", [128, 8], F32, kind="ExternalOutput").ap(),
            "d_a0": nc.dram_tensor("d_a0", [128, D], F32, kind="ExternalOutput").ap(),
            "d_h0": nc.dram_tensor("d_h0", [128, FHC * T], BF16, kind="ExternalOutput").ap(),
            "d_ln3": nc.dram_tensor("d_ln3", [128, 32 * T], BF16, kind="ExternalOutput").ap(),
            "d_ysb": nc.dram_tensor("d_ysb", [128, 8 * D], BF16, kind="ExternalOutput").ap(),
            "d_xt0": nc.dram_tensor("d_xt0", [128, D], F32, kind="ExternalOutput").ap(),
            "d_ln1t0": nc.dram_tensor("d_ln1t0", [128, D], F32, kind="ExternalOutput").ap(),
            "d_ln1fm": nc.dram_tensor("d_ln1fm", [128, 32 * 512], BF16, kind="ExternalOutput").ap(),
            "d_wq0": nc.dram_tensor("d_wq0", [128, D], BF16, kind="ExternalOutput").ap(),
        }

    EM = Em()
    outer = ExitStack()
    live = {}

    def alloc(name, shape, dtype, kind="sbuf", side="left"):
        if kind == "sbuf":
            cm = nc.sbuf_tensor(name, shape, dtype, side=side)
        else:
            cm = nc.psum_tensor(name, shape, dtype)
        ap = cm.__enter__()
        live[name] = cm
        return ap

    def free(*names):
        for n in names:
            live.pop(n).__exit__(None, None, None)

    with outer:
        sems = {s: outer.enter_context(nc.semaphore(s)) for s in ENG_SEM.values()}

        P = [alloc(f"p{i}", [128, 512], F32, "psum") for i in range(8)]
        Pb = bufs(8, "psum")

        ident = alloc("ident_sb", [128, 128], F32)
        ident_b = Buf("ident")
        c_eps = alloc("c_eps", [128, 1], F32)
        c_s32 = alloc("c_s32", [128, 1], F32)
        c_nb12 = alloc("c_nb12", [128, 1], F32)
        c_i4096 = alloc("c_i4096", [128, 1], F32)
        ones_bf = alloc("ones_bf", [128, 1], BF16)
        consts_b = Buf("consts")
        bqkv_sb = alloc("bqkv_sb", [128, 24], F32)
        bqkv_b = Buf("bqkv")

        scr_bf = alloc("scr_bf", [128, D], BF16)
        scr_b = Buf("scr")
        stn = []
        for i in range(2):
            stn.append({
                "sums": alloc(f"sums_{i}", [128, 2], F32), "sumsb": Buf(f"sums_{i}"),
                "mean": alloc(f"mean_{i}", [128, 1], F32), "meanb": Buf(f"mean_{i}"),
                "msq": alloc(f"msq_{i}", [128, 1], F32), "msqb": Buf(f"msq_{i}"),
                "veps": alloc(f"veps_{i}", [128, 1], F32), "vepsb": Buf(f"veps_{i}"),
                "std": alloc(f"std_{i}", [128, 1], F32), "stdb": Buf(f"std_{i}"),
                "rcp": alloc(f"rcp_{i}", [128, 1], F32), "rcpb": Buf(f"rcp_{i}"),
            })

        def ln_stats(st, src_ap, src_buf, tag):
            EM.emit("vector", lambda v, o=st["sums"][:, 0:1], i_=src_ap:
                    v.tensor_reduce(o, i_, mybir.AxisListType.X, OP.add),
                    reads=[src_buf], writes=[st["sumsb"]], inc=True, tag=tag)
            EM.emit("scalar", lambda s, o=scr_bf[:], i_=src_ap, a=st["sums"][:, 1:2]:
                    s.activation(o, i_, AF.Square, accum_out=a),
                    reads=[src_buf], writes=[scr_b, st["sumsb"]], inc=True, tag=tag)
            EM.emit("scalar", lambda s, o=st["mean"][:], i_=st["sums"][:, 0:1], sc=c_i4096[:, 0:1]:
                    s.activation(o, i_, AF_COPY, scale=sc),
                    reads=[st["sumsb"], consts_b], writes=[st["meanb"]], inc=True, tag=tag)
            EM.emit("scalar", lambda s, o=st["msq"][:], i_=st["mean"][:]:
                    s.activation(o, i_, AF.Square),
                    reads=[st["meanb"]], writes=[st["msqb"]], inc=True, tag=tag)
            EM.emit("vector", lambda v, o=st["veps"][:], i_=st["sums"][:, 1:2], m2=st["msq"][:, 0:1]:
                    v.tensor_scalar(o, i_, 1.0 / D, m2, OP.mult, OP.subtract),
                    reads=[st["sumsb"], st["msqb"]], writes=[st["vepsb"]], inc=True, tag=tag)
            EM.emit("scalar", lambda s, o=st["std"][:], i_=st["veps"][:], b=c_eps[:, 0:1]:
                    s.activation(o, i_, AF.Sqrt, bias=b),
                    reads=[st["vepsb"], consts_b], writes=[st["stdb"]], inc=True, tag=tag)
            EM.emit("vector", lambda v, o=st["rcp"][:], i_=st["std"][:]: v.reciprocal(o, i_),
                    reads=[st["stdb"]], writes=[st["rcpb"]], inc=True, tag=tag)

        def ln_norm(st, out_ap, out_buf, src_ap, src_buf, tag):
            EM.emit("vector",
                    lambda v, o=out_ap, i_=src_ap, b=st["mean"][:, 0:1], sc=st["rcp"][:, 0:1]:
                    v.tensor_scalar(o, i_, b, sc, OP.subtract, OP.mult),
                    reads=[src_buf, st["meanb"], st["rcpb"]], writes=[out_buf], inc=True, tag=tag)

        EM.emit("vector", lambda v: v.memset(c_eps[:], 1e-5), writes=[consts_b], tag="init")
        EM.emit("vector", lambda v: v.memset(c_s32[:], SCALE), writes=[consts_b], tag="init")
        EM.emit("vector", lambda v: v.memset(c_nb12[:], C_SHIFT), writes=[consts_b], tag="init")
        EM.emit("vector", lambda v: v.memset(c_i4096[:], 1.0 / D), writes=[consts_b], tag="init")
        EM.emit("vector", lambda v: v.memset(ones_bf[:], 1.0), writes=[consts_b], inc=True, tag="init")
        EM.emit("sync", lambda sy: sy.dma_start(out=ident[:], in_=d_id),
                writes=[ident_b], inc=True, tag="init")
        EM.emit("sync", lambda sy: sy.dma_start(out=bqkv_sb[:], in_=d_bqkv),
                writes=[bqkv_b], inc=True, tag="init")

        q_fm = alloc("q_fm", [128, 8 * T], BF16)
        q_fm_b = Buf("q_fm")
        k_fm = alloc("k_fm", [128, 8 * T], BF16)
        k_fm_b = Buf("k_fm")
        v_fm = alloc("v_fm", [128, 8 * T], F32)
        v_fm_b = Buf("v_fm")
        v_tm = alloc("v_tm", [128, 8 * T], BF16)
        v_tm_b = Buf("v_tm")
        x_t = [alloc(f"x_t{i}", [128, D], F32) for i in range(2)]
        x_t_b = bufs(2, "x_t")
        ln1_t = [alloc(f"ln1_t{i}", [128, D], F32) for i in range(2)]
        ln1_t_b = bufs(2, "ln1_t")
        ln1_fm = alloc("ln1_fm", [128, 32 * 512], BF16)
        ln1_fm_b = Buf("ln1_fm")
        wq_sl = [alloc(f"wq_sl{i}", [128, D], BF16) for i in range(2)]
        wq_sl_b = bufs(2, "wq_sl")

        def transpose_block(src_ap, src_buf, n_dc, dst_fn, tag):
            for g in range(n_dc // 4):
                bank = 6 + (g % 2)
                for j in range(4):
                    dc = g * 4 + j
                    EM.emit("tensor",
                            lambda t, o=P[bank][:, j * 128:(j + 1) * 128],
                            i_=src_ap[:, dc * 128:(dc + 1) * 128], idn=ident[:]:
                            t.transpose(o, i_, idn),
                            reads=[src_buf, ident_b], writes=[Pb[bank]],
                            inc=(j == 3), record=(j == 3), tag=tag)
                for j in range(4):
                    dc = g * 4 + j
                    dst_ap, dst_buf = dst_fn(dc)
                    EM.emit("vector",
                            lambda v, o=dst_ap, i_=P[bank][:, j * 128:(j + 1) * 128]:
                            v.tensor_copy(o, i_),
                            reads=[Pb[bank]], writes=[dst_buf],
                            inc=(j == 3), record=(j == 3), tag=tag)

        for h in range(2):
            for sl in range(4):
                s_g = h * 4 + sl
                par = s_g % 2
                EM.emit("sync",
                        lambda sy, o=x_t[par][:], i_=d_x[s_g * 128:(s_g + 1) * 128, :]:
                        sy.dma_start(out=o, in_=i_),
                        writes=[x_t_b[par]], inc=True, tag="A_ln")
                st = stn[par]
                ln_stats(st, x_t[par][:], x_t_b[par], "A_ln")
                ln_norm(st, ln1_t[par][:], ln1_t_b[par], x_t[par][:], x_t_b[par], "A_ln")
                if dbg and s_g == 0:
                    EM.emit("sync", lambda sy: sy.dma_start(out=d_dbg["d_xt0"], in_=x_t[0][:]),
                            reads=[x_t_b[0]], inc=True, tag="dbg")
                    EM.emit("sync", lambda sy: sy.dma_start(out=d_dbg["d_ln1t0"], in_=ln1_t[0][:]),
                            reads=[ln1_t_b[0]], inc=True, tag="dbg")
                transpose_block(
                    ln1_t[par][:], ln1_t_b[par], 32,
                    lambda dc, sl=sl: (ln1_fm[:, dc * 512 + sl * 128: dc * 512 + (sl + 1) * 128], ln1_fm_b),
                    "A_tr")
            if dbg and h == 1:
                EM.emit("sync", lambda sy: sy.dma_start(out=d_dbg["d_ln1fm"], in_=ln1_fm[:]),
                        reads=[ln1_fm_b], inc=True, tag="dbg")
            for fc in range(24):
                par = fc % 2
                EM.emit("gpsimd",
                        lambda g, o=wq_sl[par][:], i_=d_wq[fc * 128:(fc + 1) * 128, :]:
                        g.dma_start(out=o, in_=i_),
                        writes=[wq_sl_b[par]], inc=True, tag="A_qkv")
                if dbg and h == 0 and fc == 0:
                    EM.emit("sync", lambda sy: sy.dma_start(out=d_dbg["d_wq0"], in_=wq_sl[0][:]),
                            reads=[wq_sl_b[0]], inc=True, tag="dbg")
                bank = fc % 2
                for dc in range(32):
                    EM.emit("tensor",
                            lambda t, o=P[bank][:, 0:512],
                            l=wq_sl[par][:, dc * 128:(dc + 1) * 128],
                            r=ln1_fm[:, dc * 512:(dc + 1) * 512],
                            st_=(dc == 0), sp=(dc == 31):
                            t.matmul(o, l, r, start=st_, stop=sp, skip_group_check=True),
                            reads=[wq_sl_b[par], ln1_fm_b], writes=[Pb[bank]],
                            inc=(dc == 31), record=(dc == 31), tag="A_qkv")
                if fc < 8:
                    dst_ap, dst_buf, odt = q_fm, q_fm_b, BF16
                    col = fc * T + h * 512
                elif fc < 16:
                    dst_ap, dst_buf, odt = k_fm, k_fm_b, BF16
                    col = (fc - 8) * T + h * 512
                else:
                    dst_ap, dst_buf, odt = v_fm, v_fm_b, F32
                    col = (fc - 16) * T + h * 512
                EM.emit("vector",
                        lambda v, o=dst_ap[:, col:col + 512], i_=P[bank][:, 0:512],
                        b=bqkv_sb[:, fc:fc + 1]:
                        v.tensor_scalar_add(o, i_, b),
                        reads=[Pb[bank], bqkv_b], writes=[dst_buf], inc=True, tag="A_qkv")

        cik_b = bufs(8, "cik")
        for i in range(8):
            EM.emit("sync",
                    lambda sy, o=d_cik[i * 128:(i + 1) * 128, :], i_=k_fm[:, i * T:(i + 1) * T]:
                    sy.dma_start(out=o, in_=i_),
                    reads=[k_fm_b], writes=[cik_b[i]], inc=True, tag="A_cc")
        cok_b = Buf("cok")
        EM.emit("gpsimd",
                lambda g: g.collective_compute(
                    "AllGather", OP.bypass,
                    replica_groups=[[0, 1, 2, 3, 4, 5, 6, 7]],
                    ins=[d_cik], outs=[d_cok]),
                reads=cik_b, writes=[cok_b], inc=1, tag="A_cc")

        for kt in range(8):
            for g2 in range(2):
                bank = 6 + ((kt * 2 + g2) % 2)
                for j in range(4):
                    dhc = g2 * 4 + j
                    EM.emit("tensor",
                            lambda t, o=P[bank][:, j * 128:(j + 1) * 128],
                            i_=v_fm[:, dhc * T + kt * 128: dhc * T + (kt + 1) * 128], idn=ident[:]:
                            t.transpose(o, i_, idn),
                            reads=[v_fm_b, ident_b], writes=[Pb[bank]],
                            inc=(j == 3), record=(j == 3), tag="A_vtr")
                EM.emit("vector",
                        lambda v, o=v_tm[:, kt * T + g2 * 512: kt * T + (g2 + 1) * 512],
                        i_=P[bank][:, 0:512]:
                        v.tensor_copy(o, i_),
                        reads=[Pb[bank]], writes=[v_tm_b], inc=True, tag="A_vtr")

        civ_b = bufs(8, "civ")
        for i in range(8):
            EM.emit("sync",
                    lambda sy, o=d_civ[i * 128:(i + 1) * 128, :], i_=v_tm[:, i * T:(i + 1) * T]:
                    sy.dma_start(out=o, in_=i_),
                    reads=[v_tm_b], writes=[civ_b[i]], inc=True, tag="A_cc")
        cov_b = Buf("cov")
        EM.emit("gpsimd",
                lambda g: g.collective_compute(
                    "AllGather", OP.bypass,
                    replica_groups=[[0, 1, 2, 3, 4, 5, 6, 7]],
                    ins=[d_civ], outs=[d_cov]),
                reads=civ_b, writes=[cov_b], inc=1, tag="A_cc")

        pc_wo = Buf("pc_wo")
        pc_wu = bufs(2, "pc_wu")
        pc_wd = bufs(2, "pc_wd")
        half_cols = FHC * D
        EM.emit("gpsimd", lambda g: g.dma_start(out=d_wob, in_=d_wo),
                writes=[pc_wo], inc=True, dram_only=True, tag="A_pc")
        EM.emit("gpsimd",
                lambda g, o=d_wub[:, 0:half_cols], i_=d_wu[:, 0:half_cols]:
                g.dma_start(out=o, in_=i_),
                writes=[pc_wu[0]], inc=True, dram_only=True, tag="A_pc")

        if dbg:
            EM.emit("sync", lambda sy: sy.dma_start(out=d_dbg["d_qfm"], in_=q_fm[:]),
                    reads=[q_fm_b], inc=True, tag="dbg")
            EM.emit("sync", lambda sy: sy.dma_start(out=d_dbg["d_kfm"], in_=k_fm[:]),
                    reads=[k_fm_b], inc=True, tag="dbg")
            EM.emit("sync", lambda sy: sy.dma_start(out=d_dbg["d_vtm"], in_=v_tm[:]),
                    reads=[v_tm_b], inc=True, tag="dbg")

        EM.fence()
        free("wq_sl1", "wq_sl0", "ln1_fm", "ln1_t1", "ln1_t0", "x_t1", "x_t0",
             "v_tm", "v_fm", "k_fm")

        wo_bf = alloc("wo_bf", [128, 8 * D], BF16, side="right")
        wo_b = bufs(8, "wo")
        o_sb = alloc("o_sb", [128, 8 * T], F32, side="right")
        o_sb_b = Buf("o_sb")
        bo_bc = alloc("bo_bc", [128, D], F32, side="right")
        bo_b = Buf("bo")
        rden = alloc("rden", [128, 8], F32, side="right")
        rden_b = Buf("rden")
        k_sl = [alloc(f"k_sl{i}", [128, 8 * T], BF16) for i in range(2)]
        k_sl_b = [bufs(8, f"k_sl{i}_") for i in range(2)]
        v_sl = [alloc(f"v_sl{i}", [128, 8 * T], BF16) for i in range(2)]
        v_sl_b = [bufs(8, f"v_sl{i}_") for i in range(2)]
        p_t = [alloc(f"p_t{i}", [128, 256], BF16) for i in range(2)]
        p_t_b = bufs(2, "p_t")

        for i in range(8):
            EM.emit("sync",
                    lambda sy, o=wo_bf[:, i * D:(i + 1) * D], i_=d_wob[i * 128:(i + 1) * 128, :]:
                    sy.dma_start(out=o, in_=i_),
                    reads=[pc_wo], writes=[wo_b[i]], inc=True, tag="B")
        EM.emit("sync", lambda sy: sy.dma_start(out=bo_bc[:], in_=d_bo),
                writes=[bo_b], inc=True, tag="B")

        for cb in range(8):
            par = cb % 2
            for i in range(8):
                EM.emit("sync",
                        lambda sy, o=k_sl[par][:, i * T:(i + 1) * T],
                        i_=d_cok[cb * DH + i * 128: cb * DH + (i + 1) * 128, :]:
                        sy.dma_start(out=o, in_=i_),
                        reads=[cok_b], writes=[k_sl_b[par][i]], inc=True, tag="B")
            for i in range(8):
                EM.emit("sync",
                        lambda sy, o=v_sl[par][:, i * T:(i + 1) * T],
                        i_=d_cov[cb * T + i * 128: cb * T + (i + 1) * 128, :]:
                        sy.dma_start(out=o, in_=i_),
                        reads=[cov_b], writes=[v_sl_b[par][i]], inc=True, tag="B")

            for qg in range(4):
                def scores(kt):
                    bank = kt % 2
                    for dhc in range(8):
                        EM.emit("tensor",
                                lambda t, o=P[bank][:, 0:256],
                                l=k_sl[par][:, dhc * T + kt * 128: dhc * T + (kt + 1) * 128],
                                r=q_fm[:, dhc * T + qg * 256: dhc * T + (qg + 1) * 256],
                                st_=(dhc == 0), sp=(dhc == 7):
                                t.matmul(o, l, r, start=st_, stop=sp, skip_group_check=True),
                                reads=[k_sl_b[par][dhc], q_fm_b], writes=[Pb[bank]],
                                inc=(dhc == 7), record=False, tag="B_s")
                    EM.note_readers([k_sl_b[par][d_] for d_ in range(8)] + [q_fm_b], "tensor")
                    EM.emit("scalar",
                            lambda s, o=p_t[bank][:], i_=P[bank][:, 0:256],
                            b=c_nb12[:, 0:1], sc=c_s32[:, 0:1]:
                            s.activation(o, i_, AF.Exp, bias=b, scale=sc),
                            reads=[Pb[bank], consts_b], writes=[p_t_b[bank]], inc=True, tag="B_s")

                def pv_den(kt):
                    bank = kt % 2
                    first_b = (cb == 0 and qg == 0 and kt == 0)
                    for sub in range(2):
                        EM.emit("tensor",
                                lambda t, o=P[6][:, qg * 2 + sub: qg * 2 + sub + 1],
                                l=p_t[bank][:, sub * 128:(sub + 1) * 128], r=ones_bf[:],
                                st_=(first_b and sub == 0),
                                sp=(cb == 7 and qg == 3 and kt == 7 and sub == 1):
                                t.matmul(o, l, r, start=st_, stop=sp, skip_group_check=True),
                                reads=[p_t_b[bank], consts_b], writes=[Pb[6]],
                                inc=(cb == 7 and qg == 3 and kt == 7 and sub == 1),
                                record=False, tag="B_d")
                    for dhc in range(8):
                        ob = 2 + dhc // 2
                        EM.emit("tensor",
                                lambda t, o=P[ob][:, (dhc % 2) * 256:(dhc % 2) * 256 + 256],
                                l=v_sl[par][:, kt * T + dhc * 128: kt * T + (dhc + 1) * 128],
                                r=p_t[bank][:],
                                st_=(kt == 0 and dhc % 2 == 0),
                                sp=(kt == 7 and dhc % 2 == 1):
                                t.matmul(o, l, r, start=st_, stop=sp, skip_group_check=True),
                                reads=[v_sl_b[par][kt], p_t_b[bank]], writes=[Pb[ob]],
                                inc=(dhc == 7 or (kt == 7 and dhc % 2 == 1)),
                                record=False, tag="B_pv")
                    EM.note_readers([v_sl_b[par][kt], p_t_b[bank], consts_b], "tensor")

                scores(0)
                for kt in range(1, 8):
                    scores(kt)
                    pv_den(kt - 1)
                pv_den(7)
                for dhc in range(8):
                    ob = 2 + dhc // 2
                    osl = o_sb[:, dhc * T + qg * 256: dhc * T + (qg + 1) * 256]
                    psl = P[ob][:, (dhc % 2) * 256:(dhc % 2) * 256 + 256]
                    if cb == 0:
                        EM.emit("vector", lambda v, o=osl, i_=psl: v.tensor_copy(o, i_),
                                reads=[Pb[ob]], writes=[o_sb_b], inc=True, tag="B_ev")
                    else:
                        EM.emit("vector", lambda v, o=osl, i_=psl: v.tensor_tensor(o, o, i_, OP.add),
                                reads=[Pb[ob], o_sb_b], writes=[o_sb_b], inc=True, tag="B_ev")

        EM.emit("vector", lambda v: v.reciprocal(rden[:], P[6][:, 0:8]),
                reads=[Pb[6]], writes=[rden_b], inc=True, tag="B")

        if dbg:
            EM.emit("sync", lambda sy: sy.dma_start(out=d_dbg["d_osb"], in_=o_sb[:]),
                    reads=[o_sb_b], inc=True, tag="dbg")
            EM.emit("sync", lambda sy: sy.dma_start(out=d_dbg["d_den"], in_=rden[:]),
                    reads=[rden_b], inc=True, tag="dbg")

        EM.fence()
        free("p_t1", "p_t0", "v_sl1", "v_sl0", "k_sl1", "k_sl0", "q_fm")

        o_bf = [alloc(f"o_bf{i}", [128, 8 * 128], BF16) for i in range(2)]
        o_bf_b = bufs(2, "o_bf")
        a_sb = [alloc(f"a_sb{i}", [128, D], F32) for i in range(2)]
        a_sb_b = bufs(2, "a_sb")
        lnb = [alloc("lnb0", [128, D], F32)]
        lnb_b = bufs(1, "lnb")
        xr_t = [alloc(f"xr_t{i}", [128, D], F32) for i in range(2)]
        xr_t_b = bufs(2, "xr")
        x2_b = bufs(8, "x2")

        for sg in range(8):
            par = sg % 2
            EM.emit("sync",
                    lambda sy, o=xr_t[par][:], i_=d_x[sg * 128:(sg + 1) * 128, :]:
                    sy.dma_start(out=o, in_=i_),
                    writes=[xr_t_b[par]], inc=True, tag="C")
            for dhc in range(8):
                EM.emit("vector",
                        lambda v, o=o_bf[par][:, dhc * 128:(dhc + 1) * 128],
                        i_=o_sb[:, dhc * T + sg * 128: dhc * T + (sg + 1) * 128]:
                        v.tensor_copy(o, i_),
                        reads=[o_sb_b], writes=[o_bf_b[par]],
                        inc=(dhc == 7), record=(dhc == 7), tag="C")
            for cc in range(8):
                bank = cc % 2
                for dhc in range(8):
                    EM.emit("tensor",
                            lambda t, o=P[bank][:, 0:512],
                            l=o_bf[par][:, dhc * 128:(dhc + 1) * 128],
                            r=wo_bf[:, dhc * D + cc * 512: dhc * D + (cc + 1) * 512],
                            st_=(dhc == 0), sp=(dhc == 7):
                            t.matmul(o, l, r, start=st_, stop=sp, skip_group_check=True),
                            reads=[o_bf_b[par], wo_b[dhc]], writes=[Pb[bank]],
                            inc=(dhc == 7), record=False, tag="C_mm")
                EM.note_readers([o_bf_b[par]] + wo_b, "tensor")
                EM.emit("scalar",
                        lambda s, o=a_sb[par][:, cc * 512:(cc + 1) * 512],
                        i_=P[bank][:, 0:512], sc=rden[:, sg:sg + 1]:
                        s.activation(o, i_, AF_COPY, scale=sc),
                        reads=[Pb[bank], rden_b], writes=[a_sb_b[par]], inc=True, tag="C_mm")
            EM.emit("vector", lambda v, o=a_sb[par][:], b=bo_bc[:]:
                    v.tensor_tensor(o, o, b, OP.add),
                    reads=[a_sb_b[par], bo_b], writes=[a_sb_b[par]], inc=True, tag="C")
            if dbg and sg == 0:
                EM.emit("sync", lambda sy: sy.dma_start(out=d_dbg["d_a0"], in_=a_sb[0][:]),
                        reads=[a_sb_b[0]], inc=True, tag="dbg")
            st = stn[par]
            ln_stats(st, a_sb[par][:], a_sb_b[par], "C")
            ln_norm(st, lnb[0][:], lnb_b[0], a_sb[par][:], a_sb_b[par], "C")
            EM.emit("vector", lambda v, o=lnb[0][:], x_=xr_t[par][:]:
                    v.tensor_tensor(o, o, x_, OP.add),
                    reads=[lnb_b[0], xr_t_b[par]], writes=[lnb_b[0]], inc=True, tag="C")
            EM.emit("gpsimd",
                    lambda g, o=d_x2[sg * 128:(sg + 1) * 128, :], i_=lnb[0][:]:
                    g.dma_start(out=o, in_=i_),
                    reads=[lnb_b[0]], writes=[x2_b[sg]], inc=True, tag="C")

        EM.fence()
        free("xr_t1", "xr_t0", "lnb0", "a_sb1", "a_sb0", "o_bf1", "o_bf0",
             "rden", "bo_bc", "o_sb", "wo_bf")

        ln3_fm = alloc("ln3_fm", [128, 32 * T], BF16, side="right")
        ln3_fm_b = Buf("ln3_fm")
        hsb = alloc("hsb", [128, FHC * T], BF16)
        hsb_b = Buf("hsb")
        x2t = [alloc(f"x2t{i}", [128, D], BF16) for i in range(2)]
        x2t_b = bufs(2, "x2t")
        ln3t = [alloc(f"ln3t{i}", [128, D], F32) for i in range(2)]
        ln3t_b = bufs(2, "ln3t")

        for sg in range(8):
            par = sg % 2
            EM.emit("sync",
                    lambda sy, o=x2t[par][:], i_=d_x2[sg * 128:(sg + 1) * 128, :]:
                    sy.dma_start(out=o, in_=i_),
                    reads=[x2_b[sg]], writes=[x2t_b[par]], inc=True, tag="Dp")
            st = stn[par]
            ln_stats(st, x2t[par][:], x2t_b[par], "Dp")
            ln_norm(st, ln3t[par][:], ln3t_b[par], x2t[par][:], x2t_b[par], "Dp")
            transpose_block(
                ln3t[par][:], ln3t_b[par], 32,
                lambda dc, sg=sg: (ln3_fm[:, dc * T + sg * 128: dc * T + (sg + 1) * 128], ln3_fm_b),
                "Dp_tr")
        if dbg:
            EM.emit("sync", lambda sy: sy.dma_start(out=d_dbg["d_ln3"], in_=ln3_fm[:]),
                    reads=[ln3_fm_b], inc=True, tag="dbg")

        EM.fence()
        free("ln3t1", "ln3t0", "x2t1", "x2t0")
        EM.emit("gpsimd",
                lambda g, o=d_wub[:, half_cols:2 * half_cols],
                i_=d_wu[:, half_cols:2 * half_cols]:
                g.dma_start(out=o, in_=i_),
                writes=[pc_wu[1]], inc=True, dram_only=True, tag="Dp")
        for c in range(2):
            EM.emit("gpsimd",
                    lambda g, o=d_wdb[c * 512:(c + 1) * 512, :],
                    i_=d_wd[c * 512:(c + 1) * 512, :]:
                    g.dma_start(out=o, in_=i_),
                    writes=[pc_wd[c]], inc=True, dram_only=True, tag="Dp")
        y0_b = {}
        ysb_b = bufs(8, "ysb")
        WSEG = 22 * 512

        for fh in range(2):
            if fh == 1:
                EM.fence()
            wg_sl = [alloc(f"wg_sl{fh}_{i}", [128, D], BF16) for i in range(2)]
            wg_sl_b = bufs(2, "wg_sl")
            wu_sl = [alloc(f"wu_sl{fh}_{i}", [128, D], BF16) for i in range(2)]
            wu_sl_b = bufs(2, "wu_sl")
            sg_t = [alloc(f"sg_t{fh}_{i}", [128, 512], F32) for i in range(2)]
            sg_t_b = bufs(2, "sg_t")
            for fc in range(FHC):
                fg = fh * FHC + fc
                par = fc % 2
                EM.emit("gpsimd",
                        lambda g, o=wg_sl[par][:], i_=d_wg[:, fg * D:(fg + 1) * D]:
                        g.dma_start(out=o, in_=i_),
                        writes=[wg_sl_b[par]], inc=True, tag="D_gu")
                EM.emit("sync",
                        lambda sy, o=wu_sl[par][:], i_=d_wub[:, fg * D:(fg + 1) * D]:
                        sy.dma_start(out=o, in_=i_),
                        reads=[pc_wu[fh]], writes=[wu_sl_b[par]], inc=True, tag="D_gu")
                for grp in range(2):
                    gb, ub = grp, 2 + grp
                    for dc in range(32):
                        EM.emit("tensor",
                                lambda t, o=P[gb][:, 0:512],
                                l=wg_sl[par][:, dc * 128:(dc + 1) * 128],
                                r=ln3_fm[:, dc * T + grp * 512: dc * T + grp * 512 + 512],
                                st_=(dc == 0), sp=(dc == 31):
                                t.matmul(o, l, r, start=st_, stop=sp, skip_group_check=True),
                                reads=[wg_sl_b[par], ln3_fm_b], writes=[Pb[gb]],
                                inc=(dc == 31), record=False, tag="D_gu")
                    for dc in range(32):
                        EM.emit("tensor",
                                lambda t, o=P[ub][:, 0:512],
                                l=wu_sl[par][:, dc * 128:(dc + 1) * 128],
                                r=ln3_fm[:, dc * T + grp * 512: dc * T + grp * 512 + 512],
                                st_=(dc == 0), sp=(dc == 31):
                                t.matmul(o, l, r, start=st_, stop=sp, skip_group_check=True),
                                reads=[wu_sl_b[par], ln3_fm_b], writes=[Pb[ub]],
                                inc=(dc == 31), record=False, tag="D_gu")
                    EM.note_readers([wg_sl_b[par], wu_sl_b[par], ln3_fm_b], "tensor")
                    EM.emit("scalar",
                            lambda s, o=sg_t[grp][:], i_=P[gb][:, 0:512]:
                            s.activation(o, i_, AF.Silu),
                            reads=[Pb[gb]], writes=[sg_t_b[grp]], inc=True, tag="D_gu")
                    EM.emit("vector",
                            lambda v, o=hsb[:, fc * T + grp * 512: fc * T + grp * 512 + 512],
                            a=sg_t[grp][:], b=P[ub][:, 0:512]:
                            v.tensor_tensor(o, a, b, OP.mult),
                            reads=[sg_t_b[grp], Pb[ub]], writes=[hsb_b], inc=True, tag="D_gu")
            if dbg and fh == 0:
                EM.emit("sync", lambda sy: sy.dma_start(out=d_dbg["d_h0"], in_=hsb[:]),
                        reads=[hsb_b], inc=True, tag="dbg")
            EM.fence()
            free(f"sg_t{fh}_1", f"sg_t{fh}_0", f"wu_sl{fh}_1", f"wu_sl{fh}_0",
                 f"wg_sl{fh}_1", f"wg_sl{fh}_0")
            if fh == 0:
                wd_sl = [alloc(f"wd_sl0_{i}", [128, WSEG], BF16) for i in range(2)]
                wd_sl_b = bufs(2, "wd_sl")
                y0t = [alloc(f"y0t{i}", [128, 512], BF16) for i in range(4)]
                y0t_b = bufs(4, "y0t")
            else:
                free("ln3_fm")
                ysb = alloc("ysb", [128, 8 * D], BF16, side="right")
                wd_sl = [alloc(f"wd_sl1_{i}", [128, WSEG], BF16) for i in range(2)]
                wd_sl_b = bufs(2, "wd_sl")
                y0r = [alloc(f"y0r{i}", [128, 512], BF16) for i in range(2)]
                y0r_b = bufs(2, "y0r")
            for cbk in range(8):
                for h2 in range(2):
                    seg = 22 if h2 == 0 else 21
                    spar = (cbk * 2 + h2) % 2
                    col0 = (fh * FHC + h2 * 22) * 512
                    EM.emit("sync",
                            lambda sy, o=wd_sl[spar][:, 0:seg * 512],
                            i_=d_wdb[cbk * 128:(cbk + 1) * 128, col0:col0 + seg * 512]:
                            sy.dma_start(out=o, in_=i_),
                            reads=[pc_wd[cbk // 4]], writes=[wd_sl_b[spar]], inc=True, tag="D_dn")
                    for fcl in range(seg):
                        fc = h2 * 22 + fcl
                        for sub in range(8):
                            EM.emit("tensor",
                                    lambda t, o=P[sub][:, 0:512],
                                    l=hsb[:, fc * T + sub * 128: fc * T + (sub + 1) * 128],
                                    r=wd_sl[spar][:, fcl * 512:(fcl + 1) * 512],
                                    st_=(fc == 0), sp=(fc == FHC - 1):
                                    t.matmul(o, l, r, start=st_, stop=sp, skip_group_check=True),
                                    reads=[hsb_b, wd_sl_b[spar]], writes=[Pb[sub]],
                                    inc=(sub == 7 or fc == FHC - 1), record=False, tag="D_dn")
                    EM.note_readers([hsb_b, wd_sl_b[spar]], "tensor")
                if fh == 0:
                    for sub in range(8):
                        yt = sub % 4
                        eng = "scalar" if sub % 2 == 0 else "vector"
                        if eng == "scalar":
                            EM.emit("scalar",
                                    lambda s, o=y0t[yt][:], i_=P[sub][:, 0:512]:
                                    s.activation(o, i_, AF_COPY),
                                    reads=[Pb[sub]], writes=[y0t_b[yt]], inc=True, tag="D_dn")
                        else:
                            EM.emit("vector",
                                    lambda v, o=y0t[yt][:], i_=P[sub][:, 0:512]:
                                    v.tensor_copy(o, i_),
                                    reads=[Pb[sub]], writes=[y0t_b[yt]], inc=True, tag="D_dn")
                        b = Buf(f"y0_{sub}_{cbk}")
                        y0_b[(sub, cbk)] = b
                        EM.emit("sync",
                                lambda sy, o=d_y0[sub * 128:(sub + 1) * 128, cbk * 512:(cbk + 1) * 512],
                                i_=y0t[yt][:]:
                                sy.dma_start(out=o, in_=i_),
                                reads=[y0t_b[yt]], writes=[b], inc=True, tag="D_dn")
                else:
                    for sub in range(8):
                        par2 = sub % 2
                        EM.emit("sync",
                                lambda sy, o=y0r[par2][:],
                                i_=d_y0[sub * 128:(sub + 1) * 128, cbk * 512:(cbk + 1) * 512]:
                                sy.dma_start(out=o, in_=i_),
                                reads=[y0_b[(sub, cbk)]], writes=[y0r_b[par2]], inc=True, tag="D_dn")
                        EM.emit("vector",
                                lambda v, o=ysb[:, sub * D + cbk * 512: sub * D + (cbk + 1) * 512],
                                a=y0r[par2][:], b=P[sub][:, 0:512]:
                                v.tensor_tensor(o, a, b, OP.add),
                                reads=[y0r_b[par2], Pb[sub]], writes=[ysb_b[sub]], inc=True, tag="D_dn")
            if fh == 0:
                free("y0t3", "y0t2", "y0t1", "y0t0", "wd_sl0_1", "wd_sl0_0")

        if dbg:
            EM.emit("sync", lambda sy: sy.dma_start(out=d_dbg["d_ysb"], in_=ysb[:]),
                    reads=ysb_b, inc=True, tag="dbg")

        EM.fence()
        free("y0r1", "y0r0", "wd_sl1_1", "wd_sl1_0", "hsb")
        x2f = [alloc(f"x2f{i}", [128, D], BF16) for i in range(2)]
        x2f_b = bufs(2, "x2f")
        outf = [alloc(f"outf{i}", [128, D], F32) for i in range(2)]
        outf_b = bufs(2, "outf")
        out_done = bufs(8, "out")

        for sub in range(8):
            par = sub % 2
            EM.emit("sync",
                    lambda sy, o=x2f[par][:], i_=d_x2[sub * 128:(sub + 1) * 128, :]:
                    sy.dma_start(out=o, in_=i_),
                    reads=[x2_b[sub]], writes=[x2f_b[par]], inc=True, tag="E")
            st = stn[par]
            ysl = ysb[:, sub * D:(sub + 1) * D]
            ln_stats(st, ysl, ysb_b[sub], "E")
            ln_norm(st, outf[par][:], outf_b[par], ysl, ysb_b[sub], "E")
            EM.emit("vector", lambda v, o=outf[par][:], x_=x2f[par][:]:
                    v.tensor_tensor(o, o, x_, OP.add),
                    reads=[outf_b[par], x2f_b[par]], writes=[outf_b[par]], inc=True, tag="E")
            EM.emit("sync",
                    lambda sy, o=d_out[sub * 128:(sub + 1) * 128, :], i_=outf[par][:]:
                    sy.dma_start(out=o, in_=i_),
                    reads=[outf_b[par]], writes=[out_done[sub]], inc=True, tag="E")
        EM.emit("sync", None, reads=out_done, tag="E")

        blk = outer.enter_context(nc.Block())
        EM.replay(nc, blk, sems)
    return nc


def _prep(L0_attn_x, L0_attn_Wqkv, L0_attn_bqkv, L0_attn_Wo, L0_attn_bo,
          L0_ffn_W_gate, L0_ffn_W_up, L0_ffn_W_down):
    x = np.ascontiguousarray(np.asarray(L0_attn_x, dtype=np.float32).reshape(S, D))
    Wqkv = np.asarray(L0_attn_Wqkv, dtype=np.float32)
    wq_t = np.ascontiguousarray(
        Wqkv.reshape(32, 128, 24, 128).transpose(2, 1, 0, 3)).reshape(24 * 128, D)
    bqkv_t = np.ascontiguousarray(
        np.asarray(L0_attn_bqkv, dtype=np.float32).reshape(24, 128).T)
    wo = np.ascontiguousarray(np.asarray(L0_attn_Wo, dtype=np.float32))
    bo_b = np.ascontiguousarray(
        np.broadcast_to(np.asarray(L0_attn_bo, dtype=np.float32), (128, D)))
    wg_t = np.ascontiguousarray(
        np.asarray(L0_ffn_W_gate, dtype=np.float32)
        .reshape(NFC, 128, 32, 128).transpose(3, 0, 2, 1)).reshape(128, NFC * D)
    wu_t = np.ascontiguousarray(
        np.asarray(L0_ffn_W_up, dtype=np.float32)
        .reshape(NFC, 128, 32, 128).transpose(3, 0, 2, 1)).reshape(128, NFC * D)
    wd_t = np.ascontiguousarray(
        np.asarray(L0_ffn_W_down, dtype=np.float32)
        .reshape(8, 512, NFC, 128).transpose(0, 3, 2, 1)).reshape(8 * 128, NFC * 512)
    ident = np.eye(128, dtype=np.float32)
    common = {"wq": wq_t, "bqkv": bqkv_t, "wo": wo, "bo": bo_b,
              "wg": wg_t, "wu": wu_t, "wd": wd_t, "ident": ident}
    return x, common


def kernel(L0_attn_x, L0_attn_Wqkv, L0_attn_bqkv, L0_attn_Wo, L0_attn_bo,
           L0_ffn_W_gate, L0_ffn_W_up, L0_ffn_W_down, _dbg=False):
    x, common = _prep(L0_attn_x, L0_attn_Wqkv, L0_attn_bqkv, L0_attn_Wo,
                      L0_attn_bo, L0_ffn_W_gate, L0_ffn_W_up, L0_ffn_W_down)
    nc = build_program(dbg=_dbg)
    in_maps = [dict(common, x=np.ascontiguousarray(x[c * T:(c + 1) * T]))
               for c in range(8)]
    res = run_bass_kernel_spmd(nc, in_maps, core_ids=list(range(8)))
    out = np.concatenate([res.results[c]["out"] for c in range(8)], axis=0)
    out = out.reshape(1, S, D).astype(np.float32)
    if _dbg:
        return out, res
    return out


# version 11
# speedup vs baseline: 1.2469x; 1.2469x over previous
import numpy as np
from contextlib import ExitStack

import concourse.bass as bass
import concourse.mybir as mybir
from concourse.bass_utils import run_bass_kernel_spmd

F32 = mybir.dt.float32
BF16 = mybir.dt.bfloat16
AF = mybir.ActivationFunctionType
OP = mybir.AluOpType
AF_COPY = getattr(AF, "Copy", None)
if AF_COPY is None:
    AF_COPY = getattr(AF, "Identity")

ENGINES = ("tensor", "vector", "scalar", "sync", "gpsimd")
ENG_SEM = {"tensor": "mt", "vector": "mv", "scalar": "ms", "sync": "dsy", "gpsimd": "dgp"}
COMPUTE = ("tensor", "vector", "scalar")


class Buf:
    __slots__ = ("name", "ready", "readers")

    def __init__(self, name=""):
        self.name = name
        self.ready = None
        self.readers = {}


def bufs(n, name):
    return [Buf(f"{name}{i}") for i in range(n)]


class Em:
    def __init__(self):
        self.ops = {e: [] for e in ENGINES}
        self.cnt = {s: 0 for s in ENG_SEM.values()}
        self.cnt_sbuf = {s: 0 for s in ENG_SEM.values()}
        self.waited = {e: {} for e in ENGINES}
        self.pfence = {e: None for e in ENGINES}
        self.last_writes = {e: frozenset() for e in ENGINES}

    def emit(self, eng, build, reads=(), writes=(), inc=False, record=True,
             dram_only=False, tag=""):
        own = ENG_SEM[eng]
        if eng == "vector" and any(id(b) in self.last_writes[eng] for b in reads):
            self.ops[eng].append(((), lambda v: v.engine_nop(), None, "nopg"))
            self.last_writes[eng] = frozenset()
        waits = {}

        def need(sem, val, why):
            if sem == own and eng in COMPUTE:
                return
            if val is None:
                raise RuntimeError(f"un-inc'd dependency: {why} [{tag}] on {eng}")
            if self.waited[eng].get(sem, 0) >= val:
                return
            if waits.get(sem, 0) < val:
                waits[sem] = val

        if eng in ("sync", "gpsimd"):
            need(own, self.cnt[own], "dma-serialize")
        pf = self.pfence[eng]
        if pf:
            self.pfence[eng] = None
            for s, v in pf.items():
                need(s, v, "fence")
        for b in reads:
            r = b.ready
            if r is None:
                continue
            if r[0] == "eng":
                if r[1] != eng:
                    raise RuntimeError(f"cross-engine read of un-inc'd {b.name} [{tag}]")
            else:
                need(r[0], r[1], f"read {b.name}")
        for b in writes:
            for s, v in b.readers.items():
                if s == own and eng in COMPUTE:
                    continue
                need(s, v, f"WAR {b.name}")
            r = b.ready
            if r is not None:
                if r[0] == "eng":
                    if r[1] != eng:
                        raise RuntimeError(f"WAW cross-engine un-inc'd {b.name} [{tag}]")
                else:
                    need(r[0], r[1], f"WAW {b.name}")
        increc = None
        if inc:
            amt = (16 if eng in ("sync", "gpsimd") else 1) if inc is True else int(inc)
            self.cnt[own] += amt
            if not dram_only:
                self.cnt_sbuf[own] = self.cnt[own]
            increc = (own, amt)
        self.ops[eng].append((tuple(sorted(waits.items())), build, increc, tag))
        self.last_writes[eng] = frozenset(id(b) for b in writes)
        for s, v in waits.items():
            self.waited[eng][s] = v
        mark = (own, self.cnt[own]) if inc else ("eng", eng)
        for b in writes:
            b.ready = mark
            b.readers = {}
        if record:
            for b in reads:
                b.readers[own] = self.cnt[own] if inc else None

    def note_readers(self, rbufs, eng):
        own = ENG_SEM[eng]
        for b in rbufs:
            b.readers[own] = self.cnt[own]

    def fence(self):
        snap = {s: v for s, v in self.cnt_sbuf.items() if v > 0}
        for e in ENGINES:
            pf = self.pfence[e] or {}
            keys = set(pf) | set(snap)
            self.pfence[e] = {s: max(pf.get(s, 0), snap.get(s, 0)) for s in keys}

    def replay(self, nc, blk, sems):
        for eng in ENGINES:
            ops = self.ops[eng]

            def make(ops=ops):
                def body(e):
                    for waits, build, increc, tag in ops:
                        for s, v in waits:
                            e.wait_ge(sems[s], v)
                        ins = build(e) if build is not None else None
                        if increc is not None:
                            ins.then_inc(sems[increc[0]], increc[1])
                return body
            getattr(blk, eng)(make())


T = 1024
S = 8192
D = 4096
DH = 1024
F = 11008
NT = 8
NFC = 86
FHC = 43
C_SHIFT = -12.0
SCALE = 1.0 / 32.0


def build_program(dbg=False):
    nc = bass.Bass("TRN2", target_bir_lowering=False)

    d_x = nc.dram_tensor("x", [T, D], F32, kind="ExternalInput").ap()
    d_wq = nc.dram_tensor("wq", [24 * 128, D], F32, kind="ExternalInput").ap()
    d_bqkv = nc.dram_tensor("bqkv", [128, 24], F32, kind="ExternalInput").ap()
    d_wo = nc.dram_tensor("wo", [DH, D], F32, kind="ExternalInput").ap()
    d_bo = nc.dram_tensor("bo", [128, D], F32, kind="ExternalInput").ap()
    d_wg = nc.dram_tensor("wg", [128, NFC * D], F32, kind="ExternalInput").ap()
    d_wu = nc.dram_tensor("wu", [128, NFC * D], F32, kind="ExternalInput").ap()
    d_wd = nc.dram_tensor("wd", [8 * 128, NFC * 512], F32, kind="ExternalInput").ap()
    d_id = nc.dram_tensor("ident", [128, 128], F32, kind="ExternalInput").ap()
    d_out = nc.dram_tensor("out", [T, D], F32, kind="ExternalOutput").ap()

    xk = "ExternalOutput" if dbg else "Internal"
    d_x2 = nc.dram_tensor("x2d", [T, D], BF16, kind=xk).ap()
    d_y0 = nc.dram_tensor("y0d", [T, D], BF16, kind=xk).ap()
    d_cik = nc.dram_tensor("cc_in_k", [DH, T], BF16, kind="Internal").ap()
    d_civ = nc.dram_tensor("cc_in_v", [T, DH], BF16, kind="Internal").ap()
    d_cok = nc.dram_tensor("cc_out_k", [8 * DH, T], BF16, addr_space="Shared", kind="Internal").ap()
    d_cov = nc.dram_tensor("cc_out_v", [S, DH], BF16, addr_space="Shared", kind="Internal").ap()
    if dbg:
        d_dbg = {
            "d_qfm": nc.dram_tensor("d_qfm", [128, 8 * T], BF16, kind="ExternalOutput").ap(),
            "d_kfm": nc.dram_tensor("d_kfm", [128, 8 * T], BF16, kind="ExternalOutput").ap(),
            "d_vtm": nc.dram_tensor("d_vtm", [128, 8 * T], BF16, kind="ExternalOutput").ap(),
            "d_osb": nc.dram_tensor("d_osb", [128, 8 * T], BF16, kind="ExternalOutput").ap(),
            "d_den": nc.dram_tensor("d_den", [128, 8], F32, kind="ExternalOutput").ap(),
            "d_a0": nc.dram_tensor("d_a0", [128, D], F32, kind="ExternalOutput").ap(),
            "d_h0": nc.dram_tensor("d_h0", [128, FHC * T], BF16, kind="ExternalOutput").ap(),
            "d_ln3": nc.dram_tensor("d_ln3", [128, 32 * T], BF16, kind="ExternalOutput").ap(),
            "d_ysb": nc.dram_tensor("d_ysb", [128, 8 * D], BF16, kind="ExternalOutput").ap(),
            "d_xt0": nc.dram_tensor("d_xt0", [128, D], F32, kind="ExternalOutput").ap(),
            "d_ln1t0": nc.dram_tensor("d_ln1t0", [128, D], F32, kind="ExternalOutput").ap(),
            "d_ln1fm": nc.dram_tensor("d_ln1fm", [128, 32 * 512], BF16, kind="ExternalOutput").ap(),
            "d_wq0": nc.dram_tensor("d_wq0", [128, D], BF16, kind="ExternalOutput").ap(),
        }

    EM = Em()
    outer = ExitStack()
    live = {}

    def alloc(name, shape, dtype, kind="sbuf", side="left"):
        if kind == "sbuf":
            cm = nc.sbuf_tensor(name, shape, dtype, side=side)
        else:
            cm = nc.psum_tensor(name, shape, dtype)
        ap = cm.__enter__()
        live[name] = cm
        return ap

    def free(*names):
        for n in names:
            live.pop(n).__exit__(None, None, None)

    with outer:
        sems = {s: outer.enter_context(nc.semaphore(s)) for s in ENG_SEM.values()}

        P = [alloc(f"p{i}", [128, 512], F32, "psum") for i in range(6)]
        Pb = bufs(6, "psum")

        ident = alloc("ident_sb", [128, 128], F32)
        ident_b = Buf("ident")
        c_eps = alloc("c_eps", [128, 1], F32)
        c_s32 = alloc("c_s32", [128, 1], F32)
        c_nb12 = alloc("c_nb12", [128, 1], F32)
        c_i4096 = alloc("c_i4096", [128, 1], F32)
        ones_bf = alloc("ones_bf", [128, 1], BF16)
        consts_b = Buf("consts")
        bqkv_sb = alloc("bqkv_sb", [128, 24], F32)
        bqkv_b = Buf("bqkv")

        scr_bf = alloc("scr_bf", [128, D], BF16)
        scr_b = Buf("scr")
        stn = []
        for i in range(2):
            stn.append({
                "sums": alloc(f"sums_{i}", [128, 2], F32), "sumsb": Buf(f"sums_{i}"),
                "mean": alloc(f"mean_{i}", [128, 1], F32), "meanb": Buf(f"mean_{i}"),
                "msq": alloc(f"msq_{i}", [128, 1], F32), "msqb": Buf(f"msq_{i}"),
                "veps": alloc(f"veps_{i}", [128, 1], F32), "vepsb": Buf(f"veps_{i}"),
                "std": alloc(f"std_{i}", [128, 1], F32), "stdb": Buf(f"std_{i}"),
                "rcp": alloc(f"rcp_{i}", [128, 1], F32), "rcpb": Buf(f"rcp_{i}"),
            })

        def ln_stats(st, src_ap, src_buf, tag):
            EM.emit("vector", lambda v, o=st["sums"][:, 0:1], i_=src_ap:
                    v.tensor_reduce(o, i_, mybir.AxisListType.X, OP.add),
                    reads=[src_buf], writes=[st["sumsb"]], inc=True, tag=tag)
            EM.emit("scalar", lambda s, o=scr_bf[:], i_=src_ap, a=st["sums"][:, 1:2]:
                    s.activation(o, i_, AF.Square, accum_out=a),
                    reads=[src_buf], writes=[scr_b, st["sumsb"]], inc=True, tag=tag)
            EM.emit("scalar", lambda s, o=st["mean"][:], i_=st["sums"][:, 0:1], sc=c_i4096[:, 0:1]:
                    s.activation(o, i_, AF_COPY, scale=sc),
                    reads=[st["sumsb"], consts_b], writes=[st["meanb"]], inc=True, tag=tag)
            EM.emit("scalar", lambda s, o=st["msq"][:], i_=st["mean"][:]:
                    s.activation(o, i_, AF.Square),
                    reads=[st["meanb"]], writes=[st["msqb"]], inc=True, tag=tag)
            EM.emit("vector", lambda v, o=st["veps"][:], i_=st["sums"][:, 1:2], m2=st["msq"][:, 0:1]:
                    v.tensor_scalar(o, i_, 1.0 / D, m2, OP.mult, OP.subtract),
                    reads=[st["sumsb"], st["msqb"]], writes=[st["vepsb"]], inc=True, tag=tag)
            EM.emit("scalar", lambda s, o=st["std"][:], i_=st["veps"][:], b=c_eps[:, 0:1]:
                    s.activation(o, i_, AF.Sqrt, bias=b),
                    reads=[st["vepsb"], consts_b], writes=[st["stdb"]], inc=True, tag=tag)
            EM.emit("vector", lambda v, o=st["rcp"][:], i_=st["std"][:]: v.reciprocal(o, i_),
                    reads=[st["stdb"]], writes=[st["rcpb"]], inc=True, tag=tag)

        def ln_norm(st, out_ap, out_buf, src_ap, src_buf, tag):
            EM.emit("vector",
                    lambda v, o=out_ap, i_=src_ap, b=st["mean"][:, 0:1], sc=st["rcp"][:, 0:1]:
                    v.tensor_scalar(o, i_, b, sc, OP.subtract, OP.mult),
                    reads=[src_buf, st["meanb"], st["rcpb"]], writes=[out_buf], inc=True, tag=tag)

        EM.emit("vector", lambda v: v.memset(c_eps[:], 1e-5), writes=[consts_b], tag="init")
        EM.emit("vector", lambda v: v.memset(c_s32[:], SCALE), writes=[consts_b], tag="init")
        EM.emit("vector", lambda v: v.memset(c_nb12[:], C_SHIFT), writes=[consts_b], tag="init")
        EM.emit("vector", lambda v: v.memset(c_i4096[:], 1.0 / D), writes=[consts_b], tag="init")
        EM.emit("vector", lambda v: v.memset(ones_bf[:], 1.0), writes=[consts_b], inc=True, tag="init")
        EM.emit("sync", lambda sy: sy.dma_start(out=ident[:], in_=d_id),
                writes=[ident_b], inc=True, tag="init")
        identb = alloc("identb_sb", [128, 128], BF16)
        identb_b = Buf("identb")
        EM.emit("gpsimd", lambda g: g.dma_start(out=identb[:], in_=d_id),
                writes=[identb_b], inc=True, tag="init")
        EM.emit("sync", lambda sy: sy.dma_start(out=bqkv_sb[:], in_=d_bqkv),
                writes=[bqkv_b], inc=True, tag="init")

        ptr = [alloc(f"ptr{i}", [128, 1024], BF16, "psum") for i in range(2)]
        ptrb = bufs(2, "ptr")
        q_fm = alloc("q_fm", [128, 8 * T], BF16)
        q_fm_b = Buf("q_fm")
        k_fm = alloc("k_fm", [128, 8 * T], BF16)
        k_fm_b = Buf("k_fm")
        v_fm = alloc("v_fm", [128, 8 * T], BF16)
        v_fm_b = Buf("v_fm")
        v_tm = alloc("v_tm", [128, 8 * T], BF16)
        v_tm_b = Buf("v_tm")
        x_t = [alloc(f"x_t{i}", [128, D], F32) for i in range(2)]
        x_t_b = bufs(2, "x_t")
        ln1_t = [alloc(f"ln1_t{i}", [128, D], BF16) for i in range(2)]
        ln1_t_b = bufs(2, "ln1_t")
        ln1_fm = alloc("ln1_fm", [128, 32 * T], BF16)
        ln1_fm_b = Buf("ln1_fm")
        wq_sl = [alloc(f"wq_sl{i}", [128, D], BF16) for i in range(2)]
        wq_sl_b = bufs(2, "wq_sl")

        def transpose_block(trp, trb, src_ap, src_buf, n_dc, dst_fn, tag):
            for g in range(n_dc // 4):
                bank = g % 2
                for j in range(4):
                    dc = g * 4 + j
                    EM.emit("tensor",
                            lambda t, o=trp[bank][:, j * 128:(j + 1) * 128],
                            i_=src_ap[:, dc * 128:(dc + 1) * 128], idn=identb[:]:
                            t.transpose(o, i_, idn),
                            reads=[src_buf, identb_b], writes=[trb[bank]],
                            inc=(j == 3), record=(j == 3), tag=tag)
                for j in range(4):
                    dc = g * 4 + j
                    dst_ap, dst_buf = dst_fn(dc)
                    EM.emit("vector",
                            lambda v, o=dst_ap, i_=trp[bank][:, j * 128:(j + 1) * 128]:
                            v.tensor_copy(o, i_),
                            reads=[trb[bank]], writes=[dst_buf],
                            inc=(j == 3), record=(j == 3), tag=tag)

        for s_g in range(8):
            par = s_g % 2
            EM.emit("sync",
                    lambda sy, o=x_t[par][:], i_=d_x[s_g * 128:(s_g + 1) * 128, :]:
                    sy.dma_start(out=o, in_=i_),
                    writes=[x_t_b[par]], inc=True, tag="A_ln")
            st = stn[par]
            ln_stats(st, x_t[par][:], x_t_b[par], "A_ln")
            ln_norm(st, ln1_t[par][:], ln1_t_b[par], x_t[par][:], x_t_b[par], "A_ln")
            if dbg and s_g == 0:
                EM.emit("sync", lambda sy: sy.dma_start(out=d_dbg["d_xt0"], in_=x_t[0][:]),
                        reads=[x_t_b[0]], inc=True, tag="dbg")
                EM.emit("gpsimd", lambda g: g.dma_start(out=d_dbg["d_ln1t0"], in_=ln1_t[0][:]),
                        reads=[ln1_t_b[0]], inc=True, tag="dbg")
            transpose_block(
                ptr, ptrb, ln1_t[par][:], ln1_t_b[par], 32,
                lambda dc, s_g=s_g: (ln1_fm[:, dc * T + s_g * 128: dc * T + (s_g + 1) * 128], ln1_fm_b),
                "A_tr")
        if dbg:
            EM.emit("sync", lambda sy: sy.dma_start(out=d_dbg["d_ln1fm"], in_=ln1_fm[:, 0:32 * 512]),
                    reads=[ln1_fm_b], inc=True, tag="dbg")
        cik_b = bufs(8, "cik")
        civ_b = bufs(8, "civ")
        cok_b = Buf("cok")
        cov_b = Buf("cov")
        for fc in list(range(8, 24)) + list(range(0, 8)):
            par = fc % 2
            EM.emit("gpsimd",
                    lambda g, o=wq_sl[par][:], i_=d_wq[fc * 128:(fc + 1) * 128, :]:
                    g.dma_start(out=o, in_=i_),
                    writes=[wq_sl_b[par]], inc=True, tag="A_qkv")
            if dbg and fc == 0:
                EM.emit("sync", lambda sy: sy.dma_start(out=d_dbg["d_wq0"], in_=wq_sl[0][:]),
                        reads=[wq_sl_b[0]], inc=True, tag="dbg")
            if fc < 8:
                dst_ap, dst_buf = q_fm, q_fm_b
                col0 = fc * T
            elif fc < 16:
                dst_ap, dst_buf = k_fm, k_fm_b
                col0 = (fc - 8) * T
            else:
                dst_ap, dst_buf = v_fm, v_fm_b
                col0 = (fc - 16) * T
            for h in range(2):
                bank = (fc * 2 + h) % 2
                for dc in range(32):
                    EM.emit("tensor",
                            lambda t, o=P[bank][:, 0:512],
                            l=wq_sl[par][:, dc * 128:(dc + 1) * 128],
                            r=ln1_fm[:, dc * T + h * 512: dc * T + (h + 1) * 512],
                            st_=(dc == 0), sp=(dc == 31):
                            t.matmul(o, l, r, start=st_, stop=sp, skip_group_check=True),
                            reads=[wq_sl_b[par], ln1_fm_b], writes=[Pb[bank]],
                            inc=(dc == 31), record=(dc == 31), tag="A_qkv")
                EM.emit("vector",
                        lambda v, o=dst_ap[:, col0 + h * 512: col0 + (h + 1) * 512],
                        i_=P[bank][:, 0:512], b=bqkv_sb[:, fc:fc + 1]:
                        v.tensor_scalar_add(o, i_, b),
                        reads=[Pb[bank], bqkv_b], writes=[dst_buf], inc=True, tag="A_qkv")
            if fc == 15:
                for i in range(8):
                    EM.emit("sync",
                            lambda sy, o=d_cik[i * 128:(i + 1) * 128, :], i_=k_fm[:, i * T:(i + 1) * T]:
                            sy.dma_start(out=o, in_=i_),
                            reads=[k_fm_b], writes=[cik_b[i]], inc=True, tag="A_cc")
                EM.emit("gpsimd",
                        lambda g: g.collective_compute(
                            "AllGather", OP.bypass,
                            replica_groups=[[0, 1, 2, 3, 4, 5, 6, 7]],
                            ins=[d_cik], outs=[d_cok]),
                        reads=cik_b, writes=[cok_b], inc=1, tag="A_cc")
            if fc == 23:
                for kt in range(8):
                    for g2 in range(2):
                        bank = (kt * 2 + g2) % 2
                        for j in range(4):
                            dhc = g2 * 4 + j
                            EM.emit("tensor",
                                    lambda t, o=ptr[bank][:, j * 128:(j + 1) * 128],
                                    i_=v_fm[:, dhc * T + kt * 128: dhc * T + (kt + 1) * 128], idn=identb[:]:
                                    t.transpose(o, i_, idn),
                                    reads=[v_fm_b, identb_b], writes=[ptrb[bank]],
                                    inc=(j == 3), record=(j == 3), tag="A_vtr")
                        EM.emit("vector",
                                lambda v, o=v_tm[:, kt * T + g2 * 512: kt * T + (g2 + 1) * 512],
                                i_=ptr[bank][:, 0:512]:
                                v.tensor_copy(o, i_),
                                reads=[ptrb[bank]], writes=[v_tm_b], inc=True, tag="A_vtr")
                for i in range(8):
                    EM.emit("sync",
                            lambda sy, o=d_civ[i * 128:(i + 1) * 128, :], i_=v_tm[:, i * T:(i + 1) * T]:
                            sy.dma_start(out=o, in_=i_),
                            reads=[v_tm_b], writes=[civ_b[i]], inc=True, tag="A_cc")

        EM.emit("gpsimd",
                lambda g: g.collective_compute(
                    "AllGather", OP.bypass,
                    replica_groups=[[0, 1, 2, 3, 4, 5, 6, 7]],
                    ins=[d_civ], outs=[d_cov]),
                reads=civ_b, writes=[cov_b], inc=1, tag="A_cc")

        if dbg:
            EM.emit("sync", lambda sy: sy.dma_start(out=d_dbg["d_qfm"], in_=q_fm[:]),
                    reads=[q_fm_b], inc=True, tag="dbg")
            EM.emit("sync", lambda sy: sy.dma_start(out=d_dbg["d_kfm"], in_=k_fm[:]),
                    reads=[k_fm_b], inc=True, tag="dbg")
            EM.emit("sync", lambda sy: sy.dma_start(out=d_dbg["d_vtm"], in_=v_tm[:]),
                    reads=[v_tm_b], inc=True, tag="dbg")

        EM.fence()
        free("wq_sl1", "wq_sl0", "ln1_fm", "ln1_t1", "ln1_t0", "x_t1", "x_t0",
             "v_tm", "v_fm", "k_fm")
        free("ptr1", "ptr0")
        pden = alloc("pden", [128, 512], F32, "psum")
        pden_b = Buf("pden")

        wo_bf = alloc("wo_bf", [128, 8 * D], BF16, side="right")
        wo_b = bufs(8, "wo")
        o_sb = alloc("o_sb", [128, 8 * T], BF16, side="right")
        o_sb_b = Buf("o_sb")
        bo_bc = alloc("bo_bc", [128, D], BF16, side="right")
        bo_b = Buf("bo")
        rden = alloc("rden", [128, 8], F32, side="right")
        rden_b = Buf("rden")
        k_sl = [alloc(f"k_sl{i}", [128, 8 * T], BF16) for i in range(2)]
        k_sl_b = [bufs(8, f"k_sl{i}_") for i in range(2)]
        v_sl = [alloc(f"v_sl{i}", [128, 8 * T], BF16) for i in range(2)]
        v_sl_b = [bufs(8, f"v_sl{i}_") for i in range(2)]
        p_t8 = alloc("p_t8", [128, 8 * 512], BF16)
        p_t8_b = bufs(8, "p_t8")

        for i in range(8):
            EM.emit("gpsimd",
                    lambda g, o=wo_bf[:, i * D:(i + 1) * D], i_=d_wo[i * 128:(i + 1) * 128, :]:
                    g.dma_start(out=o, in_=i_),
                    writes=[wo_b[i]], inc=True, tag="B")
        EM.emit("gpsimd", lambda g: g.dma_start(out=bo_bc[:], in_=d_bo),
                writes=[bo_b], inc=True, tag="B")

        for cb in range(8):
            par = cb % 2
            for i in range(8):
                EM.emit("sync",
                        lambda sy, o=k_sl[par][:, i * T:(i + 1) * T],
                        i_=d_cok[cb * DH + i * 128: cb * DH + (i + 1) * 128, :]:
                        sy.dma_start(out=o, in_=i_),
                        reads=[cok_b], writes=[k_sl_b[par][i]], inc=True, tag="B")
            for i in range(8):
                EM.emit("sync",
                        lambda sy, o=v_sl[par][:, i * T:(i + 1) * T],
                        i_=d_cov[cb * T + i * 128: cb * T + (i + 1) * 128, :]:
                        sy.dma_start(out=o, in_=i_),
                        reads=[cov_b], writes=[v_sl_b[par][i]], inc=True, tag="B")

            for qp in range(2):
                for kt in range(8):
                    bank = kt % 2
                    for dhc in range(8):
                        EM.emit("tensor",
                                lambda t, o=P[bank][:, 0:512],
                                l=k_sl[par][:, dhc * T + kt * 128: dhc * T + (kt + 1) * 128],
                                r=q_fm[:, dhc * T + qp * 512: dhc * T + (qp + 1) * 512],
                                st_=(dhc == 0), sp=(dhc == 7):
                                t.matmul(o, l, r, start=st_, stop=sp, skip_group_check=True),
                                reads=[k_sl_b[par][dhc], q_fm_b], writes=[Pb[bank]],
                                inc=(dhc == 7), record=False, tag="B_s")
                    EM.note_readers([k_sl_b[par][d_] for d_ in range(8)] + [q_fm_b], "tensor")
                    EM.emit("scalar",
                            lambda s, o=p_t8[:, kt * 512:(kt + 1) * 512], i_=P[bank][:, 0:512],
                            b=c_nb12[:, 0:1], sc=c_s32[:, 0:1]:
                            s.activation(o, i_, AF.Exp, bias=b, scale=sc),
                            reads=[Pb[bank], consts_b], writes=[p_t8_b[kt]], inc=True, tag="B_s")
                    for sub in range(4):
                        EM.emit("tensor",
                                lambda t, o=pden[:, qp * 4 + sub: qp * 4 + sub + 1],
                                l=p_t8[:, kt * 512 + sub * 128: kt * 512 + (sub + 1) * 128],
                                r=ones_bf[:],
                                st_=(cb == 0 and qp == 0 and kt == 0 and sub == 0),
                                sp=(cb == 7 and qp == 1 and kt == 7 and sub == 3):
                                t.matmul(o, l, r, start=st_, stop=sp, skip_group_check=True),
                                reads=[p_t8_b[kt], consts_b], writes=[pden_b],
                                inc=(cb == 7 and qp == 1 and kt == 7 and sub == 3),
                                record=False, tag="B_d")
                for qh in range(2):
                    qg = qp * 2 + qh
                    for kt in range(8):
                        for dhc in range(8):
                            ob = 2 + dhc // 2
                            EM.emit("tensor",
                                    lambda t, o=P[ob][:, (dhc % 2) * 256:(dhc % 2) * 256 + 256],
                                    l=v_sl[par][:, kt * T + dhc * 128: kt * T + (dhc + 1) * 128],
                                    r=p_t8[:, kt * 512 + qh * 256: kt * 512 + qh * 256 + 256],
                                    st_=(kt == 0 and dhc % 2 == 0),
                                    sp=(kt == 7 and dhc % 2 == 1):
                                    t.matmul(o, l, r, start=st_, stop=sp, skip_group_check=True),
                                    reads=[v_sl_b[par][kt], p_t8_b[kt]], writes=[Pb[ob]],
                                    inc=(dhc == 7 or (kt == 7 and dhc % 2 == 1)),
                                    record=False, tag="B_pv")
                    EM.note_readers([v_sl_b[par][k_] for k_ in range(8)]
                                    + [p_t8_b[k_] for k_ in range(8)] + [consts_b], "tensor")
                    for dhc in range(8):
                        ob = 2 + dhc // 2
                        osl = o_sb[:, dhc * T + qg * 256: dhc * T + (qg + 1) * 256]
                        psl = P[ob][:, (dhc % 2) * 256:(dhc % 2) * 256 + 256]
                        if cb == 0:
                            EM.emit("vector", lambda v, o=osl, i_=psl: v.tensor_copy(o, i_),
                                    reads=[Pb[ob]], writes=[o_sb_b], inc=True, tag="B_ev")
                        else:
                            EM.emit("vector", lambda v, o=osl, i_=psl: v.tensor_tensor(o, o, i_, OP.add),
                                    reads=[Pb[ob], o_sb_b], writes=[o_sb_b], inc=True, tag="B_ev")

        EM.emit("vector", lambda v: v.reciprocal(rden[:], pden[:, 0:8]),
                reads=[pden_b], writes=[rden_b], inc=True, tag="B")

        if dbg:
            EM.emit("sync", lambda sy: sy.dma_start(out=d_dbg["d_osb"], in_=o_sb[:]),
                    reads=[o_sb_b], inc=True, tag="dbg")
            EM.emit("sync", lambda sy: sy.dma_start(out=d_dbg["d_den"], in_=rden[:]),
                    reads=[rden_b], inc=True, tag="dbg")

        EM.fence()
        free("p_t8", "v_sl1", "v_sl0", "k_sl1", "k_sl0", "q_fm")
        free("pden")

        a_sb = [alloc(f"a_sb{i}", [128, D], F32) for i in range(3)]
        a_sb_b = bufs(3, "a_sb")
        lnb = [alloc("lnb0", [128, D], F32)]
        lnb_b = bufs(1, "lnb")
        xr_t = [alloc(f"xr_t{i}", [128, D], F32) for i in range(2)]
        xr_t_b = bufs(2, "xr")
        x2_b = bufs(8, "x2")

        for sg in range(8):
            par = sg % 3
            spar = sg % 2
            EM.emit("sync",
                    lambda sy, o=xr_t[spar][:], i_=d_x[sg * 128:(sg + 1) * 128, :]:
                    sy.dma_start(out=o, in_=i_),
                    writes=[xr_t_b[spar]], inc=True, tag="C")
            for cc in range(8):
                bank = cc % 2
                for dhc in range(8):
                    EM.emit("tensor",
                            lambda t, o=P[bank][:, 0:512],
                            l=o_sb[:, dhc * T + sg * 128: dhc * T + (sg + 1) * 128],
                            r=wo_bf[:, dhc * D + cc * 512: dhc * D + (cc + 1) * 512],
                            st_=(dhc == 0), sp=(dhc == 7):
                            t.matmul(o, l, r, start=st_, stop=sp, skip_group_check=True),
                            reads=[o_sb_b, wo_b[dhc]], writes=[Pb[bank]],
                            inc=(dhc == 7), record=False, tag="C_mm")
                EM.note_readers([o_sb_b] + wo_b, "tensor")
                EM.emit("scalar",
                        lambda s, o=a_sb[par][:, cc * 512:(cc + 1) * 512],
                        i_=P[bank][:, 0:512], sc=rden[:, sg:sg + 1]:
                        s.activation(o, i_, AF_COPY, scale=sc),
                        reads=[Pb[bank], rden_b], writes=[a_sb_b[par]], inc=True, tag="C_mm")
            EM.emit("vector", lambda v, o=a_sb[par][:], b=bo_bc[:]:
                    v.tensor_tensor(o, o, b, OP.add),
                    reads=[a_sb_b[par], bo_b], writes=[a_sb_b[par]], inc=True, tag="C")
            if dbg and sg == 0:
                EM.emit("sync", lambda sy: sy.dma_start(out=d_dbg["d_a0"], in_=a_sb[0][:]),
                        reads=[a_sb_b[0]], inc=True, tag="dbg")
            st = stn[sg % 2]
            ln_stats(st, a_sb[par][:], a_sb_b[par], "C")
            ln_norm(st, lnb[0][:], lnb_b[0], a_sb[par][:], a_sb_b[par], "C")
            EM.emit("vector", lambda v, o=lnb[0][:], x_=xr_t[spar][:]:
                    v.tensor_tensor(o, o, x_, OP.add),
                    reads=[lnb_b[0], xr_t_b[spar]], writes=[lnb_b[0]], inc=True, tag="C")
            EM.emit("gpsimd",
                    lambda g, o=d_x2[sg * 128:(sg + 1) * 128, :], i_=lnb[0][:]:
                    g.dma_start(out=o, in_=i_),
                    reads=[lnb_b[0]], writes=[x2_b[sg]], inc=True, tag="C")

        EM.fence()
        free("xr_t1", "xr_t0", "lnb0", "a_sb2", "a_sb1", "a_sb0",
             "rden", "bo_bc", "o_sb", "wo_bf")

        ln3_fm = alloc("ln3_fm", [128, 32 * T], BF16, side="right")
        ln3_fm_b = Buf("ln3_fm")
        hsb = alloc("hsb", [128, FHC * T], BF16)
        hsb_b = Buf("hsb")
        x2t = [alloc(f"x2t{i}", [128, D], BF16) for i in range(2)]
        x2t_b = bufs(2, "x2t")
        ln3t = [alloc(f"ln3t{i}", [128, D], BF16) for i in range(2)]
        ln3t_b = bufs(2, "ln3t")
        ptr2 = [alloc(f"ptr2_{i}", [128, 1024], BF16, "psum") for i in range(2)]
        ptr2b = bufs(2, "ptr2")

        for sg in range(8):
            par = sg % 2
            EM.emit("sync",
                    lambda sy, o=x2t[par][:], i_=d_x2[sg * 128:(sg + 1) * 128, :]:
                    sy.dma_start(out=o, in_=i_),
                    reads=[x2_b[sg]], writes=[x2t_b[par]], inc=True, tag="Dp")
            st = stn[par]
            ln_stats(st, x2t[par][:], x2t_b[par], "Dp")
            ln_norm(st, ln3t[par][:], ln3t_b[par], x2t[par][:], x2t_b[par], "Dp")
            transpose_block(
                ptr2, ptr2b, ln3t[par][:], ln3t_b[par], 32,
                lambda dc, sg=sg: (ln3_fm[:, dc * T + sg * 128: dc * T + (sg + 1) * 128], ln3_fm_b),
                "Dp_tr")
        if dbg:
            EM.emit("sync", lambda sy: sy.dma_start(out=d_dbg["d_ln3"], in_=ln3_fm[:]),
                    reads=[ln3_fm_b], inc=True, tag="dbg")

        EM.fence()
        free("ptr2_1", "ptr2_0")
        free("ln3t1", "ln3t0", "x2t1", "x2t0")
        y0_b = {}
        ysb_b = bufs(8, "ysb")
        WSEG = 22 * 512

        for fh in range(2):
            if fh == 1:
                EM.fence()
            wg_sl = [alloc(f"wg_sl{fh}_{i}", [128, D], BF16) for i in range(2)]
            wg_sl_b = bufs(2, "wg_sl")
            wu_sl = [alloc(f"wu_sl{fh}_{i}", [128, D], BF16) for i in range(2)]
            wu_sl_b = bufs(2, "wu_sl")
            sg_t = [alloc(f"sg_t{fh}_{i}", [128, 512], F32) for i in range(2)]
            sg_t_b = bufs(2, "sg_t")
            for fc in range(FHC):
                fg = fh * FHC + fc
                par = fc % 2
                EM.emit("gpsimd",
                        lambda g, o=wg_sl[par][:], i_=d_wg[:, fg * D:(fg + 1) * D]:
                        g.dma_start(out=o, in_=i_),
                        writes=[wg_sl_b[par]], inc=True, tag="D_gu")
                EM.emit("gpsimd",
                        lambda g, o=wu_sl[par][:], i_=d_wu[:, fg * D:(fg + 1) * D]:
                        g.dma_start(out=o, in_=i_),
                        writes=[wu_sl_b[par]], inc=True, tag="D_gu")
                for grp in range(2):
                    gb, ub = grp, 2 + grp
                    for dc in range(32):
                        EM.emit("tensor",
                                lambda t, o=P[gb][:, 0:512],
                                l=wg_sl[par][:, dc * 128:(dc + 1) * 128],
                                r=ln3_fm[:, dc * T + grp * 512: dc * T + grp * 512 + 512],
                                st_=(dc == 0), sp=(dc == 31):
                                t.matmul(o, l, r, start=st_, stop=sp, skip_group_check=True),
                                reads=[wg_sl_b[par], ln3_fm_b], writes=[Pb[gb]],
                                inc=(dc == 31), record=False, tag="D_gu")
                    for dc in range(32):
                        EM.emit("tensor",
                                lambda t, o=P[ub][:, 0:512],
                                l=wu_sl[par][:, dc * 128:(dc + 1) * 128],
                                r=ln3_fm[:, dc * T + grp * 512: dc * T + grp * 512 + 512],
                                st_=(dc == 0), sp=(dc == 31):
                                t.matmul(o, l, r, start=st_, stop=sp, skip_group_check=True),
                                reads=[wu_sl_b[par], ln3_fm_b], writes=[Pb[ub]],
                                inc=(dc == 31), record=False, tag="D_gu")
                    EM.note_readers([wg_sl_b[par], wu_sl_b[par], ln3_fm_b], "tensor")
                    EM.emit("scalar",
                            lambda s, o=sg_t[grp][:], i_=P[gb][:, 0:512]:
                            s.activation(o, i_, AF.Silu),
                            reads=[Pb[gb]], writes=[sg_t_b[grp]], inc=True, tag="D_gu")
                    EM.emit("vector",
                            lambda v, o=hsb[:, fc * T + grp * 512: fc * T + grp * 512 + 512],
                            a=sg_t[grp][:], b=P[ub][:, 0:512]:
                            v.tensor_tensor(o, a, b, OP.mult),
                            reads=[sg_t_b[grp], Pb[ub]], writes=[hsb_b], inc=True, tag="D_gu")
            if dbg and fh == 0:
                EM.emit("sync", lambda sy: sy.dma_start(out=d_dbg["d_h0"], in_=hsb[:]),
                        reads=[hsb_b], inc=True, tag="dbg")
            EM.fence()
            free(f"sg_t{fh}_1", f"sg_t{fh}_0", f"wu_sl{fh}_1", f"wu_sl{fh}_0",
                 f"wg_sl{fh}_1", f"wg_sl{fh}_0")
            if fh == 0:
                wd_sl = [alloc(f"wd_sl0_{i}", [128, WSEG], BF16) for i in range(2)]
                wd_sl_b = bufs(2, "wd_sl")
                y0t = [alloc(f"y0t{i}", [128, 512], BF16) for i in range(4)]
                y0t_b = bufs(4, "y0t")
            else:
                free("ln3_fm")
                ysb = alloc("ysb", [128, 8 * D], BF16, side="right")
                wd_sl = [alloc(f"wd_sl1_{i}", [128, WSEG], BF16) for i in range(2)]
                wd_sl_b = bufs(2, "wd_sl")
                y0r = [alloc(f"y0r{i}", [128, 512], BF16) for i in range(4)]
                y0r_b = bufs(4, "y0r")
            y67 = [alloc(f"y67_{fh}_{i}", [128, 512], F32, "psum") for i in range(2)]
            PB8 = P + y67
            Pb8 = Pb + bufs(2, f"y67_{fh}_")

            def y0r_load(sub, cbk):
                par2 = sub % 4
                EM.emit("sync",
                        lambda sy, o=y0r[par2][:],
                        i_=d_y0[sub * 128:(sub + 1) * 128, cbk * 512:(cbk + 1) * 512]:
                        sy.dma_start(out=o, in_=i_),
                        reads=[y0_b[(sub, cbk)]], writes=[y0r_b[par2]], inc=True, tag="D_dn")

            for cbk in range(8):
                if fh == 1:
                    for sub in range(4):
                        y0r_load(sub, cbk)
                for h2 in range(2):
                    seg = 22 if h2 == 0 else 21
                    spar = (cbk * 2 + h2) % 2
                    col0 = (fh * FHC + h2 * 22) * 512
                    EM.emit("gpsimd",
                            lambda g, o=wd_sl[spar][:, 0:seg * 512],
                            i_=d_wd[cbk * 128:(cbk + 1) * 128, col0:col0 + seg * 512]:
                            g.dma_start(out=o, in_=i_),
                            writes=[wd_sl_b[spar]], inc=True, tag="D_dn")
                    for fcl in range(seg):
                        fc = h2 * 22 + fcl
                        for sub in range(8):
                            EM.emit("tensor",
                                    lambda t, o=PB8[sub][:, 0:512],
                                    l=hsb[:, fc * T + sub * 128: fc * T + (sub + 1) * 128],
                                    r=wd_sl[spar][:, fcl * 512:(fcl + 1) * 512],
                                    st_=(fc == 0), sp=(fc == FHC - 1):
                                    t.matmul(o, l, r, start=st_, stop=sp, skip_group_check=True),
                                    reads=[hsb_b, wd_sl_b[spar]], writes=[Pb8[sub]],
                                    inc=(sub == 7 or fc == FHC - 1), record=False, tag="D_dn")
                    EM.note_readers([hsb_b, wd_sl_b[spar]], "tensor")
                if fh == 0:
                    for sub in range(8):
                        yt = sub % 4
                        eng = "scalar" if sub % 2 == 0 else "vector"
                        if eng == "scalar":
                            EM.emit("scalar",
                                    lambda s, o=y0t[yt][:], i_=PB8[sub][:, 0:512]:
                                    s.activation(o, i_, AF_COPY),
                                    reads=[Pb8[sub]], writes=[y0t_b[yt]], inc=True, tag="D_dn")
                        else:
                            EM.emit("vector",
                                    lambda v, o=y0t[yt][:], i_=PB8[sub][:, 0:512]:
                                    v.tensor_copy(o, i_),
                                    reads=[Pb8[sub]], writes=[y0t_b[yt]], inc=True, tag="D_dn")
                        b = Buf(f"y0_{sub}_{cbk}")
                        y0_b[(sub, cbk)] = b
                        EM.emit("sync",
                                lambda sy, o=d_y0[sub * 128:(sub + 1) * 128, cbk * 512:(cbk + 1) * 512],
                                i_=y0t[yt][:]:
                                sy.dma_start(out=o, in_=i_),
                                reads=[y0t_b[yt]], writes=[b], inc=True, tag="D_dn")
                else:
                    for sub in range(8):
                        par2 = sub % 4
                        EM.emit("vector",
                                lambda v, o=ysb[:, sub * D + cbk * 512: sub * D + (cbk + 1) * 512],
                                a=y0r[par2][:], b=PB8[sub][:, 0:512]:
                                v.tensor_tensor(o, a, b, OP.add),
                                reads=[y0r_b[par2], Pb8[sub]], writes=[ysb_b[sub]], inc=True, tag="D_dn")
                        if sub < 4:
                            y0r_load(sub + 4, cbk)
            free("y67_%d_1" % fh, "y67_%d_0" % fh)
            if fh == 0:
                free("y0t3", "y0t2", "y0t1", "y0t0", "wd_sl0_1", "wd_sl0_0")

        if dbg:
            EM.emit("sync", lambda sy: sy.dma_start(out=d_dbg["d_ysb"], in_=ysb[:]),
                    reads=ysb_b, inc=True, tag="dbg")

        EM.fence()
        free("y0r3", "y0r2", "y0r1", "y0r0", "wd_sl1_1", "wd_sl1_0", "hsb")
        x2f = [alloc(f"x2f{i}", [128, D], BF16) for i in range(2)]
        x2f_b = bufs(2, "x2f")
        outf = [alloc(f"outf{i}", [128, D], F32) for i in range(2)]
        outf_b = bufs(2, "outf")
        out_done = bufs(8, "out")

        for sub in range(8):
            par = sub % 2
            EM.emit("sync",
                    lambda sy, o=x2f[par][:], i_=d_x2[sub * 128:(sub + 1) * 128, :]:
                    sy.dma_start(out=o, in_=i_),
                    reads=[x2_b[sub]], writes=[x2f_b[par]], inc=True, tag="E")
            st = stn[par]
            ysl = ysb[:, sub * D:(sub + 1) * D]
            ln_stats(st, ysl, ysb_b[sub], "E")
            ln_norm(st, outf[par][:], outf_b[par], ysl, ysb_b[sub], "E")
            EM.emit("vector", lambda v, o=outf[par][:], x_=x2f[par][:]:
                    v.tensor_tensor(o, o, x_, OP.add),
                    reads=[outf_b[par], x2f_b[par]], writes=[outf_b[par]], inc=True, tag="E")
            EM.emit("sync",
                    lambda sy, o=d_out[sub * 128:(sub + 1) * 128, :], i_=outf[par][:]:
                    sy.dma_start(out=o, in_=i_),
                    reads=[outf_b[par]], writes=[out_done[sub]], inc=True, tag="E")
        EM.emit("sync", None, reads=out_done, tag="E")

        blk = outer.enter_context(nc.Block())
        EM.replay(nc, blk, sems)
    return nc


def _prep(L0_attn_x, L0_attn_Wqkv, L0_attn_bqkv, L0_attn_Wo, L0_attn_bo,
          L0_ffn_W_gate, L0_ffn_W_up, L0_ffn_W_down):
    x = np.ascontiguousarray(np.asarray(L0_attn_x, dtype=np.float32).reshape(S, D))
    Wqkv = np.asarray(L0_attn_Wqkv, dtype=np.float32)
    wq_t = np.ascontiguousarray(
        Wqkv.reshape(32, 128, 24, 128).transpose(2, 1, 0, 3)).reshape(24 * 128, D)
    bqkv_t = np.ascontiguousarray(
        np.asarray(L0_attn_bqkv, dtype=np.float32).reshape(24, 128).T)
    wo = np.ascontiguousarray(np.asarray(L0_attn_Wo, dtype=np.float32))
    bo_b = np.ascontiguousarray(
        np.broadcast_to(np.asarray(L0_attn_bo, dtype=np.float32), (128, D)))
    wg_t = np.ascontiguousarray(
        np.asarray(L0_ffn_W_gate, dtype=np.float32)
        .reshape(NFC, 128, 32, 128).transpose(3, 0, 2, 1)).reshape(128, NFC * D)
    wu_t = np.ascontiguousarray(
        np.asarray(L0_ffn_W_up, dtype=np.float32)
        .reshape(NFC, 128, 32, 128).transpose(3, 0, 2, 1)).reshape(128, NFC * D)
    wd_t = np.ascontiguousarray(
        np.asarray(L0_ffn_W_down, dtype=np.float32)
        .reshape(8, 512, NFC, 128).transpose(0, 3, 2, 1)).reshape(8 * 128, NFC * 512)
    ident = np.eye(128, dtype=np.float32)
    common = {"wq": wq_t, "bqkv": bqkv_t, "wo": wo, "bo": bo_b,
              "wg": wg_t, "wu": wu_t, "wd": wd_t, "ident": ident}
    return x, common


def kernel(L0_attn_x, L0_attn_Wqkv, L0_attn_bqkv, L0_attn_Wo, L0_attn_bo,
           L0_ffn_W_gate, L0_ffn_W_up, L0_ffn_W_down, _dbg=False):
    x, common = _prep(L0_attn_x, L0_attn_Wqkv, L0_attn_bqkv, L0_attn_Wo,
                      L0_attn_bo, L0_ffn_W_gate, L0_ffn_W_up, L0_ffn_W_down)
    nc = build_program(dbg=_dbg)
    in_maps = [dict(common, x=np.ascontiguousarray(x[c * T:(c + 1) * T]))
               for c in range(8)]
    res = run_bass_kernel_spmd(nc, in_maps, core_ids=list(range(8)))
    out = np.concatenate([res.results[c]["out"] for c in range(8)], axis=0)
    out = out.reshape(1, S, D).astype(np.float32)
    if _dbg:
        return out, res
    return out


# version 12
# speedup vs baseline: 1.2518x; 1.0039x over previous
import numpy as np
from contextlib import ExitStack

import concourse.bass as bass
import concourse.mybir as mybir
from concourse.bass_utils import run_bass_kernel_spmd

F32 = mybir.dt.float32
BF16 = mybir.dt.bfloat16
AF = mybir.ActivationFunctionType
OP = mybir.AluOpType
AF_COPY = getattr(AF, "Copy", None)
if AF_COPY is None:
    AF_COPY = getattr(AF, "Identity")

ENGINES = ("tensor", "vector", "scalar", "sync", "gpsimd")
ENG_SEM = {"tensor": "mt", "vector": "mv", "scalar": "ms", "sync": "dsy", "gpsimd": "dgp"}
COMPUTE = ("tensor", "vector", "scalar")


class Buf:
    __slots__ = ("name", "ready", "readers")

    def __init__(self, name=""):
        self.name = name
        self.ready = None
        self.readers = {}


def bufs(n, name):
    return [Buf(f"{name}{i}") for i in range(n)]


class Em:
    def __init__(self):
        self.ops = {e: [] for e in ENGINES}
        self.cnt = {s: 0 for s in ENG_SEM.values()}
        self.cnt_sbuf = {s: 0 for s in ENG_SEM.values()}
        self.waited = {e: {} for e in ENGINES}
        self.pfence = {e: None for e in ENGINES}
        self.last_writes = {e: frozenset() for e in ENGINES}

    def emit(self, eng, build, reads=(), writes=(), inc=False, record=True,
             dram_only=False, tag=""):
        own = ENG_SEM[eng]
        if eng == "vector" and any(id(b) in self.last_writes[eng] for b in reads):
            self.ops[eng].append(((), lambda v: v.engine_nop(), None, "nopg"))
            self.last_writes[eng] = frozenset()
        waits = {}

        def need(sem, val, why):
            if sem == own and eng in COMPUTE:
                return
            if val is None:
                raise RuntimeError(f"un-inc'd dependency: {why} [{tag}] on {eng}")
            if self.waited[eng].get(sem, 0) >= val:
                return
            if waits.get(sem, 0) < val:
                waits[sem] = val

        if eng in ("sync", "gpsimd"):
            need(own, self.cnt[own], "dma-serialize")
        pf = self.pfence[eng]
        if pf:
            self.pfence[eng] = None
            for s, v in pf.items():
                need(s, v, "fence")
        for b in reads:
            r = b.ready
            if r is None:
                continue
            if r[0] == "eng":
                if r[1] != eng:
                    raise RuntimeError(f"cross-engine read of un-inc'd {b.name} [{tag}]")
            else:
                need(r[0], r[1], f"read {b.name}")
        for b in writes:
            for s, v in b.readers.items():
                if s == own and eng in COMPUTE:
                    continue
                need(s, v, f"WAR {b.name}")
            r = b.ready
            if r is not None:
                if r[0] == "eng":
                    if r[1] != eng:
                        raise RuntimeError(f"WAW cross-engine un-inc'd {b.name} [{tag}]")
                else:
                    need(r[0], r[1], f"WAW {b.name}")
        increc = None
        if inc:
            amt = (16 if eng in ("sync", "gpsimd") else 1) if inc is True else int(inc)
            self.cnt[own] += amt
            if not dram_only:
                self.cnt_sbuf[own] = self.cnt[own]
            increc = (own, amt)
        self.ops[eng].append((tuple(sorted(waits.items())), build, increc, tag))
        self.last_writes[eng] = frozenset(id(b) for b in writes)
        for s, v in waits.items():
            self.waited[eng][s] = v
        mark = (own, self.cnt[own]) if inc else ("eng", eng)
        for b in writes:
            b.ready = mark
            b.readers = {}
        if record:
            for b in reads:
                b.readers[own] = self.cnt[own] if inc else None

    def note_readers(self, rbufs, eng):
        own = ENG_SEM[eng]
        for b in rbufs:
            b.readers[own] = self.cnt[own]

    def fence(self):
        snap = {s: v for s, v in self.cnt_sbuf.items() if v > 0}
        for e in ENGINES:
            pf = self.pfence[e] or {}
            keys = set(pf) | set(snap)
            self.pfence[e] = {s: max(pf.get(s, 0), snap.get(s, 0)) for s in keys}

    def replay(self, nc, blk, sems):
        for eng in ENGINES:
            ops = self.ops[eng]

            def make(ops=ops):
                def body(e):
                    for waits, build, increc, tag in ops:
                        for s, v in waits:
                            e.wait_ge(sems[s], v)
                        ins = build(e) if build is not None else None
                        if increc is not None:
                            ins.then_inc(sems[increc[0]], increc[1])
                return body
            getattr(blk, eng)(make())


T = 1024
S = 8192
D = 4096
DH = 1024
F = 11008
NT = 8
NFC = 86
FHC = 43
C_SHIFT = -12.0
SCALE = 1.0 / 32.0


def build_program(dbg=False):
    nc = bass.Bass("TRN2", target_bir_lowering=False)

    d_x = nc.dram_tensor("x", [T, D], F32, kind="ExternalInput").ap()
    d_wq = nc.dram_tensor("wq", [24 * 128, D], F32, kind="ExternalInput").ap()
    d_bqkv = nc.dram_tensor("bqkv", [128, 24], F32, kind="ExternalInput").ap()
    d_wo = nc.dram_tensor("wo", [DH, D], F32, kind="ExternalInput").ap()
    d_bo = nc.dram_tensor("bo", [128, D], F32, kind="ExternalInput").ap()
    d_wg = nc.dram_tensor("wg", [128, NFC * D], F32, kind="ExternalInput").ap()
    d_wu = nc.dram_tensor("wu", [128, NFC * D], F32, kind="ExternalInput").ap()
    d_wd = nc.dram_tensor("wd", [8 * 128, NFC * 512], F32, kind="ExternalInput").ap()
    d_id = nc.dram_tensor("ident", [128, 128], F32, kind="ExternalInput").ap()
    d_out = nc.dram_tensor("out", [T, D], F32, kind="ExternalOutput").ap()

    xk = "ExternalOutput" if dbg else "Internal"
    d_x2 = nc.dram_tensor("x2d", [T, D], BF16, kind=xk).ap()
    d_y0 = nc.dram_tensor("y0d", [T, D], BF16, kind=xk).ap()
    d_cik = nc.dram_tensor("cc_in_k", [DH, T], BF16, kind="Internal").ap()
    d_civ = nc.dram_tensor("cc_in_v", [T, DH], BF16, kind="Internal").ap()
    d_cok = nc.dram_tensor("cc_out_k", [8 * DH, T], BF16, addr_space="Shared", kind="Internal").ap()
    d_cov = nc.dram_tensor("cc_out_v", [S, DH], BF16, addr_space="Shared", kind="Internal").ap()
    if dbg:
        d_dbg = {
            "d_qfm": nc.dram_tensor("d_qfm", [128, 8 * T], BF16, kind="ExternalOutput").ap(),
            "d_kfm": nc.dram_tensor("d_kfm", [128, 8 * T], BF16, kind="ExternalOutput").ap(),
            "d_vtm": nc.dram_tensor("d_vtm", [128, 8 * T], BF16, kind="ExternalOutput").ap(),
            "d_osb": nc.dram_tensor("d_osb", [128, 8 * T], BF16, kind="ExternalOutput").ap(),
            "d_den": nc.dram_tensor("d_den", [128, 8], F32, kind="ExternalOutput").ap(),
            "d_a0": nc.dram_tensor("d_a0", [128, D], F32, kind="ExternalOutput").ap(),
            "d_h0": nc.dram_tensor("d_h0", [128, FHC * T], BF16, kind="ExternalOutput").ap(),
            "d_ln3": nc.dram_tensor("d_ln3", [128, 32 * T], BF16, kind="ExternalOutput").ap(),
            "d_ysb": nc.dram_tensor("d_ysb", [128, 8 * D], BF16, kind="ExternalOutput").ap(),
            "d_xt0": nc.dram_tensor("d_xt0", [128, D], F32, kind="ExternalOutput").ap(),
            "d_ln1t0": nc.dram_tensor("d_ln1t0", [128, D], F32, kind="ExternalOutput").ap(),
            "d_ln1fm": nc.dram_tensor("d_ln1fm", [128, 32 * 512], BF16, kind="ExternalOutput").ap(),
            "d_wq0": nc.dram_tensor("d_wq0", [128, D], BF16, kind="ExternalOutput").ap(),
        }

    EM = Em()
    outer = ExitStack()
    live = {}

    def alloc(name, shape, dtype, kind="sbuf", side="left"):
        if kind == "sbuf":
            cm = nc.sbuf_tensor(name, shape, dtype, side=side)
        else:
            cm = nc.psum_tensor(name, shape, dtype)
        ap = cm.__enter__()
        live[name] = cm
        return ap

    def free(*names):
        for n in names:
            live.pop(n).__exit__(None, None, None)

    with outer:
        sems = {s: outer.enter_context(nc.semaphore(s)) for s in ENG_SEM.values()}

        P = [alloc(f"p{i}", [128, 512], F32, "psum") for i in range(6)]
        Pb = bufs(6, "psum")

        ident = alloc("ident_sb", [128, 128], F32)
        ident_b = Buf("ident")
        c_eps = alloc("c_eps", [128, 1], F32)
        c_s32 = alloc("c_s32", [128, 1], F32)
        c_nb12 = alloc("c_nb12", [128, 1], F32)
        c_i4096 = alloc("c_i4096", [128, 1], F32)
        ones_bf = alloc("ones_bf", [128, 1], BF16)
        consts_b = Buf("consts")
        bqkv_sb = alloc("bqkv_sb", [128, 24], F32)
        bqkv_b = Buf("bqkv")

        scr_bf = alloc("scr_bf", [128, D], BF16)
        scr_b = Buf("scr")
        stn = []
        for i in range(2):
            stn.append({
                "sums": alloc(f"sums_{i}", [128, 2], F32), "sumsb": Buf(f"sums_{i}"),
                "mean": alloc(f"mean_{i}", [128, 1], F32), "meanb": Buf(f"mean_{i}"),
                "msq": alloc(f"msq_{i}", [128, 1], F32), "msqb": Buf(f"msq_{i}"),
                "veps": alloc(f"veps_{i}", [128, 1], F32), "vepsb": Buf(f"veps_{i}"),
                "std": alloc(f"std_{i}", [128, 1], F32), "stdb": Buf(f"std_{i}"),
                "rcp": alloc(f"rcp_{i}", [128, 1], F32), "rcpb": Buf(f"rcp_{i}"),
            })

        def ln_stats(st, src_ap, src_buf, tag):
            EM.emit("vector", lambda v, o=st["sums"][:, 0:1], i_=src_ap:
                    v.tensor_reduce(o, i_, mybir.AxisListType.X, OP.add),
                    reads=[src_buf], writes=[st["sumsb"]], inc=True, tag=tag)
            EM.emit("scalar", lambda s, o=scr_bf[:], i_=src_ap, a=st["sums"][:, 1:2]:
                    s.activation(o, i_, AF.Square, accum_out=a),
                    reads=[src_buf], writes=[scr_b, st["sumsb"]], inc=True, tag=tag)
            EM.emit("scalar", lambda s, o=st["mean"][:], i_=st["sums"][:, 0:1], sc=c_i4096[:, 0:1]:
                    s.activation(o, i_, AF_COPY, scale=sc),
                    reads=[st["sumsb"], consts_b], writes=[st["meanb"]], inc=True, tag=tag)
            EM.emit("scalar", lambda s, o=st["msq"][:], i_=st["mean"][:]:
                    s.activation(o, i_, AF.Square),
                    reads=[st["meanb"]], writes=[st["msqb"]], inc=True, tag=tag)
            EM.emit("vector", lambda v, o=st["veps"][:], i_=st["sums"][:, 1:2], m2=st["msq"][:, 0:1]:
                    v.tensor_scalar(o, i_, 1.0 / D, m2, OP.mult, OP.subtract),
                    reads=[st["sumsb"], st["msqb"]], writes=[st["vepsb"]], inc=True, tag=tag)
            EM.emit("scalar", lambda s, o=st["std"][:], i_=st["veps"][:], b=c_eps[:, 0:1]:
                    s.activation(o, i_, AF.Sqrt, bias=b),
                    reads=[st["vepsb"], consts_b], writes=[st["stdb"]], inc=True, tag=tag)
            EM.emit("vector", lambda v, o=st["rcp"][:], i_=st["std"][:]: v.reciprocal(o, i_),
                    reads=[st["stdb"]], writes=[st["rcpb"]], inc=True, tag=tag)

        def ln_norm(st, out_ap, out_buf, src_ap, src_buf, tag):
            EM.emit("vector",
                    lambda v, o=out_ap, i_=src_ap, b=st["mean"][:, 0:1], sc=st["rcp"][:, 0:1]:
                    v.tensor_scalar(o, i_, b, sc, OP.subtract, OP.mult),
                    reads=[src_buf, st["meanb"], st["rcpb"]], writes=[out_buf], inc=True, tag=tag)

        EM.emit("vector", lambda v: v.memset(c_eps[:], 1e-5), writes=[consts_b], tag="init")
        EM.emit("vector", lambda v: v.memset(c_s32[:], SCALE), writes=[consts_b], tag="init")
        EM.emit("vector", lambda v: v.memset(c_nb12[:], C_SHIFT), writes=[consts_b], tag="init")
        EM.emit("vector", lambda v: v.memset(c_i4096[:], 1.0 / D), writes=[consts_b], tag="init")
        EM.emit("vector", lambda v: v.memset(ones_bf[:], 1.0), writes=[consts_b], inc=True, tag="init")
        EM.emit("sync", lambda sy: sy.dma_start(out=ident[:], in_=d_id),
                writes=[ident_b], inc=True, tag="init")
        identb = alloc("identb_sb", [128, 128], BF16)
        identb_b = Buf("identb")
        EM.emit("gpsimd", lambda g: g.dma_start(out=identb[:], in_=d_id),
                writes=[identb_b], inc=True, tag="init")
        EM.emit("sync", lambda sy: sy.dma_start(out=bqkv_sb[:], in_=d_bqkv),
                writes=[bqkv_b], inc=True, tag="init")

        ptr = [alloc(f"ptr{i}", [128, 1024], BF16, "psum") for i in range(2)]
        ptrb = bufs(2, "ptr")
        q_fm = alloc("q_fm", [128, 8 * T], BF16)
        q_fm_b = Buf("q_fm")
        k_fm = alloc("k_fm", [128, 8 * T], BF16)
        k_fm_b = Buf("k_fm")
        v_fm = alloc("v_fm", [128, 8 * T], BF16)
        v_fm_b = Buf("v_fm")
        v_tm = alloc("v_tm", [128, 8 * T], BF16)
        v_tm_b = Buf("v_tm")
        x_t = [alloc(f"x_t{i}", [128, D], F32) for i in range(2)]
        x_t_b = bufs(2, "x_t")
        ln1_t = [alloc(f"ln1_t{i}", [128, D], BF16) for i in range(2)]
        ln1_t_b = bufs(2, "ln1_t")
        ln1_fm = alloc("ln1_fm", [128, 32 * T], BF16)
        ln1_fm_b = Buf("ln1_fm")
        wq_sl = [alloc(f"wq_sl{i}", [128, D], BF16) for i in range(2)]
        wq_sl_b = bufs(2, "wq_sl")

        def transpose_block(trp, trb, src_ap, src_buf, n_dc, dst_fn, tag):
            for g in range(n_dc // 4):
                bank = g % 2
                for j in range(4):
                    dc = g * 4 + j
                    EM.emit("tensor",
                            lambda t, o=trp[bank][:, j * 128:(j + 1) * 128],
                            i_=src_ap[:, dc * 128:(dc + 1) * 128], idn=identb[:]:
                            t.transpose(o, i_, idn),
                            reads=[src_buf, identb_b], writes=[trb[bank]],
                            inc=(j == 3), record=(j == 3), tag=tag)
                for j in range(4):
                    dc = g * 4 + j
                    dst_ap, dst_buf = dst_fn(dc)
                    EM.emit("vector",
                            lambda v, o=dst_ap, i_=trp[bank][:, j * 128:(j + 1) * 128]:
                            v.tensor_copy(o, i_),
                            reads=[trb[bank]], writes=[dst_buf],
                            inc=(j == 3), record=(j == 3), tag=tag)

        for s_g in range(8):
            par = s_g % 2
            EM.emit("sync",
                    lambda sy, o=x_t[par][:], i_=d_x[s_g * 128:(s_g + 1) * 128, :]:
                    sy.dma_start(out=o, in_=i_),
                    writes=[x_t_b[par]], inc=True, tag="A_ln")
            st = stn[par]
            ln_stats(st, x_t[par][:], x_t_b[par], "A_ln")
            ln_norm(st, ln1_t[par][:], ln1_t_b[par], x_t[par][:], x_t_b[par], "A_ln")
            if dbg and s_g == 0:
                EM.emit("sync", lambda sy: sy.dma_start(out=d_dbg["d_xt0"], in_=x_t[0][:]),
                        reads=[x_t_b[0]], inc=True, tag="dbg")
                EM.emit("gpsimd", lambda g: g.dma_start(out=d_dbg["d_ln1t0"], in_=ln1_t[0][:]),
                        reads=[ln1_t_b[0]], inc=True, tag="dbg")
            transpose_block(
                ptr, ptrb, ln1_t[par][:], ln1_t_b[par], 32,
                lambda dc, s_g=s_g: (ln1_fm[:, dc * T + s_g * 128: dc * T + (s_g + 1) * 128], ln1_fm_b),
                "A_tr")
        if dbg:
            EM.emit("sync", lambda sy: sy.dma_start(out=d_dbg["d_ln1fm"], in_=ln1_fm[:, 0:32 * 512]),
                    reads=[ln1_fm_b], inc=True, tag="dbg")
        cik_b = bufs(8, "cik")
        civ_b = bufs(8, "civ")
        cok_b = Buf("cok")
        cov_b = Buf("cov")
        for fc in list(range(8, 24)) + list(range(0, 8)):
            par = fc % 2
            EM.emit("gpsimd",
                    lambda g, o=wq_sl[par][:], i_=d_wq[fc * 128:(fc + 1) * 128, :]:
                    g.dma_start(out=o, in_=i_),
                    writes=[wq_sl_b[par]], inc=True, tag="A_qkv")
            if dbg and fc == 0:
                EM.emit("sync", lambda sy: sy.dma_start(out=d_dbg["d_wq0"], in_=wq_sl[0][:]),
                        reads=[wq_sl_b[0]], inc=True, tag="dbg")
            if fc < 8:
                dst_ap, dst_buf = q_fm, q_fm_b
                col0 = fc * T
            elif fc < 16:
                dst_ap, dst_buf = k_fm, k_fm_b
                col0 = (fc - 8) * T
            else:
                dst_ap, dst_buf = v_fm, v_fm_b
                col0 = (fc - 16) * T
            for h in range(2):
                bank = (fc * 2 + h) % 2
                for dc in range(32):
                    EM.emit("tensor",
                            lambda t, o=P[bank][:, 0:512],
                            l=wq_sl[par][:, dc * 128:(dc + 1) * 128],
                            r=ln1_fm[:, dc * T + h * 512: dc * T + (h + 1) * 512],
                            st_=(dc == 0), sp=(dc == 31):
                            t.matmul(o, l, r, start=st_, stop=sp, skip_group_check=True),
                            reads=[wq_sl_b[par], ln1_fm_b], writes=[Pb[bank]],
                            inc=(dc == 31), record=(dc == 31), tag="A_qkv")
                EM.emit("vector",
                        lambda v, o=dst_ap[:, col0 + h * 512: col0 + (h + 1) * 512],
                        i_=P[bank][:, 0:512], b=bqkv_sb[:, fc:fc + 1]:
                        v.tensor_scalar_add(o, i_, b),
                        reads=[Pb[bank], bqkv_b], writes=[dst_buf], inc=True, tag="A_qkv")
            if fc == 15:
                for i in range(8):
                    EM.emit("sync",
                            lambda sy, o=d_cik[i * 128:(i + 1) * 128, :], i_=k_fm[:, i * T:(i + 1) * T]:
                            sy.dma_start(out=o, in_=i_),
                            reads=[k_fm_b], writes=[cik_b[i]], inc=True, tag="A_cc")
            if fc == 23:
                for kt in range(8):
                    for g2 in range(2):
                        bank = (kt * 2 + g2) % 2
                        for j in range(4):
                            dhc = g2 * 4 + j
                            EM.emit("tensor",
                                    lambda t, o=ptr[bank][:, j * 128:(j + 1) * 128],
                                    i_=v_fm[:, dhc * T + kt * 128: dhc * T + (kt + 1) * 128], idn=identb[:]:
                                    t.transpose(o, i_, idn),
                                    reads=[v_fm_b, identb_b], writes=[ptrb[bank]],
                                    inc=(j == 3), record=(j == 3), tag="A_vtr")
                        EM.emit("vector",
                                lambda v, o=v_tm[:, kt * T + g2 * 512: kt * T + (g2 + 1) * 512],
                                i_=ptr[bank][:, 0:512]:
                                v.tensor_copy(o, i_),
                                reads=[ptrb[bank]], writes=[v_tm_b], inc=True, tag="A_vtr")
                for i in range(8):
                    EM.emit("sync",
                            lambda sy, o=d_civ[i * 128:(i + 1) * 128, :], i_=v_tm[:, i * T:(i + 1) * T]:
                            sy.dma_start(out=o, in_=i_),
                            reads=[v_tm_b], writes=[civ_b[i]], inc=True, tag="A_cc")

        EM.emit("gpsimd",
                lambda g: g.collective_compute(
                    "AllGather", OP.bypass,
                    replica_groups=[[0, 1, 2, 3, 4, 5, 6, 7]],
                    ins=[d_cik], outs=[d_cok]),
                reads=cik_b, writes=[cok_b], inc=1, tag="A_cc")
        EM.emit("gpsimd",
                lambda g: g.collective_compute(
                    "AllGather", OP.bypass,
                    replica_groups=[[0, 1, 2, 3, 4, 5, 6, 7]],
                    ins=[d_civ], outs=[d_cov]),
                reads=civ_b, writes=[cov_b], inc=1, tag="A_cc")

        if dbg:
            EM.emit("sync", lambda sy: sy.dma_start(out=d_dbg["d_qfm"], in_=q_fm[:]),
                    reads=[q_fm_b], inc=True, tag="dbg")
            EM.emit("sync", lambda sy: sy.dma_start(out=d_dbg["d_kfm"], in_=k_fm[:]),
                    reads=[k_fm_b], inc=True, tag="dbg")
            EM.emit("sync", lambda sy: sy.dma_start(out=d_dbg["d_vtm"], in_=v_tm[:]),
                    reads=[v_tm_b], inc=True, tag="dbg")

        EM.fence()
        free("wq_sl1", "wq_sl0", "ln1_fm", "ln1_t1", "ln1_t0", "x_t1", "x_t0",
             "v_tm", "v_fm", "k_fm")
        free("ptr1", "ptr0")
        pden = alloc("pden", [128, 512], F32, "psum")
        pden_b = Buf("pden")

        wo_bf = alloc("wo_bf", [128, 8 * D], BF16, side="right")
        wo_b = bufs(8, "wo")
        o_sb = alloc("o_sb", [128, 8 * T], BF16, side="right")
        o_sb_b = Buf("o_sb")
        bo_bc = alloc("bo_bc", [128, D], BF16, side="right")
        bo_b = Buf("bo")
        rden = alloc("rden", [128, 8], F32, side="right")
        rden_b = Buf("rden")
        k_sl = [alloc(f"k_sl{i}", [128, 8 * T], BF16) for i in range(2)]
        k_sl_b = [bufs(8, f"k_sl{i}_") for i in range(2)]
        v_sl = [alloc(f"v_sl{i}", [128, 8 * T], BF16) for i in range(2)]
        v_sl_b = [bufs(8, f"v_sl{i}_") for i in range(2)]
        p_t8 = alloc("p_t8", [128, 8 * 512], BF16)
        p_t8_b = bufs(8, "p_t8")

        for i in range(8):
            EM.emit("gpsimd",
                    lambda g, o=wo_bf[:, i * D:(i + 1) * D], i_=d_wo[i * 128:(i + 1) * 128, :]:
                    g.dma_start(out=o, in_=i_),
                    writes=[wo_b[i]], inc=True, tag="B")
        EM.emit("gpsimd", lambda g: g.dma_start(out=bo_bc[:], in_=d_bo),
                writes=[bo_b], inc=True, tag="B")

        for cb in range(8):
            par = cb % 2
            for i in range(8):
                EM.emit("sync",
                        lambda sy, o=k_sl[par][:, i * T:(i + 1) * T],
                        i_=d_cok[cb * DH + i * 128: cb * DH + (i + 1) * 128, :]:
                        sy.dma_start(out=o, in_=i_),
                        reads=[cok_b], writes=[k_sl_b[par][i]], inc=True, tag="B")
            for i in range(8):
                EM.emit("sync",
                        lambda sy, o=v_sl[par][:, i * T:(i + 1) * T],
                        i_=d_cov[cb * T + i * 128: cb * T + (i + 1) * 128, :]:
                        sy.dma_start(out=o, in_=i_),
                        reads=[cov_b], writes=[v_sl_b[par][i]], inc=True, tag="B")

            for qp in range(2):
                for kt in range(8):
                    bank = kt % 2
                    for dhc in range(8):
                        EM.emit("tensor",
                                lambda t, o=P[bank][:, 0:512],
                                l=k_sl[par][:, dhc * T + kt * 128: dhc * T + (kt + 1) * 128],
                                r=q_fm[:, dhc * T + qp * 512: dhc * T + (qp + 1) * 512],
                                st_=(dhc == 0), sp=(dhc == 7):
                                t.matmul(o, l, r, start=st_, stop=sp, skip_group_check=True),
                                reads=[k_sl_b[par][dhc], q_fm_b], writes=[Pb[bank]],
                                inc=(dhc == 7), record=False, tag="B_s")
                    EM.note_readers([k_sl_b[par][d_] for d_ in range(8)] + [q_fm_b], "tensor")
                    EM.emit("scalar",
                            lambda s, o=p_t8[:, kt * 512:(kt + 1) * 512], i_=P[bank][:, 0:512],
                            b=c_nb12[:, 0:1], sc=c_s32[:, 0:1]:
                            s.activation(o, i_, AF.Exp, bias=b, scale=sc),
                            reads=[Pb[bank], consts_b], writes=[p_t8_b[kt]], inc=True, tag="B_s")
                    for sub in range(4):
                        EM.emit("tensor",
                                lambda t, o=pden[:, qp * 4 + sub: qp * 4 + sub + 1],
                                l=p_t8[:, kt * 512 + sub * 128: kt * 512 + (sub + 1) * 128],
                                r=ones_bf[:],
                                st_=(cb == 0 and qp == 0 and kt == 0 and sub == 0),
                                sp=(cb == 7 and qp == 1 and kt == 7 and sub == 3):
                                t.matmul(o, l, r, start=st_, stop=sp, skip_group_check=True),
                                reads=[p_t8_b[kt], consts_b], writes=[pden_b],
                                inc=(cb == 7 and qp == 1 and kt == 7 and sub == 3),
                                record=False, tag="B_d")
                for qh in range(2):
                    qg = qp * 2 + qh
                    for kt in range(8):
                        for dhc in range(8):
                            ob = 2 + dhc // 2
                            EM.emit("tensor",
                                    lambda t, o=P[ob][:, (dhc % 2) * 256:(dhc % 2) * 256 + 256],
                                    l=v_sl[par][:, kt * T + dhc * 128: kt * T + (dhc + 1) * 128],
                                    r=p_t8[:, kt * 512 + qh * 256: kt * 512 + qh * 256 + 256],
                                    st_=(kt == 0 and dhc % 2 == 0),
                                    sp=(kt == 7 and dhc % 2 == 1):
                                    t.matmul(o, l, r, start=st_, stop=sp, skip_group_check=True),
                                    reads=[v_sl_b[par][kt], p_t8_b[kt]], writes=[Pb[ob]],
                                    inc=(dhc == 7 or (kt == 7 and dhc % 2 == 1)),
                                    record=False, tag="B_pv")
                    EM.note_readers([v_sl_b[par][k_] for k_ in range(8)]
                                    + [p_t8_b[k_] for k_ in range(8)] + [consts_b], "tensor")
                    for dhc in range(8):
                        ob = 2 + dhc // 2
                        osl = o_sb[:, dhc * T + qg * 256: dhc * T + (qg + 1) * 256]
                        psl = P[ob][:, (dhc % 2) * 256:(dhc % 2) * 256 + 256]
                        if cb == 0:
                            EM.emit("vector", lambda v, o=osl, i_=psl: v.tensor_copy(o, i_),
                                    reads=[Pb[ob]], writes=[o_sb_b], inc=True, tag="B_ev")
                        else:
                            EM.emit("vector", lambda v, o=osl, i_=psl: v.tensor_tensor(o, o, i_, OP.add),
                                    reads=[Pb[ob], o_sb_b], writes=[o_sb_b], inc=True, tag="B_ev")

        EM.emit("vector", lambda v: v.reciprocal(rden[:], pden[:, 0:8]),
                reads=[pden_b], writes=[rden_b], inc=True, tag="B")

        if dbg:
            EM.emit("sync", lambda sy: sy.dma_start(out=d_dbg["d_osb"], in_=o_sb[:]),
                    reads=[o_sb_b], inc=True, tag="dbg")
            EM.emit("sync", lambda sy: sy.dma_start(out=d_dbg["d_den"], in_=rden[:]),
                    reads=[rden_b], inc=True, tag="dbg")

        EM.fence()
        free("p_t8", "v_sl1", "v_sl0", "k_sl1", "k_sl0", "q_fm")
        free("pden")

        a_sb = [alloc(f"a_sb{i}", [128, D], F32) for i in range(3)]
        a_sb_b = bufs(3, "a_sb")
        lnb = [alloc("lnb0", [128, D], F32)]
        lnb_b = bufs(1, "lnb")
        xr_t = [alloc(f"xr_t{i}", [128, D], F32) for i in range(2)]
        xr_t_b = bufs(2, "xr")
        x2_b = bufs(8, "x2")

        for sg in range(8):
            par = sg % 3
            spar = sg % 2
            EM.emit("sync",
                    lambda sy, o=xr_t[spar][:], i_=d_x[sg * 128:(sg + 1) * 128, :]:
                    sy.dma_start(out=o, in_=i_),
                    writes=[xr_t_b[spar]], inc=True, tag="C")
            for cc in range(8):
                bank = cc % 2
                for dhc in range(8):
                    EM.emit("tensor",
                            lambda t, o=P[bank][:, 0:512],
                            l=o_sb[:, dhc * T + sg * 128: dhc * T + (sg + 1) * 128],
                            r=wo_bf[:, dhc * D + cc * 512: dhc * D + (cc + 1) * 512],
                            st_=(dhc == 0), sp=(dhc == 7):
                            t.matmul(o, l, r, start=st_, stop=sp, skip_group_check=True),
                            reads=[o_sb_b, wo_b[dhc]], writes=[Pb[bank]],
                            inc=(dhc == 7), record=False, tag="C_mm")
                EM.note_readers([o_sb_b] + wo_b, "tensor")
                EM.emit("scalar",
                        lambda s, o=a_sb[par][:, cc * 512:(cc + 1) * 512],
                        i_=P[bank][:, 0:512], sc=rden[:, sg:sg + 1]:
                        s.activation(o, i_, AF_COPY, scale=sc),
                        reads=[Pb[bank], rden_b], writes=[a_sb_b[par]], inc=True, tag="C_mm")
            EM.emit("vector", lambda v, o=a_sb[par][:], b=bo_bc[:]:
                    v.tensor_tensor(o, o, b, OP.add),
                    reads=[a_sb_b[par], bo_b], writes=[a_sb_b[par]], inc=True, tag="C")
            if dbg and sg == 0:
                EM.emit("sync", lambda sy: sy.dma_start(out=d_dbg["d_a0"], in_=a_sb[0][:]),
                        reads=[a_sb_b[0]], inc=True, tag="dbg")
            st = stn[sg % 2]
            ln_stats(st, a_sb[par][:], a_sb_b[par], "C")
            ln_norm(st, lnb[0][:], lnb_b[0], a_sb[par][:], a_sb_b[par], "C")
            EM.emit("vector", lambda v, o=lnb[0][:], x_=xr_t[spar][:]:
                    v.tensor_tensor(o, o, x_, OP.add),
                    reads=[lnb_b[0], xr_t_b[spar]], writes=[lnb_b[0]], inc=True, tag="C")
            EM.emit("gpsimd",
                    lambda g, o=d_x2[sg * 128:(sg + 1) * 128, :], i_=lnb[0][:]:
                    g.dma_start(out=o, in_=i_),
                    reads=[lnb_b[0]], writes=[x2_b[sg]], inc=True, tag="C")

        EM.fence()
        free("xr_t1", "xr_t0", "lnb0", "a_sb2", "a_sb1", "a_sb0",
             "rden", "bo_bc", "o_sb", "wo_bf")

        ln3_fm = alloc("ln3_fm", [128, 32 * T], BF16, side="right")
        ln3_fm_b = Buf("ln3_fm")
        hsb = alloc("hsb", [128, FHC * T], BF16)
        hsb_b = Buf("hsb")
        x2t = [alloc(f"x2t{i}", [128, D], BF16) for i in range(2)]
        x2t_b = bufs(2, "x2t")
        ln3t = [alloc(f"ln3t{i}", [128, D], BF16) for i in range(2)]
        ln3t_b = bufs(2, "ln3t")
        ptr2 = [alloc(f"ptr2_{i}", [128, 1024], BF16, "psum") for i in range(2)]
        ptr2b = bufs(2, "ptr2")

        for sg in range(8):
            par = sg % 2
            EM.emit("sync",
                    lambda sy, o=x2t[par][:], i_=d_x2[sg * 128:(sg + 1) * 128, :]:
                    sy.dma_start(out=o, in_=i_),
                    reads=[x2_b[sg]], writes=[x2t_b[par]], inc=True, tag="Dp")
            st = stn[par]
            ln_stats(st, x2t[par][:], x2t_b[par], "Dp")
            ln_norm(st, ln3t[par][:], ln3t_b[par], x2t[par][:], x2t_b[par], "Dp")
            transpose_block(
                ptr2, ptr2b, ln3t[par][:], ln3t_b[par], 32,
                lambda dc, sg=sg: (ln3_fm[:, dc * T + sg * 128: dc * T + (sg + 1) * 128], ln3_fm_b),
                "Dp_tr")
        if dbg:
            EM.emit("sync", lambda sy: sy.dma_start(out=d_dbg["d_ln3"], in_=ln3_fm[:]),
                    reads=[ln3_fm_b], inc=True, tag="dbg")

        EM.fence()
        free("ptr2_1", "ptr2_0")
        free("ln3t1", "ln3t0", "x2t1", "x2t0")
        y0_b = {}
        ysb_b = bufs(8, "ysb")
        WSEG = 22 * 512

        for fh in range(2):
            if fh == 1:
                EM.fence()
            wg_sl = [alloc(f"wg_sl{fh}_{i}", [128, D], BF16) for i in range(2)]
            wg_sl_b = bufs(2, "wg_sl")
            wu_sl = [alloc(f"wu_sl{fh}_{i}", [128, D], BF16) for i in range(2)]
            wu_sl_b = bufs(2, "wu_sl")
            sg_t = [alloc(f"sg_t{fh}_{i}", [128, 512], F32) for i in range(2)]
            sg_t_b = bufs(2, "sg_t")
            for fc in range(FHC):
                fg = fh * FHC + fc
                par = fc % 2
                EM.emit("gpsimd",
                        lambda g, o=wg_sl[par][:], i_=d_wg[:, fg * D:(fg + 1) * D]:
                        g.dma_start(out=o, in_=i_),
                        writes=[wg_sl_b[par]], inc=True, tag="D_gu")
                EM.emit("gpsimd",
                        lambda g, o=wu_sl[par][:], i_=d_wu[:, fg * D:(fg + 1) * D]:
                        g.dma_start(out=o, in_=i_),
                        writes=[wu_sl_b[par]], inc=True, tag="D_gu")
                for grp in range(2):
                    gb, ub = grp, 2 + grp
                    for dc in range(32):
                        EM.emit("tensor",
                                lambda t, o=P[gb][:, 0:512],
                                l=wg_sl[par][:, dc * 128:(dc + 1) * 128],
                                r=ln3_fm[:, dc * T + grp * 512: dc * T + grp * 512 + 512],
                                st_=(dc == 0), sp=(dc == 31):
                                t.matmul(o, l, r, start=st_, stop=sp, skip_group_check=True),
                                reads=[wg_sl_b[par], ln3_fm_b], writes=[Pb[gb]],
                                inc=(dc == 31), record=False, tag="D_gu")
                    for dc in range(32):
                        EM.emit("tensor",
                                lambda t, o=P[ub][:, 0:512],
                                l=wu_sl[par][:, dc * 128:(dc + 1) * 128],
                                r=ln3_fm[:, dc * T + grp * 512: dc * T + grp * 512 + 512],
                                st_=(dc == 0), sp=(dc == 31):
                                t.matmul(o, l, r, start=st_, stop=sp, skip_group_check=True),
                                reads=[wu_sl_b[par], ln3_fm_b], writes=[Pb[ub]],
                                inc=(dc == 31), record=False, tag="D_gu")
                    EM.note_readers([wg_sl_b[par], wu_sl_b[par], ln3_fm_b], "tensor")
                    EM.emit("scalar",
                            lambda s, o=sg_t[grp][:], i_=P[gb][:, 0:512]:
                            s.activation(o, i_, AF.Silu),
                            reads=[Pb[gb]], writes=[sg_t_b[grp]], inc=True, tag="D_gu")
                    EM.emit("vector",
                            lambda v, o=hsb[:, fc * T + grp * 512: fc * T + grp * 512 + 512],
                            a=sg_t[grp][:], b=P[ub][:, 0:512]:
                            v.tensor_tensor(o, a, b, OP.mult),
                            reads=[sg_t_b[grp], Pb[ub]], writes=[hsb_b], inc=True, tag="D_gu")
            if dbg and fh == 0:
                EM.emit("sync", lambda sy: sy.dma_start(out=d_dbg["d_h0"], in_=hsb[:]),
                        reads=[hsb_b], inc=True, tag="dbg")
            EM.fence()
            free(f"sg_t{fh}_1", f"sg_t{fh}_0", f"wu_sl{fh}_1", f"wu_sl{fh}_0",
                 f"wg_sl{fh}_1", f"wg_sl{fh}_0")
            if fh == 0:
                wd_sl = [alloc(f"wd_sl0_{i}", [128, WSEG], BF16) for i in range(2)]
                wd_sl_b = bufs(2, "wd_sl")
                y0t = [alloc(f"y0t{i}", [128, 512], BF16) for i in range(4)]
                y0t_b = bufs(4, "y0t")
            else:
                free("ln3_fm")
                ysb = alloc("ysb", [128, 8 * D], BF16, side="right")
                wd_sl = [alloc(f"wd_sl1_{i}", [128, WSEG], BF16) for i in range(2)]
                wd_sl_b = bufs(2, "wd_sl")
                y0r = [alloc(f"y0r{i}", [128, 512], BF16) for i in range(4)]
                y0r_b = bufs(4, "y0r")
            y67 = [alloc(f"y67_{fh}_{i}", [128, 512], F32, "psum") for i in range(2)]
            PB8 = P + y67
            Pb8 = Pb + bufs(2, f"y67_{fh}_")

            def y0r_load(sub, cbk):
                par2 = sub % 4
                EM.emit("sync",
                        lambda sy, o=y0r[par2][:],
                        i_=d_y0[sub * 128:(sub + 1) * 128, cbk * 512:(cbk + 1) * 512]:
                        sy.dma_start(out=o, in_=i_),
                        reads=[y0_b[(sub, cbk)]], writes=[y0r_b[par2]], inc=True, tag="D_dn")

            for cbk in range(8):
                if fh == 1:
                    for sub in range(4):
                        y0r_load(sub, cbk)
                for h2 in range(2):
                    seg = 22 if h2 == 0 else 21
                    spar = (cbk * 2 + h2) % 2
                    col0 = (fh * FHC + h2 * 22) * 512
                    EM.emit("gpsimd",
                            lambda g, o=wd_sl[spar][:, 0:seg * 512],
                            i_=d_wd[cbk * 128:(cbk + 1) * 128, col0:col0 + seg * 512]:
                            g.dma_start(out=o, in_=i_),
                            writes=[wd_sl_b[spar]], inc=True, tag="D_dn")
                    for fcl in range(seg):
                        fc = h2 * 22 + fcl
                        for sub in range(8):
                            EM.emit("tensor",
                                    lambda t, o=PB8[sub][:, 0:512],
                                    l=hsb[:, fc * T + sub * 128: fc * T + (sub + 1) * 128],
                                    r=wd_sl[spar][:, fcl * 512:(fcl + 1) * 512],
                                    st_=(fc == 0), sp=(fc == FHC - 1):
                                    t.matmul(o, l, r, start=st_, stop=sp, skip_group_check=True),
                                    reads=[hsb_b, wd_sl_b[spar]], writes=[Pb8[sub]],
                                    inc=(sub == 7 or fc == FHC - 1), record=False, tag="D_dn")
                    EM.note_readers([hsb_b, wd_sl_b[spar]], "tensor")
                if fh == 0:
                    for sub in range(8):
                        yt = sub % 4
                        eng = "scalar" if sub % 2 == 0 else "vector"
                        if eng == "scalar":
                            EM.emit("scalar",
                                    lambda s, o=y0t[yt][:], i_=PB8[sub][:, 0:512]:
                                    s.activation(o, i_, AF_COPY),
                                    reads=[Pb8[sub]], writes=[y0t_b[yt]], inc=True, tag="D_dn")
                        else:
                            EM.emit("vector",
                                    lambda v, o=y0t[yt][:], i_=PB8[sub][:, 0:512]:
                                    v.tensor_copy(o, i_),
                                    reads=[Pb8[sub]], writes=[y0t_b[yt]], inc=True, tag="D_dn")
                        b = Buf(f"y0_{sub}_{cbk}")
                        y0_b[(sub, cbk)] = b
                        EM.emit("sync",
                                lambda sy, o=d_y0[sub * 128:(sub + 1) * 128, cbk * 512:(cbk + 1) * 512],
                                i_=y0t[yt][:]:
                                sy.dma_start(out=o, in_=i_),
                                reads=[y0t_b[yt]], writes=[b], inc=True, tag="D_dn")
                else:
                    for sub in range(8):
                        par2 = sub % 4
                        EM.emit("vector",
                                lambda v, o=ysb[:, sub * D + cbk * 512: sub * D + (cbk + 1) * 512],
                                a=y0r[par2][:], b=PB8[sub][:, 0:512]:
                                v.tensor_tensor(o, a, b, OP.add),
                                reads=[y0r_b[par2], Pb8[sub]], writes=[ysb_b[sub]], inc=True, tag="D_dn")
                        if sub < 4:
                            y0r_load(sub + 4, cbk)
            free("y67_%d_1" % fh, "y67_%d_0" % fh)
            if fh == 0:
                free("y0t3", "y0t2", "y0t1", "y0t0", "wd_sl0_1", "wd_sl0_0")

        if dbg:
            EM.emit("sync", lambda sy: sy.dma_start(out=d_dbg["d_ysb"], in_=ysb[:]),
                    reads=ysb_b, inc=True, tag="dbg")

        EM.fence()
        free("y0r3", "y0r2", "y0r1", "y0r0", "wd_sl1_1", "wd_sl1_0", "hsb")
        x2f = [alloc(f"x2f{i}", [128, D], BF16) for i in range(2)]
        x2f_b = bufs(2, "x2f")
        outf = [alloc(f"outf{i}", [128, D], F32) for i in range(2)]
        outf_b = bufs(2, "outf")
        out_done = bufs(8, "out")

        for sub in range(8):
            par = sub % 2
            EM.emit("sync",
                    lambda sy, o=x2f[par][:], i_=d_x2[sub * 128:(sub + 1) * 128, :]:
                    sy.dma_start(out=o, in_=i_),
                    reads=[x2_b[sub]], writes=[x2f_b[par]], inc=True, tag="E")
            st = stn[par]
            ysl = ysb[:, sub * D:(sub + 1) * D]
            ln_stats(st, ysl, ysb_b[sub], "E")
            ln_norm(st, outf[par][:], outf_b[par], ysl, ysb_b[sub], "E")
            EM.emit("vector", lambda v, o=outf[par][:], x_=x2f[par][:]:
                    v.tensor_tensor(o, o, x_, OP.add),
                    reads=[outf_b[par], x2f_b[par]], writes=[outf_b[par]], inc=True, tag="E")
            EM.emit("sync",
                    lambda sy, o=d_out[sub * 128:(sub + 1) * 128, :], i_=outf[par][:]:
                    sy.dma_start(out=o, in_=i_),
                    reads=[outf_b[par]], writes=[out_done[sub]], inc=True, tag="E")
        EM.emit("sync", None, reads=out_done, tag="E")

        blk = outer.enter_context(nc.Block())
        EM.replay(nc, blk, sems)
    return nc


def _prep(L0_attn_x, L0_attn_Wqkv, L0_attn_bqkv, L0_attn_Wo, L0_attn_bo,
          L0_ffn_W_gate, L0_ffn_W_up, L0_ffn_W_down):
    x = np.ascontiguousarray(np.asarray(L0_attn_x, dtype=np.float32).reshape(S, D))
    Wqkv = np.asarray(L0_attn_Wqkv, dtype=np.float32)
    wq_t = np.ascontiguousarray(
        Wqkv.reshape(32, 128, 24, 128).transpose(2, 1, 0, 3)).reshape(24 * 128, D)
    bqkv_t = np.ascontiguousarray(
        np.asarray(L0_attn_bqkv, dtype=np.float32).reshape(24, 128).T)
    wo = np.ascontiguousarray(np.asarray(L0_attn_Wo, dtype=np.float32))
    bo_b = np.ascontiguousarray(
        np.broadcast_to(np.asarray(L0_attn_bo, dtype=np.float32), (128, D)))
    wg_t = np.ascontiguousarray(
        np.asarray(L0_ffn_W_gate, dtype=np.float32)
        .reshape(NFC, 128, 32, 128).transpose(3, 0, 2, 1)).reshape(128, NFC * D)
    wu_t = np.ascontiguousarray(
        np.asarray(L0_ffn_W_up, dtype=np.float32)
        .reshape(NFC, 128, 32, 128).transpose(3, 0, 2, 1)).reshape(128, NFC * D)
    wd_t = np.ascontiguousarray(
        np.asarray(L0_ffn_W_down, dtype=np.float32)
        .reshape(8, 512, NFC, 128).transpose(0, 3, 2, 1)).reshape(8 * 128, NFC * 512)
    ident = np.eye(128, dtype=np.float32)
    common = {"wq": wq_t, "bqkv": bqkv_t, "wo": wo, "bo": bo_b,
              "wg": wg_t, "wu": wu_t, "wd": wd_t, "ident": ident}
    return x, common


def kernel(L0_attn_x, L0_attn_Wqkv, L0_attn_bqkv, L0_attn_Wo, L0_attn_bo,
           L0_ffn_W_gate, L0_ffn_W_up, L0_ffn_W_down, _dbg=False):
    x, common = _prep(L0_attn_x, L0_attn_Wqkv, L0_attn_bqkv, L0_attn_Wo,
                      L0_attn_bo, L0_ffn_W_gate, L0_ffn_W_up, L0_ffn_W_down)
    nc = build_program(dbg=_dbg)
    in_maps = [dict(common, x=np.ascontiguousarray(x[c * T:(c + 1) * T]))
               for c in range(8)]
    res = run_bass_kernel_spmd(nc, in_maps, core_ids=list(range(8)))
    out = np.concatenate([res.results[c]["out"] for c in range(8)], axis=0)
    out = out.reshape(1, S, D).astype(np.float32)
    if _dbg:
        return out, res
    return out


# version 13
# speedup vs baseline: 1.2747x; 1.0183x over previous
import numpy as np
from contextlib import ExitStack

import concourse.bass as bass
import concourse.mybir as mybir
from concourse.bass_utils import run_bass_kernel_spmd

F32 = mybir.dt.float32
BF16 = mybir.dt.bfloat16
AF = mybir.ActivationFunctionType
OP = mybir.AluOpType
AF_COPY = getattr(AF, "Copy", None)
if AF_COPY is None:
    AF_COPY = getattr(AF, "Identity")

ENGINES = ("tensor", "vector", "scalar", "sync", "gpsimd")
ENG_SEM = {"tensor": "mt", "vector": "mv", "scalar": "ms", "sync": "dsy", "gpsimd": "dgp"}
ALL_SEMS = ("mt", "mv", "ms", "dsy", "dgp", "cc")
COMPUTE = ("tensor", "vector", "scalar")


class Buf:
    __slots__ = ("name", "ready", "readers")

    def __init__(self, name=""):
        self.name = name
        self.ready = None
        self.readers = {}


def bufs(n, name):
    return [Buf(f"{name}{i}") for i in range(n)]


class Em:
    def __init__(self):
        self.ops = {e: [] for e in ENGINES}
        self.cnt = {s: 0 for s in ALL_SEMS}
        self.cnt_sbuf = {s: 0 for s in ALL_SEMS}
        self.waited = {e: {} for e in ENGINES}
        self.pfence = {e: None for e in ENGINES}
        self.last_writes = {e: frozenset() for e in ENGINES}

    def emit(self, eng, build, reads=(), writes=(), inc=False, record=True,
             dram_only=False, use_sem=None, tag=""):
        own = ENG_SEM[eng]
        inc_sem = use_sem or own
        if eng == "vector" and any(id(b) in self.last_writes[eng] for b in reads):
            self.ops[eng].append(((), lambda v: v.engine_nop(), None, "nopg"))
            self.last_writes[eng] = frozenset()
        waits = {}

        def need(sem, val, why):
            if sem == own and eng in COMPUTE:
                return
            if val is None:
                raise RuntimeError(f"un-inc'd dependency: {why} [{tag}] on {eng}")
            if self.waited[eng].get(sem, 0) >= val:
                return
            if waits.get(sem, 0) < val:
                waits[sem] = val

        if eng in ("sync", "gpsimd"):
            need(own, self.cnt[own], "dma-serialize")
        pf = self.pfence[eng]
        if pf:
            self.pfence[eng] = None
            for s, v in pf.items():
                need(s, v, "fence")
        for b in reads:
            r = b.ready
            if r is None:
                continue
            if r[0] == "eng":
                if r[1] != eng:
                    raise RuntimeError(f"cross-engine read of un-inc'd {b.name} [{tag}]")
            else:
                need(r[0], r[1], f"read {b.name}")
        for b in writes:
            for s, v in b.readers.items():
                if s == own and eng in COMPUTE:
                    continue
                need(s, v, f"WAR {b.name}")
            r = b.ready
            if r is not None:
                if r[0] == "eng":
                    if r[1] != eng:
                        raise RuntimeError(f"WAW cross-engine un-inc'd {b.name} [{tag}]")
                else:
                    need(r[0], r[1], f"WAW {b.name}")
        increc = None
        if inc:
            amt = (16 if eng in ("sync", "gpsimd") else 1) if inc is True else int(inc)
            self.cnt[inc_sem] += amt
            if not dram_only:
                self.cnt_sbuf[inc_sem] = self.cnt[inc_sem]
            increc = (inc_sem, amt)
        self.ops[eng].append((tuple(sorted(waits.items())), build, increc, tag))
        self.last_writes[eng] = frozenset(id(b) for b in writes)
        for s, v in waits.items():
            self.waited[eng][s] = v
        mark = (inc_sem, self.cnt[inc_sem]) if inc else ("eng", eng)
        for b in writes:
            b.ready = mark
            b.readers = {}
        if record:
            for b in reads:
                b.readers[own] = self.cnt[own] if (inc and inc_sem == own) else None

    def note_readers(self, rbufs, eng):
        own = ENG_SEM[eng]
        for b in rbufs:
            b.readers[own] = self.cnt[own]

    def fence(self):
        snap = {s: v for s, v in self.cnt_sbuf.items() if v > 0}
        for e in ENGINES:
            pf = self.pfence[e] or {}
            keys = set(pf) | set(snap)
            self.pfence[e] = {s: max(pf.get(s, 0), snap.get(s, 0)) for s in keys}

    def replay(self, nc, blk, sems):
        for eng in ENGINES:
            ops = self.ops[eng]

            def make(ops=ops):
                def body(e):
                    for waits, build, increc, tag in ops:
                        for s, v in waits:
                            e.wait_ge(sems[s], v)
                        ins = build(e) if build is not None else None
                        if increc is not None:
                            ins.then_inc(sems[increc[0]], increc[1])
                return body
            getattr(blk, eng)(make())


T = 1024
S = 8192
D = 4096
DH = 1024
F = 11008
NT = 8
NFC = 86
FHC = 43
C_SHIFT = -12.0
SCALE = 1.0 / 32.0


def build_program(dbg=False):
    nc = bass.Bass("TRN2", target_bir_lowering=False)

    d_x = nc.dram_tensor("x", [T, D], F32, kind="ExternalInput").ap()
    d_wq = nc.dram_tensor("wq", [24 * 128, D], F32, kind="ExternalInput").ap()
    d_bqkv = nc.dram_tensor("bqkv", [128, 24], F32, kind="ExternalInput").ap()
    d_wo = nc.dram_tensor("wo", [DH, D], F32, kind="ExternalInput").ap()
    d_bo = nc.dram_tensor("bo", [128, D], F32, kind="ExternalInput").ap()
    d_wg = nc.dram_tensor("wg", [128, NFC * D], F32, kind="ExternalInput").ap()
    d_wu = nc.dram_tensor("wu", [128, NFC * D], F32, kind="ExternalInput").ap()
    d_wd = nc.dram_tensor("wd", [8 * 128, NFC * 512], F32, kind="ExternalInput").ap()
    d_id = nc.dram_tensor("ident", [128, 128], F32, kind="ExternalInput").ap()
    d_out = nc.dram_tensor("out", [T, D], F32, kind="ExternalOutput").ap()

    xk = "ExternalOutput" if dbg else "Internal"
    d_x2 = nc.dram_tensor("x2d", [T, D], BF16, kind=xk).ap()
    d_y0 = nc.dram_tensor("y0d", [T, D], BF16, kind=xk).ap()
    d_cik = nc.dram_tensor("cc_in_k", [DH, T], BF16, kind="Internal").ap()
    d_civ = nc.dram_tensor("cc_in_v", [T, DH], BF16, kind="Internal").ap()
    d_cok = nc.dram_tensor("cc_out_k", [8 * DH, T], BF16, addr_space="Shared", kind="Internal").ap()
    d_cov = nc.dram_tensor("cc_out_v", [S, DH], BF16, addr_space="Shared", kind="Internal").ap()
    if dbg:
        d_dbg = {
            "d_qfm": nc.dram_tensor("d_qfm", [128, 8 * T], BF16, kind="ExternalOutput").ap(),
            "d_kfm": nc.dram_tensor("d_kfm", [128, 8 * T], BF16, kind="ExternalOutput").ap(),
            "d_vtm": nc.dram_tensor("d_vtm", [128, 8 * T], BF16, kind="ExternalOutput").ap(),
            "d_osb": nc.dram_tensor("d_osb", [128, 8 * T], BF16, kind="ExternalOutput").ap(),
            "d_den": nc.dram_tensor("d_den", [128, 8], F32, kind="ExternalOutput").ap(),
            "d_a0": nc.dram_tensor("d_a0", [128, D], F32, kind="ExternalOutput").ap(),
            "d_h0": nc.dram_tensor("d_h0", [128, FHC * T], BF16, kind="ExternalOutput").ap(),
            "d_ln3": nc.dram_tensor("d_ln3", [128, 32 * T], BF16, kind="ExternalOutput").ap(),
            "d_ysb": nc.dram_tensor("d_ysb", [128, 8 * D], BF16, kind="ExternalOutput").ap(),
            "d_xt0": nc.dram_tensor("d_xt0", [128, D], F32, kind="ExternalOutput").ap(),
            "d_ln1t0": nc.dram_tensor("d_ln1t0", [128, D], F32, kind="ExternalOutput").ap(),
            "d_ln1fm": nc.dram_tensor("d_ln1fm", [128, 32 * 512], BF16, kind="ExternalOutput").ap(),
            "d_wq0": nc.dram_tensor("d_wq0", [128, D], BF16, kind="ExternalOutput").ap(),
        }

    EM = Em()
    outer = ExitStack()
    live = {}

    def alloc(name, shape, dtype, kind="sbuf", side="left"):
        if kind == "sbuf":
            cm = nc.sbuf_tensor(name, shape, dtype, side=side)
        else:
            cm = nc.psum_tensor(name, shape, dtype)
        ap = cm.__enter__()
        live[name] = cm
        return ap

    def free(*names):
        for n in names:
            live.pop(n).__exit__(None, None, None)

    with outer:
        sems = {s: outer.enter_context(nc.semaphore(s)) for s in ALL_SEMS}

        P = [alloc(f"p{i}", [128, 512], F32, "psum") for i in range(6)]
        Pb = bufs(6, "psum")

        ident = alloc("ident_sb", [128, 128], F32)
        ident_b = Buf("ident")
        c_eps = alloc("c_eps", [128, 1], F32)
        c_s32 = alloc("c_s32", [128, 1], F32)
        c_nb12 = alloc("c_nb12", [128, 1], F32)
        c_i4096 = alloc("c_i4096", [128, 1], F32)
        ones_bf = alloc("ones_bf", [128, 1], BF16)
        consts_b = Buf("consts")
        bqkv_sb = alloc("bqkv_sb", [128, 24], F32)
        bqkv_b = Buf("bqkv")

        scr_bf = alloc("scr_bf", [128, D], BF16)
        scr_b = Buf("scr")
        stn = []
        for i in range(2):
            stn.append({
                "sums": alloc(f"sums_{i}", [128, 2], F32), "sumsb": Buf(f"sums_{i}"),
                "mean": alloc(f"mean_{i}", [128, 1], F32), "meanb": Buf(f"mean_{i}"),
                "msq": alloc(f"msq_{i}", [128, 1], F32), "msqb": Buf(f"msq_{i}"),
                "veps": alloc(f"veps_{i}", [128, 1], F32), "vepsb": Buf(f"veps_{i}"),
                "std": alloc(f"std_{i}", [128, 1], F32), "stdb": Buf(f"std_{i}"),
                "rcp": alloc(f"rcp_{i}", [128, 1], F32), "rcpb": Buf(f"rcp_{i}"),
            })

        def ln_stats(st, src_ap, src_buf, tag):
            EM.emit("vector", lambda v, o=st["sums"][:, 0:1], i_=src_ap:
                    v.tensor_reduce(o, i_, mybir.AxisListType.X, OP.add),
                    reads=[src_buf], writes=[st["sumsb"]], inc=True, tag=tag)
            EM.emit("scalar", lambda s, o=scr_bf[:], i_=src_ap, a=st["sums"][:, 1:2]:
                    s.activation(o, i_, AF.Square, accum_out=a),
                    reads=[src_buf], writes=[scr_b, st["sumsb"]], inc=True, tag=tag)
            EM.emit("scalar", lambda s, o=st["mean"][:], i_=st["sums"][:, 0:1], sc=c_i4096[:, 0:1]:
                    s.activation(o, i_, AF_COPY, scale=sc),
                    reads=[st["sumsb"], consts_b], writes=[st["meanb"]], inc=True, tag=tag)
            EM.emit("scalar", lambda s, o=st["msq"][:], i_=st["mean"][:]:
                    s.activation(o, i_, AF.Square),
                    reads=[st["meanb"]], writes=[st["msqb"]], inc=True, tag=tag)
            EM.emit("vector", lambda v, o=st["veps"][:], i_=st["sums"][:, 1:2], m2=st["msq"][:, 0:1]:
                    v.tensor_scalar(o, i_, 1.0 / D, m2, OP.mult, OP.subtract),
                    reads=[st["sumsb"], st["msqb"]], writes=[st["vepsb"]], inc=True, tag=tag)
            EM.emit("scalar", lambda s, o=st["std"][:], i_=st["veps"][:], b=c_eps[:, 0:1]:
                    s.activation(o, i_, AF.Sqrt, bias=b),
                    reads=[st["vepsb"], consts_b], writes=[st["stdb"]], inc=True, tag=tag)
            EM.emit("vector", lambda v, o=st["rcp"][:], i_=st["std"][:]: v.reciprocal(o, i_),
                    reads=[st["stdb"]], writes=[st["rcpb"]], inc=True, tag=tag)

        def ln_norm(st, out_ap, out_buf, src_ap, src_buf, tag):
            EM.emit("vector",
                    lambda v, o=out_ap, i_=src_ap, b=st["mean"][:, 0:1], sc=st["rcp"][:, 0:1]:
                    v.tensor_scalar(o, i_, b, sc, OP.subtract, OP.mult),
                    reads=[src_buf, st["meanb"], st["rcpb"]], writes=[out_buf], inc=True, tag=tag)

        EM.emit("vector", lambda v: v.memset(c_eps[:], 1e-5), writes=[consts_b], tag="init")
        EM.emit("vector", lambda v: v.memset(c_s32[:], SCALE), writes=[consts_b], tag="init")
        EM.emit("vector", lambda v: v.memset(c_nb12[:], C_SHIFT), writes=[consts_b], tag="init")
        EM.emit("vector", lambda v: v.memset(c_i4096[:], 1.0 / D), writes=[consts_b], tag="init")
        EM.emit("vector", lambda v: v.memset(ones_bf[:], 1.0), writes=[consts_b], inc=True, tag="init")
        EM.emit("sync", lambda sy: sy.dma_start(out=ident[:], in_=d_id),
                writes=[ident_b], inc=True, tag="init")
        identb = alloc("identb_sb", [128, 128], BF16)
        identb_b = Buf("identb")
        EM.emit("gpsimd", lambda g: g.dma_start(out=identb[:], in_=d_id),
                writes=[identb_b], inc=True, tag="init")
        EM.emit("sync", lambda sy: sy.dma_start(out=bqkv_sb[:], in_=d_bqkv),
                writes=[bqkv_b], inc=True, tag="init")

        ptr = [alloc(f"ptr{i}", [128, 1024], BF16, "psum") for i in range(2)]
        ptrb = bufs(2, "ptr")
        q_fm = alloc("q_fm", [128, 8 * T], BF16)
        q_fm_b = Buf("q_fm")
        k_fm = alloc("k_fm", [128, 8 * T], BF16)
        k_fm_b = Buf("k_fm")
        v_fm = alloc("v_fm", [128, 8 * T], BF16)
        v_fm_b = Buf("v_fm")
        v_tm = alloc("v_tm", [128, 8 * T], BF16)
        v_tm_b = Buf("v_tm")
        x_t = [alloc(f"x_t{i}", [128, D], F32) for i in range(2)]
        x_t_b = bufs(2, "x_t")
        ln1_t = [alloc(f"ln1_t{i}", [128, D], BF16) for i in range(2)]
        ln1_t_b = bufs(2, "ln1_t")
        ln1_fm = alloc("ln1_fm", [128, 32 * T], BF16)
        ln1_fm_b = Buf("ln1_fm")
        wq_sl = [alloc(f"wq_sl{i}", [128, D], BF16) for i in range(2)]
        wq_sl_b = bufs(2, "wq_sl")

        def transpose_block(trp, trb, src_ap, src_buf, n_dc, dst_fn, tag):
            for g in range(n_dc // 4):
                bank = g % 2
                for j in range(4):
                    dc = g * 4 + j
                    EM.emit("tensor",
                            lambda t, o=trp[bank][:, j * 128:(j + 1) * 128],
                            i_=src_ap[:, dc * 128:(dc + 1) * 128], idn=identb[:]:
                            t.transpose(o, i_, idn),
                            reads=[src_buf, identb_b], writes=[trb[bank]],
                            inc=(j == 3), record=(j == 3), tag=tag)
                for j in range(4):
                    dc = g * 4 + j
                    dst_ap, dst_buf = dst_fn(dc)
                    EM.emit("vector",
                            lambda v, o=dst_ap, i_=trp[bank][:, j * 128:(j + 1) * 128]:
                            v.tensor_copy(o, i_),
                            reads=[trb[bank]], writes=[dst_buf],
                            inc=(j == 3), record=(j == 3), tag=tag)

        for s_g in range(8):
            par = s_g % 2
            EM.emit("sync",
                    lambda sy, o=x_t[par][:], i_=d_x[s_g * 128:(s_g + 1) * 128, :]:
                    sy.dma_start(out=o, in_=i_),
                    writes=[x_t_b[par]], inc=True, tag="A_ln")
            st = stn[par]
            ln_stats(st, x_t[par][:], x_t_b[par], "A_ln")
            ln_norm(st, ln1_t[par][:], ln1_t_b[par], x_t[par][:], x_t_b[par], "A_ln")
            if dbg and s_g == 0:
                EM.emit("sync", lambda sy: sy.dma_start(out=d_dbg["d_xt0"], in_=x_t[0][:]),
                        reads=[x_t_b[0]], inc=True, tag="dbg")
                EM.emit("gpsimd", lambda g: g.dma_start(out=d_dbg["d_ln1t0"], in_=ln1_t[0][:]),
                        reads=[ln1_t_b[0]], inc=True, tag="dbg")
            transpose_block(
                ptr, ptrb, ln1_t[par][:], ln1_t_b[par], 32,
                lambda dc, s_g=s_g: (ln1_fm[:, dc * T + s_g * 128: dc * T + (s_g + 1) * 128], ln1_fm_b),
                "A_tr")
        if dbg:
            EM.emit("sync", lambda sy: sy.dma_start(out=d_dbg["d_ln1fm"], in_=ln1_fm[:, 0:32 * 512]),
                    reads=[ln1_fm_b], inc=True, tag="dbg")
        cik_b = bufs(8, "cik")
        civ_b = bufs(8, "civ")
        cok_b = Buf("cok")
        cov_b = Buf("cov")
        for fc in list(range(8, 24)) + list(range(0, 8)):
            par = fc % 2
            EM.emit("gpsimd",
                    lambda g, o=wq_sl[par][:], i_=d_wq[fc * 128:(fc + 1) * 128, :]:
                    g.dma_start(out=o, in_=i_),
                    writes=[wq_sl_b[par]], inc=True, tag="A_qkv")
            if dbg and fc == 0:
                EM.emit("sync", lambda sy: sy.dma_start(out=d_dbg["d_wq0"], in_=wq_sl[0][:]),
                        reads=[wq_sl_b[0]], inc=True, tag="dbg")
            if fc < 8:
                dst_ap, dst_buf = q_fm, q_fm_b
                col0 = fc * T
            elif fc < 16:
                dst_ap, dst_buf = k_fm, k_fm_b
                col0 = (fc - 8) * T
            else:
                dst_ap, dst_buf = v_fm, v_fm_b
                col0 = (fc - 16) * T
            for h in range(2):
                bank = (fc * 2 + h) % 2
                for dc in range(32):
                    EM.emit("tensor",
                            lambda t, o=P[bank][:, 0:512],
                            l=wq_sl[par][:, dc * 128:(dc + 1) * 128],
                            r=ln1_fm[:, dc * T + h * 512: dc * T + (h + 1) * 512],
                            st_=(dc == 0), sp=(dc == 31):
                            t.matmul(o, l, r, start=st_, stop=sp, skip_group_check=True),
                            reads=[wq_sl_b[par], ln1_fm_b], writes=[Pb[bank]],
                            inc=(dc == 31), record=(dc == 31), tag="A_qkv")
                EM.emit("vector",
                        lambda v, o=dst_ap[:, col0 + h * 512: col0 + (h + 1) * 512],
                        i_=P[bank][:, 0:512], b=bqkv_sb[:, fc:fc + 1]:
                        v.tensor_scalar_add(o, i_, b),
                        reads=[Pb[bank], bqkv_b], writes=[dst_buf], inc=True, tag="A_qkv")
            if fc == 15:
                for i in range(8):
                    EM.emit("sync",
                            lambda sy, o=d_cik[i * 128:(i + 1) * 128, :], i_=k_fm[:, i * T:(i + 1) * T]:
                            sy.dma_start(out=o, in_=i_),
                            reads=[k_fm_b], writes=[cik_b[i]], inc=True, tag="A_cc")
                EM.emit("gpsimd",
                        lambda g: g.collective_compute(
                            "AllGather", OP.bypass,
                            replica_groups=[[0, 1, 2, 3, 4, 5, 6, 7]],
                            ins=[d_cik], outs=[d_cok]),
                        reads=cik_b, writes=[cok_b], inc=1, dram_only=True,
                        use_sem="cc", tag="A_cc")
            if fc == 23:
                for kt in range(8):
                    for g2 in range(2):
                        bank = (kt * 2 + g2) % 2
                        for j in range(4):
                            dhc = g2 * 4 + j
                            EM.emit("tensor",
                                    lambda t, o=ptr[bank][:, j * 128:(j + 1) * 128],
                                    i_=v_fm[:, dhc * T + kt * 128: dhc * T + (kt + 1) * 128], idn=identb[:]:
                                    t.transpose(o, i_, idn),
                                    reads=[v_fm_b, identb_b], writes=[ptrb[bank]],
                                    inc=(j == 3), record=(j == 3), tag="A_vtr")
                        EM.emit("vector",
                                lambda v, o=v_tm[:, kt * T + g2 * 512: kt * T + (g2 + 1) * 512],
                                i_=ptr[bank][:, 0:512]:
                                v.tensor_copy(o, i_),
                                reads=[ptrb[bank]], writes=[v_tm_b], inc=True, tag="A_vtr")
                for i in range(8):
                    EM.emit("sync",
                            lambda sy, o=d_civ[i * 128:(i + 1) * 128, :], i_=v_tm[:, i * T:(i + 1) * T]:
                            sy.dma_start(out=o, in_=i_),
                            reads=[v_tm_b], writes=[civ_b[i]], inc=True, tag="A_cc")
                EM.emit("gpsimd",
                        lambda g: g.collective_compute(
                            "AllGather", OP.bypass,
                            replica_groups=[[0, 1, 2, 3, 4, 5, 6, 7]],
                            ins=[d_civ], outs=[d_cov]),
                        reads=civ_b, writes=[cov_b], inc=1, dram_only=True,
                        use_sem="cc", tag="A_cc")

        if dbg:
            EM.emit("sync", lambda sy: sy.dma_start(out=d_dbg["d_qfm"], in_=q_fm[:]),
                    reads=[q_fm_b], inc=True, tag="dbg")
            EM.emit("sync", lambda sy: sy.dma_start(out=d_dbg["d_kfm"], in_=k_fm[:]),
                    reads=[k_fm_b], inc=True, tag="dbg")
            EM.emit("sync", lambda sy: sy.dma_start(out=d_dbg["d_vtm"], in_=v_tm[:]),
                    reads=[v_tm_b], inc=True, tag="dbg")

        EM.fence()
        free("wq_sl1", "wq_sl0", "ln1_fm", "ln1_t1", "ln1_t0", "x_t1", "x_t0",
             "v_tm", "v_fm", "k_fm")
        free("ptr1", "ptr0")
        pden = alloc("pden", [128, 512], F32, "psum")
        pden_b = Buf("pden")

        wo_bf = alloc("wo_bf", [128, 8 * D], BF16, side="right")
        wo_b = bufs(8, "wo")
        o_sb = alloc("o_sb", [128, 8 * T], BF16, side="right")
        o_sb_b = Buf("o_sb")
        bo_bc = alloc("bo_bc", [128, D], BF16, side="right")
        bo_b = Buf("bo")
        rden = alloc("rden", [128, 8], F32, side="right")
        rden_b = Buf("rden")
        k_sl = [alloc(f"k_sl{i}", [128, 8 * T], BF16) for i in range(2)]
        k_sl_b = [bufs(8, f"k_sl{i}_") for i in range(2)]
        v_sl = [alloc(f"v_sl{i}", [128, 8 * T], BF16) for i in range(2)]
        v_sl_b = [bufs(8, f"v_sl{i}_") for i in range(2)]
        p_t8 = alloc("p_t8", [128, 8 * 512], BF16)
        p_t8_b = bufs(8, "p_t8")

        for i in range(8):
            EM.emit("gpsimd",
                    lambda g, o=wo_bf[:, i * D:(i + 1) * D], i_=d_wo[i * 128:(i + 1) * 128, :]:
                    g.dma_start(out=o, in_=i_),
                    writes=[wo_b[i]], inc=True, tag="B")
        EM.emit("gpsimd", lambda g: g.dma_start(out=bo_bc[:], in_=d_bo),
                writes=[bo_b], inc=True, tag="B")

        for cb in range(8):
            par = cb % 2
            for i in range(8):
                EM.emit("sync",
                        lambda sy, o=k_sl[par][:, i * T:(i + 1) * T],
                        i_=d_cok[cb * DH + i * 128: cb * DH + (i + 1) * 128, :]:
                        sy.dma_start(out=o, in_=i_),
                        reads=[cok_b], writes=[k_sl_b[par][i]], inc=True, tag="B")
            for i in range(8):
                EM.emit("sync",
                        lambda sy, o=v_sl[par][:, i * T:(i + 1) * T],
                        i_=d_cov[cb * T + i * 128: cb * T + (i + 1) * 128, :]:
                        sy.dma_start(out=o, in_=i_),
                        reads=[cov_b], writes=[v_sl_b[par][i]], inc=True, tag="B")

            for qp in range(2):
                for kt in range(8):
                    bank = kt % 2
                    for dhc in range(8):
                        EM.emit("tensor",
                                lambda t, o=P[bank][:, 0:512],
                                l=k_sl[par][:, dhc * T + kt * 128: dhc * T + (kt + 1) * 128],
                                r=q_fm[:, dhc * T + qp * 512: dhc * T + (qp + 1) * 512],
                                st_=(dhc == 0), sp=(dhc == 7):
                                t.matmul(o, l, r, start=st_, stop=sp, skip_group_check=True),
                                reads=[k_sl_b[par][dhc], q_fm_b], writes=[Pb[bank]],
                                inc=(dhc == 7), record=False, tag="B_s")
                    EM.note_readers([k_sl_b[par][d_] for d_ in range(8)] + [q_fm_b], "tensor")
                    EM.emit("scalar",
                            lambda s, o=p_t8[:, kt * 512:(kt + 1) * 512], i_=P[bank][:, 0:512],
                            b=c_nb12[:, 0:1], sc=c_s32[:, 0:1]:
                            s.activation(o, i_, AF.Exp, bias=b, scale=sc),
                            reads=[Pb[bank], consts_b], writes=[p_t8_b[kt]], inc=True, tag="B_s")
                    for sub in range(4):
                        EM.emit("tensor",
                                lambda t, o=pden[:, qp * 4 + sub: qp * 4 + sub + 1],
                                l=p_t8[:, kt * 512 + sub * 128: kt * 512 + (sub + 1) * 128],
                                r=ones_bf[:],
                                st_=(cb == 0 and qp == 0 and kt == 0 and sub == 0),
                                sp=(cb == 7 and qp == 1 and kt == 7 and sub == 3):
                                t.matmul(o, l, r, start=st_, stop=sp, skip_group_check=True),
                                reads=[p_t8_b[kt], consts_b], writes=[pden_b],
                                inc=(cb == 7 and qp == 1 and kt == 7 and sub == 3),
                                record=False, tag="B_d")
                for qh in range(2):
                    qg = qp * 2 + qh
                    for kt in range(8):
                        for dhc in range(8):
                            ob = 2 + dhc // 2
                            EM.emit("tensor",
                                    lambda t, o=P[ob][:, (dhc % 2) * 256:(dhc % 2) * 256 + 256],
                                    l=v_sl[par][:, kt * T + dhc * 128: kt * T + (dhc + 1) * 128],
                                    r=p_t8[:, kt * 512 + qh * 256: kt * 512 + qh * 256 + 256],
                                    st_=(kt == 0 and dhc % 2 == 0),
                                    sp=(kt == 7 and dhc % 2 == 1):
                                    t.matmul(o, l, r, start=st_, stop=sp, skip_group_check=True),
                                    reads=[v_sl_b[par][kt], p_t8_b[kt]], writes=[Pb[ob]],
                                    inc=(dhc == 7 or (kt == 7 and dhc % 2 == 1)),
                                    record=False, tag="B_pv")
                    EM.note_readers([v_sl_b[par][k_] for k_ in range(8)]
                                    + [p_t8_b[k_] for k_ in range(8)] + [consts_b], "tensor")
                    for dhc in range(8):
                        ob = 2 + dhc // 2
                        osl = o_sb[:, dhc * T + qg * 256: dhc * T + (qg + 1) * 256]
                        psl = P[ob][:, (dhc % 2) * 256:(dhc % 2) * 256 + 256]
                        if cb == 0:
                            EM.emit("vector", lambda v, o=osl, i_=psl: v.tensor_copy(o, i_),
                                    reads=[Pb[ob]], writes=[o_sb_b], inc=True, tag="B_ev")
                        else:
                            EM.emit("vector", lambda v, o=osl, i_=psl: v.tensor_tensor(o, o, i_, OP.add),
                                    reads=[Pb[ob], o_sb_b], writes=[o_sb_b], inc=True, tag="B_ev")

        EM.emit("vector", lambda v: v.reciprocal(rden[:], pden[:, 0:8]),
                reads=[pden_b], writes=[rden_b], inc=True, tag="B")

        if dbg:
            EM.emit("sync", lambda sy: sy.dma_start(out=d_dbg["d_osb"], in_=o_sb[:]),
                    reads=[o_sb_b], inc=True, tag="dbg")
            EM.emit("sync", lambda sy: sy.dma_start(out=d_dbg["d_den"], in_=rden[:]),
                    reads=[rden_b], inc=True, tag="dbg")

        EM.fence()
        free("p_t8", "v_sl1", "v_sl0", "k_sl1", "k_sl0", "q_fm")
        free("pden")

        a_sb = [alloc(f"a_sb{i}", [128, D], F32) for i in range(3)]
        a_sb_b = bufs(3, "a_sb")
        lnb = [alloc("lnb0", [128, D], F32)]
        lnb_b = bufs(1, "lnb")
        xr_t = [alloc(f"xr_t{i}", [128, D], F32) for i in range(2)]
        xr_t_b = bufs(2, "xr")
        x2_b = bufs(8, "x2")

        for sg in range(8):
            par = sg % 3
            spar = sg % 2
            EM.emit("sync",
                    lambda sy, o=xr_t[spar][:], i_=d_x[sg * 128:(sg + 1) * 128, :]:
                    sy.dma_start(out=o, in_=i_),
                    writes=[xr_t_b[spar]], inc=True, tag="C")
            for cc in range(8):
                bank = cc % 2
                for dhc in range(8):
                    EM.emit("tensor",
                            lambda t, o=P[bank][:, 0:512],
                            l=o_sb[:, dhc * T + sg * 128: dhc * T + (sg + 1) * 128],
                            r=wo_bf[:, dhc * D + cc * 512: dhc * D + (cc + 1) * 512],
                            st_=(dhc == 0), sp=(dhc == 7):
                            t.matmul(o, l, r, start=st_, stop=sp, skip_group_check=True),
                            reads=[o_sb_b, wo_b[dhc]], writes=[Pb[bank]],
                            inc=(dhc == 7), record=False, tag="C_mm")
                EM.note_readers([o_sb_b] + wo_b, "tensor")
                EM.emit("scalar",
                        lambda s, o=a_sb[par][:, cc * 512:(cc + 1) * 512],
                        i_=P[bank][:, 0:512], sc=rden[:, sg:sg + 1]:
                        s.activation(o, i_, AF_COPY, scale=sc),
                        reads=[Pb[bank], rden_b], writes=[a_sb_b[par]], inc=True, tag="C_mm")
            EM.emit("vector", lambda v, o=a_sb[par][:], b=bo_bc[:]:
                    v.tensor_tensor(o, o, b, OP.add),
                    reads=[a_sb_b[par], bo_b], writes=[a_sb_b[par]], inc=True, tag="C")
            if dbg and sg == 0:
                EM.emit("sync", lambda sy: sy.dma_start(out=d_dbg["d_a0"], in_=a_sb[0][:]),
                        reads=[a_sb_b[0]], inc=True, tag="dbg")
            st = stn[sg % 2]
            ln_stats(st, a_sb[par][:], a_sb_b[par], "C")
            ln_norm(st, lnb[0][:], lnb_b[0], a_sb[par][:], a_sb_b[par], "C")
            EM.emit("vector", lambda v, o=lnb[0][:], x_=xr_t[spar][:]:
                    v.tensor_tensor(o, o, x_, OP.add),
                    reads=[lnb_b[0], xr_t_b[spar]], writes=[lnb_b[0]], inc=True, tag="C")
            EM.emit("gpsimd",
                    lambda g, o=d_x2[sg * 128:(sg + 1) * 128, :], i_=lnb[0][:]:
                    g.dma_start(out=o, in_=i_),
                    reads=[lnb_b[0]], writes=[x2_b[sg]], inc=True, tag="C")

        EM.fence()
        free("xr_t1", "xr_t0", "lnb0", "a_sb2", "a_sb1", "a_sb0",
             "rden", "bo_bc", "o_sb", "wo_bf")

        ln3_fm = alloc("ln3_fm", [128, 32 * T], BF16, side="right")
        ln3_fm_b = Buf("ln3_fm")
        hsb = alloc("hsb", [128, FHC * T], BF16)
        hsb_b = Buf("hsb")
        x2t = [alloc(f"x2t{i}", [128, D], BF16) for i in range(2)]
        x2t_b = bufs(2, "x2t")
        ln3t = [alloc(f"ln3t{i}", [128, D], BF16) for i in range(2)]
        ln3t_b = bufs(2, "ln3t")
        ptr2 = [alloc(f"ptr2_{i}", [128, 1024], BF16, "psum") for i in range(2)]
        ptr2b = bufs(2, "ptr2")

        for sg in range(8):
            par = sg % 2
            EM.emit("sync",
                    lambda sy, o=x2t[par][:], i_=d_x2[sg * 128:(sg + 1) * 128, :]:
                    sy.dma_start(out=o, in_=i_),
                    reads=[x2_b[sg]], writes=[x2t_b[par]], inc=True, tag="Dp")
            st = stn[par]
            ln_stats(st, x2t[par][:], x2t_b[par], "Dp")
            ln_norm(st, ln3t[par][:], ln3t_b[par], x2t[par][:], x2t_b[par], "Dp")
            transpose_block(
                ptr2, ptr2b, ln3t[par][:], ln3t_b[par], 32,
                lambda dc, sg=sg: (ln3_fm[:, dc * T + sg * 128: dc * T + (sg + 1) * 128], ln3_fm_b),
                "Dp_tr")
        if dbg:
            EM.emit("sync", lambda sy: sy.dma_start(out=d_dbg["d_ln3"], in_=ln3_fm[:]),
                    reads=[ln3_fm_b], inc=True, tag="dbg")

        EM.fence()
        free("ptr2_1", "ptr2_0")
        free("ln3t1", "ln3t0", "x2t1", "x2t0")
        y0_b = {}
        ysb_b = bufs(8, "ysb")
        WSEG = 22 * 512

        for fh in range(2):
            if fh == 1:
                EM.fence()
            wg_sl = [alloc(f"wg_sl{fh}_{i}", [128, D], BF16) for i in range(2)]
            wg_sl_b = bufs(2, "wg_sl")
            wu_sl = [alloc(f"wu_sl{fh}_{i}", [128, D], BF16) for i in range(2)]
            wu_sl_b = bufs(2, "wu_sl")
            sg_t = [alloc(f"sg_t{fh}_{i}", [128, 512], F32) for i in range(2)]
            sg_t_b = bufs(2, "sg_t")
            for fc in range(FHC):
                fg = fh * FHC + fc
                par = fc % 2
                EM.emit("gpsimd",
                        lambda g, o=wg_sl[par][:], i_=d_wg[:, fg * D:(fg + 1) * D]:
                        g.dma_start(out=o, in_=i_),
                        writes=[wg_sl_b[par]], inc=True, tag="D_gu")
                EM.emit("gpsimd",
                        lambda g, o=wu_sl[par][:], i_=d_wu[:, fg * D:(fg + 1) * D]:
                        g.dma_start(out=o, in_=i_),
                        writes=[wu_sl_b[par]], inc=True, tag="D_gu")
                for grp in range(2):
                    gb, ub = grp, 2 + grp
                    for dc in range(32):
                        EM.emit("tensor",
                                lambda t, o=P[gb][:, 0:512],
                                l=wg_sl[par][:, dc * 128:(dc + 1) * 128],
                                r=ln3_fm[:, dc * T + grp * 512: dc * T + grp * 512 + 512],
                                st_=(dc == 0), sp=(dc == 31):
                                t.matmul(o, l, r, start=st_, stop=sp, skip_group_check=True),
                                reads=[wg_sl_b[par], ln3_fm_b], writes=[Pb[gb]],
                                inc=(dc == 31), record=False, tag="D_gu")
                    for dc in range(32):
                        EM.emit("tensor",
                                lambda t, o=P[ub][:, 0:512],
                                l=wu_sl[par][:, dc * 128:(dc + 1) * 128],
                                r=ln3_fm[:, dc * T + grp * 512: dc * T + grp * 512 + 512],
                                st_=(dc == 0), sp=(dc == 31):
                                t.matmul(o, l, r, start=st_, stop=sp, skip_group_check=True),
                                reads=[wu_sl_b[par], ln3_fm_b], writes=[Pb[ub]],
                                inc=(dc == 31), record=False, tag="D_gu")
                    EM.note_readers([wg_sl_b[par], wu_sl_b[par], ln3_fm_b], "tensor")
                    EM.emit("scalar",
                            lambda s, o=sg_t[grp][:], i_=P[gb][:, 0:512]:
                            s.activation(o, i_, AF.Silu),
                            reads=[Pb[gb]], writes=[sg_t_b[grp]], inc=True, tag="D_gu")
                    EM.emit("vector",
                            lambda v, o=hsb[:, fc * T + grp * 512: fc * T + grp * 512 + 512],
                            a=sg_t[grp][:], b=P[ub][:, 0:512]:
                            v.tensor_tensor(o, a, b, OP.mult),
                            reads=[sg_t_b[grp], Pb[ub]], writes=[hsb_b], inc=True, tag="D_gu")
            if dbg and fh == 0:
                EM.emit("sync", lambda sy: sy.dma_start(out=d_dbg["d_h0"], in_=hsb[:]),
                        reads=[hsb_b], inc=True, tag="dbg")
            EM.fence()
            free(f"sg_t{fh}_1", f"sg_t{fh}_0", f"wu_sl{fh}_1", f"wu_sl{fh}_0",
                 f"wg_sl{fh}_1", f"wg_sl{fh}_0")
            if fh == 0:
                wd_sl = [alloc(f"wd_sl0_{i}", [128, WSEG], BF16) for i in range(2)]
                wd_sl_b = bufs(2, "wd_sl")
                y0t = [alloc(f"y0t{i}", [128, 512], BF16) for i in range(4)]
                y0t_b = bufs(4, "y0t")
            else:
                free("ln3_fm")
                ysb = alloc("ysb", [128, 8 * D], BF16, side="right")
                wd_sl = [alloc(f"wd_sl1_{i}", [128, WSEG], BF16) for i in range(2)]
                wd_sl_b = bufs(2, "wd_sl")
                y0r = [alloc(f"y0r{i}", [128, 512], BF16) for i in range(4)]
                y0r_b = bufs(4, "y0r")
            y67 = [alloc(f"y67_{fh}_{i}", [128, 512], F32, "psum") for i in range(2)]
            PB8 = P + y67
            Pb8 = Pb + bufs(2, f"y67_{fh}_")

            def y0r_load(sub, cbk):
                par2 = sub % 4
                EM.emit("sync",
                        lambda sy, o=y0r[par2][:],
                        i_=d_y0[sub * 128:(sub + 1) * 128, cbk * 512:(cbk + 1) * 512]:
                        sy.dma_start(out=o, in_=i_),
                        reads=[y0_b[(sub, cbk)]], writes=[y0r_b[par2]], inc=True, tag="D_dn")

            for cbk in range(8):
                if fh == 1:
                    for sub in range(4):
                        y0r_load(sub, cbk)
                for h2 in range(2):
                    seg = 22 if h2 == 0 else 21
                    spar = (cbk * 2 + h2) % 2
                    col0 = (fh * FHC + h2 * 22) * 512
                    EM.emit("gpsimd",
                            lambda g, o=wd_sl[spar][:, 0:seg * 512],
                            i_=d_wd[cbk * 128:(cbk + 1) * 128, col0:col0 + seg * 512]:
                            g.dma_start(out=o, in_=i_),
                            writes=[wd_sl_b[spar]], inc=True, tag="D_dn")
                    for fcl in range(seg):
                        fc = h2 * 22 + fcl
                        for sub in range(8):
                            EM.emit("tensor",
                                    lambda t, o=PB8[sub][:, 0:512],
                                    l=hsb[:, fc * T + sub * 128: fc * T + (sub + 1) * 128],
                                    r=wd_sl[spar][:, fcl * 512:(fcl + 1) * 512],
                                    st_=(fc == 0), sp=(fc == FHC - 1):
                                    t.matmul(o, l, r, start=st_, stop=sp, skip_group_check=True),
                                    reads=[hsb_b, wd_sl_b[spar]], writes=[Pb8[sub]],
                                    inc=(sub == 7 or fc == FHC - 1), record=False, tag="D_dn")
                    EM.note_readers([hsb_b, wd_sl_b[spar]], "tensor")
                if fh == 0:
                    for sub in range(8):
                        yt = sub % 4
                        eng = "scalar" if sub % 2 == 0 else "vector"
                        if eng == "scalar":
                            EM.emit("scalar",
                                    lambda s, o=y0t[yt][:], i_=PB8[sub][:, 0:512]:
                                    s.activation(o, i_, AF_COPY),
                                    reads=[Pb8[sub]], writes=[y0t_b[yt]], inc=True, tag="D_dn")
                        else:
                            EM.emit("vector",
                                    lambda v, o=y0t[yt][:], i_=PB8[sub][:, 0:512]:
                                    v.tensor_copy(o, i_),
                                    reads=[Pb8[sub]], writes=[y0t_b[yt]], inc=True, tag="D_dn")
                        b = Buf(f"y0_{sub}_{cbk}")
                        y0_b[(sub, cbk)] = b
                        EM.emit("sync",
                                lambda sy, o=d_y0[sub * 128:(sub + 1) * 128, cbk * 512:(cbk + 1) * 512],
                                i_=y0t[yt][:]:
                                sy.dma_start(out=o, in_=i_),
                                reads=[y0t_b[yt]], writes=[b], inc=True, tag="D_dn")
                else:
                    for sub in range(8):
                        par2 = sub % 4
                        EM.emit("vector",
                                lambda v, o=ysb[:, sub * D + cbk * 512: sub * D + (cbk + 1) * 512],
                                a=y0r[par2][:], b=PB8[sub][:, 0:512]:
                                v.tensor_tensor(o, a, b, OP.add),
                                reads=[y0r_b[par2], Pb8[sub]], writes=[ysb_b[sub]], inc=True, tag="D_dn")
                        if sub < 4:
                            y0r_load(sub + 4, cbk)
            free("y67_%d_1" % fh, "y67_%d_0" % fh)
            if fh == 0:
                free("y0t3", "y0t2", "y0t1", "y0t0", "wd_sl0_1", "wd_sl0_0")

        if dbg:
            EM.emit("sync", lambda sy: sy.dma_start(out=d_dbg["d_ysb"], in_=ysb[:]),
                    reads=ysb_b, inc=True, tag="dbg")

        EM.fence()
        free("y0r3", "y0r2", "y0r1", "y0r0", "wd_sl1_1", "wd_sl1_0", "hsb")
        x2f = [alloc(f"x2f{i}", [128, D], BF16) for i in range(2)]
        x2f_b = bufs(2, "x2f")
        outf = [alloc(f"outf{i}", [128, D], F32) for i in range(2)]
        outf_b = bufs(2, "outf")
        out_done = bufs(8, "out")

        for sub in range(8):
            par = sub % 2
            EM.emit("sync",
                    lambda sy, o=x2f[par][:], i_=d_x2[sub * 128:(sub + 1) * 128, :]:
                    sy.dma_start(out=o, in_=i_),
                    reads=[x2_b[sub]], writes=[x2f_b[par]], inc=True, tag="E")
            st = stn[par]
            ysl = ysb[:, sub * D:(sub + 1) * D]
            ln_stats(st, ysl, ysb_b[sub], "E")
            ln_norm(st, outf[par][:], outf_b[par], ysl, ysb_b[sub], "E")
            EM.emit("vector", lambda v, o=outf[par][:], x_=x2f[par][:]:
                    v.tensor_tensor(o, o, x_, OP.add),
                    reads=[outf_b[par], x2f_b[par]], writes=[outf_b[par]], inc=True, tag="E")
            EM.emit("sync",
                    lambda sy, o=d_out[sub * 128:(sub + 1) * 128, :], i_=outf[par][:]:
                    sy.dma_start(out=o, in_=i_),
                    reads=[outf_b[par]], writes=[out_done[sub]], inc=True, tag="E")
        EM.emit("sync", None, reads=out_done, tag="E")

        blk = outer.enter_context(nc.Block())
        EM.replay(nc, blk, sems)
    return nc


def _prep(L0_attn_x, L0_attn_Wqkv, L0_attn_bqkv, L0_attn_Wo, L0_attn_bo,
          L0_ffn_W_gate, L0_ffn_W_up, L0_ffn_W_down):
    x = np.ascontiguousarray(np.asarray(L0_attn_x, dtype=np.float32).reshape(S, D))
    Wqkv = np.asarray(L0_attn_Wqkv, dtype=np.float32)
    wq_t = np.ascontiguousarray(
        Wqkv.reshape(32, 128, 24, 128).transpose(2, 1, 0, 3)).reshape(24 * 128, D)
    bqkv_t = np.ascontiguousarray(
        np.asarray(L0_attn_bqkv, dtype=np.float32).reshape(24, 128).T)
    wo = np.ascontiguousarray(np.asarray(L0_attn_Wo, dtype=np.float32))
    bo_b = np.ascontiguousarray(
        np.broadcast_to(np.asarray(L0_attn_bo, dtype=np.float32), (128, D)))
    wg_t = np.ascontiguousarray(
        np.asarray(L0_ffn_W_gate, dtype=np.float32)
        .reshape(NFC, 128, 32, 128).transpose(3, 0, 2, 1)).reshape(128, NFC * D)
    wu_t = np.ascontiguousarray(
        np.asarray(L0_ffn_W_up, dtype=np.float32)
        .reshape(NFC, 128, 32, 128).transpose(3, 0, 2, 1)).reshape(128, NFC * D)
    wd_t = np.ascontiguousarray(
        np.asarray(L0_ffn_W_down, dtype=np.float32)
        .reshape(8, 512, NFC, 128).transpose(0, 3, 2, 1)).reshape(8 * 128, NFC * 512)
    ident = np.eye(128, dtype=np.float32)
    common = {"wq": wq_t, "bqkv": bqkv_t, "wo": wo, "bo": bo_b,
              "wg": wg_t, "wu": wu_t, "wd": wd_t, "ident": ident}
    return x, common


def kernel(L0_attn_x, L0_attn_Wqkv, L0_attn_bqkv, L0_attn_Wo, L0_attn_bo,
           L0_ffn_W_gate, L0_ffn_W_up, L0_ffn_W_down, _dbg=False):
    x, common = _prep(L0_attn_x, L0_attn_Wqkv, L0_attn_bqkv, L0_attn_Wo,
                      L0_attn_bo, L0_ffn_W_gate, L0_ffn_W_up, L0_ffn_W_down)
    nc = build_program(dbg=_dbg)
    in_maps = [dict(common, x=np.ascontiguousarray(x[c * T:(c + 1) * T]))
               for c in range(8)]
    res = run_bass_kernel_spmd(nc, in_maps, core_ids=list(range(8)))
    out = np.concatenate([res.results[c]["out"] for c in range(8)], axis=0)
    out = out.reshape(1, S, D).astype(np.float32)
    if _dbg:
        return out, res
    return out


# version 14
# speedup vs baseline: 1.2751x; 1.0003x over previous
import numpy as np
from contextlib import ExitStack

import concourse.bass as bass
import concourse.mybir as mybir
from concourse.bass_utils import run_bass_kernel_spmd

F32 = mybir.dt.float32
BF16 = mybir.dt.bfloat16
AF = mybir.ActivationFunctionType
OP = mybir.AluOpType
AF_COPY = getattr(AF, "Copy", None)
if AF_COPY is None:
    AF_COPY = getattr(AF, "Identity")

ENGINES = ("tensor", "vector", "scalar", "sync", "gpsimd")
ENG_SEM = {"tensor": "mt", "vector": "mv", "scalar": "ms", "sync": "dsy", "gpsimd": "dgp"}
ALL_SEMS = ("mt", "mv", "ms", "dsy", "dgp", "cc")
COMPUTE = ("tensor", "vector", "scalar")


class Buf:
    __slots__ = ("name", "ready", "readers")

    def __init__(self, name=""):
        self.name = name
        self.ready = None
        self.readers = {}


def bufs(n, name):
    return [Buf(f"{name}{i}") for i in range(n)]


class Em:
    def __init__(self):
        self.ops = {e: [] for e in ENGINES}
        self.cnt = {s: 0 for s in ALL_SEMS}
        self.cnt_sbuf = {s: 0 for s in ALL_SEMS}
        self.waited = {e: {} for e in ENGINES}
        self.pfence = {e: None for e in ENGINES}
        self.last_writes = {e: frozenset() for e in ENGINES}

    def emit(self, eng, build, reads=(), writes=(), inc=False, record=True,
             dram_only=False, use_sem=None, tag=""):
        own = ENG_SEM[eng]
        inc_sem = use_sem or own
        if eng == "vector" and any(id(b) in self.last_writes[eng] for b in reads):
            self.ops[eng].append(((), lambda v: v.engine_nop(), None, "nopg"))
            self.last_writes[eng] = frozenset()
        waits = {}

        def need(sem, val, why):
            if sem == own and eng in COMPUTE:
                return
            if val is None:
                raise RuntimeError(f"un-inc'd dependency: {why} [{tag}] on {eng}")
            if self.waited[eng].get(sem, 0) >= val:
                return
            if waits.get(sem, 0) < val:
                waits[sem] = val

        if eng in ("sync", "gpsimd"):
            need(own, self.cnt[own], "dma-serialize")
        pf = self.pfence[eng]
        if pf:
            self.pfence[eng] = None
            for s, v in pf.items():
                need(s, v, "fence")
        for b in reads:
            r = b.ready
            if r is None:
                continue
            if r[0] == "eng":
                if r[1] != eng:
                    raise RuntimeError(f"cross-engine read of un-inc'd {b.name} [{tag}]")
            else:
                need(r[0], r[1], f"read {b.name}")
        for b in writes:
            for s, v in b.readers.items():
                if s == own and eng in COMPUTE:
                    continue
                need(s, v, f"WAR {b.name}")
            r = b.ready
            if r is not None:
                if r[0] == "eng":
                    if r[1] != eng:
                        raise RuntimeError(f"WAW cross-engine un-inc'd {b.name} [{tag}]")
                else:
                    need(r[0], r[1], f"WAW {b.name}")
        increc = None
        if inc:
            amt = (16 if eng in ("sync", "gpsimd") else 1) if inc is True else int(inc)
            self.cnt[inc_sem] += amt
            if not dram_only:
                self.cnt_sbuf[inc_sem] = self.cnt[inc_sem]
            increc = (inc_sem, amt)
        self.ops[eng].append((tuple(sorted(waits.items())), build, increc, tag))
        self.last_writes[eng] = frozenset(id(b) for b in writes)
        for s, v in waits.items():
            self.waited[eng][s] = v
        mark = (inc_sem, self.cnt[inc_sem]) if inc else ("eng", eng)
        for b in writes:
            b.ready = mark
            b.readers = {}
        if record:
            for b in reads:
                b.readers[own] = self.cnt[own] if (inc and inc_sem == own) else None

    def note_readers(self, rbufs, eng):
        own = ENG_SEM[eng]
        for b in rbufs:
            b.readers[own] = self.cnt[own]

    def fence(self):
        snap = {s: v for s, v in self.cnt_sbuf.items() if v > 0}
        for e in ENGINES:
            pf = self.pfence[e] or {}
            keys = set(pf) | set(snap)
            self.pfence[e] = {s: max(pf.get(s, 0), snap.get(s, 0)) for s in keys}

    def replay(self, nc, blk, sems):
        for eng in ENGINES:
            ops = self.ops[eng]

            def make(ops=ops):
                def body(e):
                    for waits, build, increc, tag in ops:
                        for s, v in waits:
                            e.wait_ge(sems[s], v)
                        ins = build(e) if build is not None else None
                        if increc is not None:
                            ins.then_inc(sems[increc[0]], increc[1])
                return body
            getattr(blk, eng)(make())


T = 1024
S = 8192
D = 4096
DH = 1024
F = 11008
NT = 8
NFC = 86
FHC = 43
C_SHIFT = -12.0
SCALE = 1.0 / 32.0


def build_program(dbg=False):
    nc = bass.Bass("TRN2", target_bir_lowering=False)

    d_x = nc.dram_tensor("x", [T, D], F32, kind="ExternalInput").ap()
    d_wq = nc.dram_tensor("wq", [24 * 128, D], F32, kind="ExternalInput").ap()
    d_bqkv = nc.dram_tensor("bqkv", [128, 24], F32, kind="ExternalInput").ap()
    d_wo = nc.dram_tensor("wo", [DH, D], F32, kind="ExternalInput").ap()
    d_bo = nc.dram_tensor("bo", [128, D], F32, kind="ExternalInput").ap()
    d_wg = nc.dram_tensor("wg", [128, NFC * D], F32, kind="ExternalInput").ap()
    d_wu = nc.dram_tensor("wu", [128, NFC * D], F32, kind="ExternalInput").ap()
    d_wd = nc.dram_tensor("wd", [8 * 128, NFC * 512], F32, kind="ExternalInput").ap()
    d_id = nc.dram_tensor("ident", [128, 128], F32, kind="ExternalInput").ap()
    d_out = nc.dram_tensor("out", [T, D], F32, kind="ExternalOutput").ap()

    xk = "ExternalOutput" if dbg else "Internal"
    d_x2 = nc.dram_tensor("x2d", [T, D], BF16, kind=xk).ap()
    d_y0 = nc.dram_tensor("y0d", [T, D], BF16, kind=xk).ap()
    d_cik = nc.dram_tensor("cc_in_k", [DH, T], BF16, kind="Internal").ap()
    d_civ = nc.dram_tensor("cc_in_v", [T, DH], BF16, kind="Internal").ap()
    d_cok = nc.dram_tensor("cc_out_k", [8 * DH, T], BF16, addr_space="Shared", kind="Internal").ap()
    d_cov = nc.dram_tensor("cc_out_v", [S, DH], BF16, addr_space="Shared", kind="Internal").ap()
    if dbg:
        d_dbg = {
            "d_qfm": nc.dram_tensor("d_qfm", [128, 8 * T], BF16, kind="ExternalOutput").ap(),
            "d_kfm": nc.dram_tensor("d_kfm", [128, 8 * T], BF16, kind="ExternalOutput").ap(),
            "d_vtm": nc.dram_tensor("d_vtm", [128, 8 * T], BF16, kind="ExternalOutput").ap(),
            "d_osb": nc.dram_tensor("d_osb", [128, 8 * T], BF16, kind="ExternalOutput").ap(),
            "d_den": nc.dram_tensor("d_den", [128, 8], F32, kind="ExternalOutput").ap(),
            "d_a0": nc.dram_tensor("d_a0", [128, D], F32, kind="ExternalOutput").ap(),
            "d_h0": nc.dram_tensor("d_h0", [128, FHC * T], BF16, kind="ExternalOutput").ap(),
            "d_ln3": nc.dram_tensor("d_ln3", [128, 32 * T], BF16, kind="ExternalOutput").ap(),
            "d_ysb": nc.dram_tensor("d_ysb", [128, 8 * D], BF16, kind="ExternalOutput").ap(),
            "d_xt0": nc.dram_tensor("d_xt0", [128, D], F32, kind="ExternalOutput").ap(),
            "d_ln1t0": nc.dram_tensor("d_ln1t0", [128, D], F32, kind="ExternalOutput").ap(),
            "d_ln1fm": nc.dram_tensor("d_ln1fm", [128, 32 * 512], BF16, kind="ExternalOutput").ap(),
            "d_wq0": nc.dram_tensor("d_wq0", [128, D], BF16, kind="ExternalOutput").ap(),
        }

    EM = Em()
    outer = ExitStack()
    live = {}

    def alloc(name, shape, dtype, kind="sbuf", side="left"):
        if kind == "sbuf":
            cm = nc.sbuf_tensor(name, shape, dtype, side=side)
        else:
            cm = nc.psum_tensor(name, shape, dtype)
        ap = cm.__enter__()
        live[name] = cm
        return ap

    def free(*names):
        for n in names:
            live.pop(n).__exit__(None, None, None)

    with outer:
        sems = {s: outer.enter_context(nc.semaphore(s)) for s in ALL_SEMS}

        P = [alloc(f"p{i}", [128, 512], F32, "psum") for i in range(6)]
        Pb = bufs(6, "psum")

        ident = alloc("ident_sb", [128, 128], F32)
        ident_b = Buf("ident")
        c_eps = alloc("c_eps", [128, 1], F32)
        c_s32 = alloc("c_s32", [128, 1], F32)
        c_nb12 = alloc("c_nb12", [128, 1], F32)
        c_i4096 = alloc("c_i4096", [128, 1], F32)
        ones_bf = alloc("ones_bf", [128, 1], BF16)
        consts_b = Buf("consts")
        bqkv_sb = alloc("bqkv_sb", [128, 24], F32)
        bqkv_b = Buf("bqkv")

        scr_bf = alloc("scr_bf", [128, D], BF16)
        scr_b = Buf("scr")
        stn = []
        for i in range(2):
            stn.append({
                "sums": alloc(f"sums_{i}", [128, 2], F32), "sumsb": Buf(f"sums_{i}"),
                "mean": alloc(f"mean_{i}", [128, 1], F32), "meanb": Buf(f"mean_{i}"),
                "msq": alloc(f"msq_{i}", [128, 1], F32), "msqb": Buf(f"msq_{i}"),
                "veps": alloc(f"veps_{i}", [128, 1], F32), "vepsb": Buf(f"veps_{i}"),
                "std": alloc(f"std_{i}", [128, 1], F32), "stdb": Buf(f"std_{i}"),
                "rcp": alloc(f"rcp_{i}", [128, 1], F32), "rcpb": Buf(f"rcp_{i}"),
            })

        def ln_stats(st, src_ap, src_buf, tag):
            EM.emit("vector", lambda v, o=st["sums"][:, 0:1], i_=src_ap:
                    v.tensor_reduce(o, i_, mybir.AxisListType.X, OP.add),
                    reads=[src_buf], writes=[st["sumsb"]], inc=True, tag=tag)
            EM.emit("scalar", lambda s, o=scr_bf[:], i_=src_ap, a=st["sums"][:, 1:2]:
                    s.activation(o, i_, AF.Square, accum_out=a),
                    reads=[src_buf], writes=[scr_b, st["sumsb"]], inc=True, tag=tag)
            EM.emit("scalar", lambda s, o=st["mean"][:], i_=st["sums"][:, 0:1], sc=c_i4096[:, 0:1]:
                    s.activation(o, i_, AF_COPY, scale=sc),
                    reads=[st["sumsb"], consts_b], writes=[st["meanb"]], inc=True, tag=tag)
            EM.emit("scalar", lambda s, o=st["msq"][:], i_=st["mean"][:]:
                    s.activation(o, i_, AF.Square),
                    reads=[st["meanb"]], writes=[st["msqb"]], inc=True, tag=tag)
            EM.emit("vector", lambda v, o=st["veps"][:], i_=st["sums"][:, 1:2], m2=st["msq"][:, 0:1]:
                    v.tensor_scalar(o, i_, 1.0 / D, m2, OP.mult, OP.subtract),
                    reads=[st["sumsb"], st["msqb"]], writes=[st["vepsb"]], inc=True, tag=tag)
            EM.emit("scalar", lambda s, o=st["std"][:], i_=st["veps"][:], b=c_eps[:, 0:1]:
                    s.activation(o, i_, AF.Sqrt, bias=b),
                    reads=[st["vepsb"], consts_b], writes=[st["stdb"]], inc=True, tag=tag)
            EM.emit("vector", lambda v, o=st["rcp"][:], i_=st["std"][:]: v.reciprocal(o, i_),
                    reads=[st["stdb"]], writes=[st["rcpb"]], inc=True, tag=tag)

        def ln_norm(st, out_ap, out_buf, src_ap, src_buf, tag):
            EM.emit("vector",
                    lambda v, o=out_ap, i_=src_ap, b=st["mean"][:, 0:1], sc=st["rcp"][:, 0:1]:
                    v.tensor_scalar(o, i_, b, sc, OP.subtract, OP.mult),
                    reads=[src_buf, st["meanb"], st["rcpb"]], writes=[out_buf], inc=True, tag=tag)

        EM.emit("vector", lambda v: v.memset(c_eps[:], 1e-5), writes=[consts_b], tag="init")
        EM.emit("vector", lambda v: v.memset(c_s32[:], SCALE), writes=[consts_b], tag="init")
        EM.emit("vector", lambda v: v.memset(c_nb12[:], C_SHIFT), writes=[consts_b], tag="init")
        EM.emit("vector", lambda v: v.memset(c_i4096[:], 1.0 / D), writes=[consts_b], tag="init")
        EM.emit("vector", lambda v: v.memset(ones_bf[:], 1.0), writes=[consts_b], inc=True, tag="init")
        EM.emit("sync", lambda sy: sy.dma_start(out=ident[:], in_=d_id),
                writes=[ident_b], inc=True, tag="init")
        identb = alloc("identb_sb", [128, 128], BF16)
        identb_b = Buf("identb")
        EM.emit("gpsimd", lambda g: g.dma_start(out=identb[:], in_=d_id),
                writes=[identb_b], inc=True, tag="init")
        EM.emit("sync", lambda sy: sy.dma_start(out=bqkv_sb[:], in_=d_bqkv),
                writes=[bqkv_b], inc=True, tag="init")

        ptr = [alloc(f"ptr{i}", [128, 1024], BF16, "psum") for i in range(2)]
        ptrb = bufs(2, "ptr")
        q_fm = alloc("q_fm", [128, 8 * T], BF16)
        q_fm_b = Buf("q_fm")
        k_fm = alloc("k_fm", [128, 8 * T], BF16)
        k_fm_b = Buf("k_fm")
        v_fm = alloc("v_fm", [128, 8 * T], BF16)
        v_fm_b = Buf("v_fm")
        v_tm = alloc("v_tm", [128, 8 * T], BF16)
        v_tm_b = Buf("v_tm")
        x_t = [alloc(f"x_t{i}", [128, D], F32) for i in range(2)]
        x_t_b = bufs(2, "x_t")
        ln1_t = [alloc(f"ln1_t{i}", [128, D], BF16) for i in range(2)]
        ln1_t_b = bufs(2, "ln1_t")
        ln1_fm = alloc("ln1_fm", [128, 32 * T], BF16)
        ln1_fm_b = Buf("ln1_fm")
        wq_sl = [alloc(f"wq_sl{i}", [128, D], BF16) for i in range(2)]
        wq_sl_b = bufs(2, "wq_sl")

        def transpose_block(trp, trb, src_ap, src_buf, n_dc, dst_fn, tag):
            for g in range(n_dc // 4):
                bank = g % 2
                for j in range(4):
                    dc = g * 4 + j
                    EM.emit("tensor",
                            lambda t, o=trp[bank][:, j * 128:(j + 1) * 128],
                            i_=src_ap[:, dc * 128:(dc + 1) * 128], idn=identb[:]:
                            t.transpose(o, i_, idn),
                            reads=[src_buf, identb_b], writes=[trb[bank]],
                            inc=(j == 3), record=(j == 3), tag=tag)
                for j in range(4):
                    dc = g * 4 + j
                    dst_ap, dst_buf = dst_fn(dc)
                    EM.emit("vector",
                            lambda v, o=dst_ap, i_=trp[bank][:, j * 128:(j + 1) * 128]:
                            v.tensor_copy(o, i_),
                            reads=[trb[bank]], writes=[dst_buf],
                            inc=(j == 3), record=(j == 3), tag=tag)

        for s_g in range(8):
            par = s_g % 2
            EM.emit("sync",
                    lambda sy, o=x_t[par][:], i_=d_x[s_g * 128:(s_g + 1) * 128, :]:
                    sy.dma_start(out=o, in_=i_),
                    writes=[x_t_b[par]], inc=True, tag="A_ln")
            st = stn[par]
            ln_stats(st, x_t[par][:], x_t_b[par], "A_ln")
            ln_norm(st, ln1_t[par][:], ln1_t_b[par], x_t[par][:], x_t_b[par], "A_ln")
            if dbg and s_g == 0:
                EM.emit("sync", lambda sy: sy.dma_start(out=d_dbg["d_xt0"], in_=x_t[0][:]),
                        reads=[x_t_b[0]], inc=True, tag="dbg")
                EM.emit("gpsimd", lambda g: g.dma_start(out=d_dbg["d_ln1t0"], in_=ln1_t[0][:]),
                        reads=[ln1_t_b[0]], inc=True, tag="dbg")
            transpose_block(
                ptr, ptrb, ln1_t[par][:], ln1_t_b[par], 32,
                lambda dc, s_g=s_g: (ln1_fm[:, dc * T + s_g * 128: dc * T + (s_g + 1) * 128], ln1_fm_b),
                "A_tr")
        if dbg:
            EM.emit("sync", lambda sy: sy.dma_start(out=d_dbg["d_ln1fm"], in_=ln1_fm[:, 0:32 * 512]),
                    reads=[ln1_fm_b], inc=True, tag="dbg")
        cik_b = bufs(8, "cik")
        civ_b = bufs(8, "civ")
        cok_b = Buf("cok")
        cov_b = Buf("cov")
        for fc in list(range(8, 24)) + list(range(0, 8)):
            par = fc % 2
            EM.emit("gpsimd",
                    lambda g, o=wq_sl[par][:], i_=d_wq[fc * 128:(fc + 1) * 128, :]:
                    g.dma_start(out=o, in_=i_),
                    writes=[wq_sl_b[par]], inc=True, tag="A_qkv")
            if dbg and fc == 0:
                EM.emit("sync", lambda sy: sy.dma_start(out=d_dbg["d_wq0"], in_=wq_sl[0][:]),
                        reads=[wq_sl_b[0]], inc=True, tag="dbg")
            if fc < 8:
                dst_ap, dst_buf = q_fm, q_fm_b
                col0 = fc * T
            elif fc < 16:
                dst_ap, dst_buf = k_fm, k_fm_b
                col0 = (fc - 8) * T
            else:
                dst_ap, dst_buf = v_fm, v_fm_b
                col0 = (fc - 16) * T
            for h in range(2):
                bank = (fc * 2 + h) % 2
                for dc in range(32):
                    EM.emit("tensor",
                            lambda t, o=P[bank][:, 0:512],
                            l=wq_sl[par][:, dc * 128:(dc + 1) * 128],
                            r=ln1_fm[:, dc * T + h * 512: dc * T + (h + 1) * 512],
                            st_=(dc == 0), sp=(dc == 31):
                            t.matmul(o, l, r, start=st_, stop=sp, skip_group_check=True),
                            reads=[wq_sl_b[par], ln1_fm_b], writes=[Pb[bank]],
                            inc=(dc == 31), record=(dc == 31), tag="A_qkv")
                EM.emit("vector",
                        lambda v, o=dst_ap[:, col0 + h * 512: col0 + (h + 1) * 512],
                        i_=P[bank][:, 0:512], b=bqkv_sb[:, fc:fc + 1]:
                        v.tensor_scalar_add(o, i_, b),
                        reads=[Pb[bank], bqkv_b], writes=[dst_buf], inc=True, tag="A_qkv")
            if fc == 15:
                for i in range(8):
                    EM.emit("sync",
                            lambda sy, o=d_cik[i * 128:(i + 1) * 128, :], i_=k_fm[:, i * T:(i + 1) * T]:
                            sy.dma_start(out=o, in_=i_),
                            reads=[k_fm_b], writes=[cik_b[i]], inc=True, tag="A_cc")
                EM.emit("gpsimd",
                        lambda g: g.collective_compute(
                            "AllGather", OP.bypass,
                            replica_groups=[[0, 1, 2, 3, 4, 5, 6, 7]],
                            ins=[d_cik], outs=[d_cok]),
                        reads=cik_b, writes=[cok_b], inc=1, dram_only=True,
                        use_sem="cc", tag="A_cc")
            if fc == 23:
                for kt in range(8):
                    for g2 in range(2):
                        bank = (kt * 2 + g2) % 2
                        for j in range(4):
                            dhc = g2 * 4 + j
                            EM.emit("tensor",
                                    lambda t, o=ptr[bank][:, j * 128:(j + 1) * 128],
                                    i_=v_fm[:, dhc * T + kt * 128: dhc * T + (kt + 1) * 128], idn=identb[:]:
                                    t.transpose(o, i_, idn),
                                    reads=[v_fm_b, identb_b], writes=[ptrb[bank]],
                                    inc=(j == 3), record=(j == 3), tag="A_vtr")
                        EM.emit("vector",
                                lambda v, o=v_tm[:, kt * T + g2 * 512: kt * T + (g2 + 1) * 512],
                                i_=ptr[bank][:, 0:512]:
                                v.tensor_copy(o, i_),
                                reads=[ptrb[bank]], writes=[v_tm_b], inc=True, tag="A_vtr")
                for i in range(8):
                    EM.emit("sync",
                            lambda sy, o=d_civ[i * 128:(i + 1) * 128, :], i_=v_tm[:, i * T:(i + 1) * T]:
                            sy.dma_start(out=o, in_=i_),
                            reads=[v_tm_b], writes=[civ_b[i]], inc=True, tag="A_cc")
                EM.emit("gpsimd",
                        lambda g: g.collective_compute(
                            "AllGather", OP.bypass,
                            replica_groups=[[0, 1, 2, 3, 4, 5, 6, 7]],
                            ins=[d_civ], outs=[d_cov]),
                        reads=civ_b, writes=[cov_b], inc=1, dram_only=True,
                        use_sem="cc", tag="A_cc")

        if dbg:
            EM.emit("sync", lambda sy: sy.dma_start(out=d_dbg["d_qfm"], in_=q_fm[:]),
                    reads=[q_fm_b], inc=True, tag="dbg")
            EM.emit("sync", lambda sy: sy.dma_start(out=d_dbg["d_kfm"], in_=k_fm[:]),
                    reads=[k_fm_b], inc=True, tag="dbg")
            EM.emit("sync", lambda sy: sy.dma_start(out=d_dbg["d_vtm"], in_=v_tm[:]),
                    reads=[v_tm_b], inc=True, tag="dbg")

        EM.fence()
        free("wq_sl1", "wq_sl0", "ln1_fm", "ln1_t1", "ln1_t0", "x_t1", "x_t0",
             "v_tm", "v_fm", "k_fm")
        free("ptr1", "ptr0")
        pden = alloc("pden", [128, 512], F32, "psum")
        pden_b = Buf("pden")

        wo_bf = alloc("wo_bf", [128, 8 * D], BF16, side="right")
        wo_b = bufs(8, "wo")
        o_sb = alloc("o_sb", [128, 8 * T], BF16, side="right")
        o_sb_b = Buf("o_sb")
        bo_bc = alloc("bo_bc", [128, D], BF16, side="right")
        bo_b = Buf("bo")
        rden = alloc("rden", [128, 8], F32, side="right")
        rden_b = Buf("rden")
        k_sl = [alloc(f"k_sl{i}", [128, 8 * T], BF16) for i in range(2)]
        k_sl_b = [bufs(8, f"k_sl{i}_") for i in range(2)]
        v_sl = [alloc(f"v_sl{i}", [128, 8 * T], BF16) for i in range(2)]
        v_sl_b = [bufs(8, f"v_sl{i}_") for i in range(2)]
        p_t8 = alloc("p_t8", [128, 8 * 512], BF16)
        p_t8_b = bufs(8, "p_t8")

        for i in range(8):
            EM.emit("gpsimd",
                    lambda g, o=wo_bf[:, i * D:(i + 1) * D], i_=d_wo[i * 128:(i + 1) * 128, :]:
                    g.dma_start(out=o, in_=i_),
                    writes=[wo_b[i]], inc=True, tag="B")
        EM.emit("gpsimd", lambda g: g.dma_start(out=bo_bc[:], in_=d_bo),
                writes=[bo_b], inc=True, tag="B")

        for cb in range(8):
            par = cb % 2
            for i in range(8):
                EM.emit("sync",
                        lambda sy, o=k_sl[par][:, i * T:(i + 1) * T],
                        i_=d_cok[cb * DH + i * 128: cb * DH + (i + 1) * 128, :]:
                        sy.dma_start(out=o, in_=i_),
                        reads=[cok_b], writes=[k_sl_b[par][i]], inc=True, tag="B")
            for i in range(8):
                EM.emit("sync",
                        lambda sy, o=v_sl[par][:, i * T:(i + 1) * T],
                        i_=d_cov[cb * T + i * 128: cb * T + (i + 1) * 128, :]:
                        sy.dma_start(out=o, in_=i_),
                        reads=[cov_b], writes=[v_sl_b[par][i]], inc=True, tag="B")

            for qp in range(2):
                for kt in range(8):
                    bank = kt % 4
                    for dhc in range(8):
                        EM.emit("tensor",
                                lambda t, o=P[bank][:, 0:512],
                                l=k_sl[par][:, dhc * T + kt * 128: dhc * T + (kt + 1) * 128],
                                r=q_fm[:, dhc * T + qp * 512: dhc * T + (qp + 1) * 512],
                                st_=(dhc == 0), sp=(dhc == 7):
                                t.matmul(o, l, r, start=st_, stop=sp, skip_group_check=True),
                                reads=[k_sl_b[par][dhc], q_fm_b], writes=[Pb[bank]],
                                inc=(dhc == 7), record=False, tag="B_s")
                    EM.note_readers([k_sl_b[par][d_] for d_ in range(8)] + [q_fm_b], "tensor")
                    EM.emit("scalar",
                            lambda s, o=p_t8[:, kt * 512:(kt + 1) * 512], i_=P[bank][:, 0:512],
                            b=c_nb12[:, 0:1], sc=c_s32[:, 0:1]:
                            s.activation(o, i_, AF.Exp, bias=b, scale=sc),
                            reads=[Pb[bank], consts_b], writes=[p_t8_b[kt]], inc=True, tag="B_s")
                    for sub in range(4):
                        EM.emit("tensor",
                                lambda t, o=pden[:, qp * 4 + sub: qp * 4 + sub + 1],
                                l=p_t8[:, kt * 512 + sub * 128: kt * 512 + (sub + 1) * 128],
                                r=ones_bf[:],
                                st_=(cb == 0 and qp == 0 and kt == 0 and sub == 0),
                                sp=(cb == 7 and qp == 1 and kt == 7 and sub == 3):
                                t.matmul(o, l, r, start=st_, stop=sp, skip_group_check=True),
                                reads=[p_t8_b[kt], consts_b], writes=[pden_b],
                                inc=(cb == 7 and qp == 1 and kt == 7 and sub == 3),
                                record=False, tag="B_d")
                for qh in range(2):
                    qg = qp * 2 + qh
                    for kt in range(8):
                        for dhc in range(8):
                            ob = 2 + dhc // 2
                            EM.emit("tensor",
                                    lambda t, o=P[ob][:, (dhc % 2) * 256:(dhc % 2) * 256 + 256],
                                    l=v_sl[par][:, kt * T + dhc * 128: kt * T + (dhc + 1) * 128],
                                    r=p_t8[:, kt * 512 + qh * 256: kt * 512 + qh * 256 + 256],
                                    st_=(kt == 0 and dhc % 2 == 0),
                                    sp=(kt == 7 and dhc % 2 == 1):
                                    t.matmul(o, l, r, start=st_, stop=sp, skip_group_check=True),
                                    reads=[v_sl_b[par][kt], p_t8_b[kt]], writes=[Pb[ob]],
                                    inc=(dhc == 7 or (kt == 7 and dhc % 2 == 1)),
                                    record=False, tag="B_pv")
                    EM.note_readers([v_sl_b[par][k_] for k_ in range(8)]
                                    + [p_t8_b[k_] for k_ in range(8)] + [consts_b], "tensor")
                    for dhc in range(8):
                        ob = 2 + dhc // 2
                        osl = o_sb[:, dhc * T + qg * 256: dhc * T + (qg + 1) * 256]
                        psl = P[ob][:, (dhc % 2) * 256:(dhc % 2) * 256 + 256]
                        if cb == 0:
                            EM.emit("vector", lambda v, o=osl, i_=psl: v.tensor_copy(o, i_),
                                    reads=[Pb[ob]], writes=[o_sb_b], inc=True, tag="B_ev")
                        else:
                            EM.emit("vector", lambda v, o=osl, i_=psl: v.tensor_tensor(o, o, i_, OP.add),
                                    reads=[Pb[ob], o_sb_b], writes=[o_sb_b], inc=True, tag="B_ev")

        EM.emit("vector", lambda v: v.reciprocal(rden[:], pden[:, 0:8]),
                reads=[pden_b], writes=[rden_b], inc=True, tag="B")

        if dbg:
            EM.emit("sync", lambda sy: sy.dma_start(out=d_dbg["d_osb"], in_=o_sb[:]),
                    reads=[o_sb_b], inc=True, tag="dbg")
            EM.emit("sync", lambda sy: sy.dma_start(out=d_dbg["d_den"], in_=rden[:]),
                    reads=[rden_b], inc=True, tag="dbg")

        EM.fence()
        free("p_t8", "v_sl1", "v_sl0", "k_sl1", "k_sl0", "q_fm")
        free("pden")

        a_sb = [alloc(f"a_sb{i}", [128, D], F32) for i in range(3)]
        a_sb_b = bufs(3, "a_sb")
        lnb = [alloc("lnb0", [128, D], F32)]
        lnb_b = bufs(1, "lnb")
        xr_t = [alloc(f"xr_t{i}", [128, D], F32) for i in range(2)]
        xr_t_b = bufs(2, "xr")
        x2_b = bufs(8, "x2")

        for sg in range(8):
            par = sg % 3
            spar = sg % 2
            EM.emit("sync",
                    lambda sy, o=xr_t[spar][:], i_=d_x[sg * 128:(sg + 1) * 128, :]:
                    sy.dma_start(out=o, in_=i_),
                    writes=[xr_t_b[spar]], inc=True, tag="C")
            for cc in range(8):
                bank = cc % 2
                for dhc in range(8):
                    EM.emit("tensor",
                            lambda t, o=P[bank][:, 0:512],
                            l=o_sb[:, dhc * T + sg * 128: dhc * T + (sg + 1) * 128],
                            r=wo_bf[:, dhc * D + cc * 512: dhc * D + (cc + 1) * 512],
                            st_=(dhc == 0), sp=(dhc == 7):
                            t.matmul(o, l, r, start=st_, stop=sp, skip_group_check=True),
                            reads=[o_sb_b, wo_b[dhc]], writes=[Pb[bank]],
                            inc=(dhc == 7), record=False, tag="C_mm")
                EM.note_readers([o_sb_b] + wo_b, "tensor")
                EM.emit("scalar",
                        lambda s, o=a_sb[par][:, cc * 512:(cc + 1) * 512],
                        i_=P[bank][:, 0:512], sc=rden[:, sg:sg + 1]:
                        s.activation(o, i_, AF_COPY, scale=sc),
                        reads=[Pb[bank], rden_b], writes=[a_sb_b[par]], inc=True, tag="C_mm")
            EM.emit("vector", lambda v, o=a_sb[par][:], b=bo_bc[:]:
                    v.tensor_tensor(o, o, b, OP.add),
                    reads=[a_sb_b[par], bo_b], writes=[a_sb_b[par]], inc=True, tag="C")
            if dbg and sg == 0:
                EM.emit("sync", lambda sy: sy.dma_start(out=d_dbg["d_a0"], in_=a_sb[0][:]),
                        reads=[a_sb_b[0]], inc=True, tag="dbg")
            st = stn[sg % 2]
            ln_stats(st, a_sb[par][:], a_sb_b[par], "C")
            ln_norm(st, lnb[0][:], lnb_b[0], a_sb[par][:], a_sb_b[par], "C")
            EM.emit("vector", lambda v, o=lnb[0][:], x_=xr_t[spar][:]:
                    v.tensor_tensor(o, o, x_, OP.add),
                    reads=[lnb_b[0], xr_t_b[spar]], writes=[lnb_b[0]], inc=True, tag="C")
            EM.emit("gpsimd",
                    lambda g, o=d_x2[sg * 128:(sg + 1) * 128, :], i_=lnb[0][:]:
                    g.dma_start(out=o, in_=i_),
                    reads=[lnb_b[0]], writes=[x2_b[sg]], inc=True, tag="C")

        EM.fence()
        free("xr_t1", "xr_t0", "lnb0", "a_sb2", "a_sb1", "a_sb0",
             "rden", "bo_bc", "o_sb", "wo_bf")

        ln3_fm = alloc("ln3_fm", [128, 32 * T], BF16, side="right")
        ln3_fm_b = Buf("ln3_fm")
        hsb = alloc("hsb", [128, FHC * T], BF16)
        hsb_b = Buf("hsb")
        x2t = [alloc(f"x2t{i}", [128, D], BF16) for i in range(2)]
        x2t_b = bufs(2, "x2t")
        ln3t = [alloc(f"ln3t{i}", [128, D], BF16) for i in range(2)]
        ln3t_b = bufs(2, "ln3t")
        ptr2 = [alloc(f"ptr2_{i}", [128, 1024], BF16, "psum") for i in range(2)]
        ptr2b = bufs(2, "ptr2")

        for sg in range(8):
            par = sg % 2
            EM.emit("sync",
                    lambda sy, o=x2t[par][:], i_=d_x2[sg * 128:(sg + 1) * 128, :]:
                    sy.dma_start(out=o, in_=i_),
                    reads=[x2_b[sg]], writes=[x2t_b[par]], inc=True, tag="Dp")
            st = stn[par]
            ln_stats(st, x2t[par][:], x2t_b[par], "Dp")
            ln_norm(st, ln3t[par][:], ln3t_b[par], x2t[par][:], x2t_b[par], "Dp")
            transpose_block(
                ptr2, ptr2b, ln3t[par][:], ln3t_b[par], 32,
                lambda dc, sg=sg: (ln3_fm[:, dc * T + sg * 128: dc * T + (sg + 1) * 128], ln3_fm_b),
                "Dp_tr")
        if dbg:
            EM.emit("sync", lambda sy: sy.dma_start(out=d_dbg["d_ln3"], in_=ln3_fm[:]),
                    reads=[ln3_fm_b], inc=True, tag="dbg")

        EM.fence()
        free("ptr2_1", "ptr2_0")
        free("ln3t1", "ln3t0", "x2t1", "x2t0")
        y0_b = {}
        ysb_b = bufs(8, "ysb")
        WSEG = 22 * 512

        for fh in range(2):
            if fh == 1:
                EM.fence()
            wg_sl = [alloc(f"wg_sl{fh}_{i}", [128, D], BF16) for i in range(2)]
            wg_sl_b = bufs(2, "wg_sl")
            wu_sl = [alloc(f"wu_sl{fh}_{i}", [128, D], BF16) for i in range(2)]
            wu_sl_b = bufs(2, "wu_sl")
            sg_t = [alloc(f"sg_t{fh}_{i}", [128, 512], F32) for i in range(2)]
            sg_t_b = bufs(2, "sg_t")
            for fc in range(FHC):
                fg = fh * FHC + fc
                par = fc % 2
                EM.emit("gpsimd",
                        lambda g, o=wg_sl[par][:], i_=d_wg[:, fg * D:(fg + 1) * D]:
                        g.dma_start(out=o, in_=i_),
                        writes=[wg_sl_b[par]], inc=True, tag="D_gu")
                EM.emit("gpsimd",
                        lambda g, o=wu_sl[par][:], i_=d_wu[:, fg * D:(fg + 1) * D]:
                        g.dma_start(out=o, in_=i_),
                        writes=[wu_sl_b[par]], inc=True, tag="D_gu")
                for grp in range(2):
                    gb, ub = grp, 2 + grp
                    for dc in range(32):
                        EM.emit("tensor",
                                lambda t, o=P[gb][:, 0:512],
                                l=wg_sl[par][:, dc * 128:(dc + 1) * 128],
                                r=ln3_fm[:, dc * T + grp * 512: dc * T + grp * 512 + 512],
                                st_=(dc == 0), sp=(dc == 31):
                                t.matmul(o, l, r, start=st_, stop=sp, skip_group_check=True),
                                reads=[wg_sl_b[par], ln3_fm_b], writes=[Pb[gb]],
                                inc=(dc == 31), record=False, tag="D_gu")
                    for dc in range(32):
                        EM.emit("tensor",
                                lambda t, o=P[ub][:, 0:512],
                                l=wu_sl[par][:, dc * 128:(dc + 1) * 128],
                                r=ln3_fm[:, dc * T + grp * 512: dc * T + grp * 512 + 512],
                                st_=(dc == 0), sp=(dc == 31):
                                t.matmul(o, l, r, start=st_, stop=sp, skip_group_check=True),
                                reads=[wu_sl_b[par], ln3_fm_b], writes=[Pb[ub]],
                                inc=(dc == 31), record=False, tag="D_gu")
                    EM.note_readers([wg_sl_b[par], wu_sl_b[par], ln3_fm_b], "tensor")
                    EM.emit("scalar",
                            lambda s, o=sg_t[grp][:], i_=P[gb][:, 0:512]:
                            s.activation(o, i_, AF.Silu),
                            reads=[Pb[gb]], writes=[sg_t_b[grp]], inc=True, tag="D_gu")
                    EM.emit("vector",
                            lambda v, o=hsb[:, fc * T + grp * 512: fc * T + grp * 512 + 512],
                            a=sg_t[grp][:], b=P[ub][:, 0:512]:
                            v.tensor_tensor(o, a, b, OP.mult),
                            reads=[sg_t_b[grp], Pb[ub]], writes=[hsb_b], inc=True, tag="D_gu")
            if dbg and fh == 0:
                EM.emit("sync", lambda sy: sy.dma_start(out=d_dbg["d_h0"], in_=hsb[:]),
                        reads=[hsb_b], inc=True, tag="dbg")
            EM.fence()
            free(f"sg_t{fh}_1", f"sg_t{fh}_0", f"wu_sl{fh}_1", f"wu_sl{fh}_0",
                 f"wg_sl{fh}_1", f"wg_sl{fh}_0")
            if fh == 0:
                wd_sl = [alloc(f"wd_sl0_{i}", [128, WSEG], BF16) for i in range(2)]
                wd_sl_b = bufs(2, "wd_sl")
                y0t = [alloc(f"y0t{i}", [128, 512], BF16) for i in range(4)]
                y0t_b = bufs(4, "y0t")
            else:
                free("ln3_fm")
                ysb = alloc("ysb", [128, 8 * D], BF16, side="right")
                wd_sl = [alloc(f"wd_sl1_{i}", [128, WSEG], BF16) for i in range(2)]
                wd_sl_b = bufs(2, "wd_sl")
                y0r = [alloc(f"y0r{i}", [128, 512], BF16) for i in range(4)]
                y0r_b = bufs(4, "y0r")
            y67 = [alloc(f"y67_{fh}_{i}", [128, 512], F32, "psum") for i in range(2)]
            PB8 = P + y67
            Pb8 = Pb + bufs(2, f"y67_{fh}_")

            def y0r_load(sub, cbk):
                par2 = sub % 4
                EM.emit("sync",
                        lambda sy, o=y0r[par2][:],
                        i_=d_y0[sub * 128:(sub + 1) * 128, cbk * 512:(cbk + 1) * 512]:
                        sy.dma_start(out=o, in_=i_),
                        reads=[y0_b[(sub, cbk)]], writes=[y0r_b[par2]], inc=True, tag="D_dn")

            for cbk in range(8):
                if fh == 1:
                    for sub in range(4):
                        y0r_load(sub, cbk)
                for h2 in range(2):
                    seg = 22 if h2 == 0 else 21
                    spar = (cbk * 2 + h2) % 2
                    col0 = (fh * FHC + h2 * 22) * 512
                    EM.emit("gpsimd",
                            lambda g, o=wd_sl[spar][:, 0:seg * 512],
                            i_=d_wd[cbk * 128:(cbk + 1) * 128, col0:col0 + seg * 512]:
                            g.dma_start(out=o, in_=i_),
                            writes=[wd_sl_b[spar]], inc=True, tag="D_dn")
                    for fcl in range(seg):
                        fc = h2 * 22 + fcl
                        for sub in range(8):
                            EM.emit("tensor",
                                    lambda t, o=PB8[sub][:, 0:512],
                                    l=hsb[:, fc * T + sub * 128: fc * T + (sub + 1) * 128],
                                    r=wd_sl[spar][:, fcl * 512:(fcl + 1) * 512],
                                    st_=(fc == 0), sp=(fc == FHC - 1):
                                    t.matmul(o, l, r, start=st_, stop=sp, skip_group_check=True),
                                    reads=[hsb_b, wd_sl_b[spar]], writes=[Pb8[sub]],
                                    inc=(sub == 7 or fc == FHC - 1), record=False, tag="D_dn")
                    EM.note_readers([hsb_b, wd_sl_b[spar]], "tensor")
                if fh == 0:
                    for sub in range(8):
                        yt = sub % 4
                        eng = "scalar" if sub % 2 == 0 else "vector"
                        if eng == "scalar":
                            EM.emit("scalar",
                                    lambda s, o=y0t[yt][:], i_=PB8[sub][:, 0:512]:
                                    s.activation(o, i_, AF_COPY),
                                    reads=[Pb8[sub]], writes=[y0t_b[yt]], inc=True, tag="D_dn")
                        else:
                            EM.emit("vector",
                                    lambda v, o=y0t[yt][:], i_=PB8[sub][:, 0:512]:
                                    v.tensor_copy(o, i_),
                                    reads=[Pb8[sub]], writes=[y0t_b[yt]], inc=True, tag="D_dn")
                        b = Buf(f"y0_{sub}_{cbk}")
                        y0_b[(sub, cbk)] = b
                        EM.emit("sync",
                                lambda sy, o=d_y0[sub * 128:(sub + 1) * 128, cbk * 512:(cbk + 1) * 512],
                                i_=y0t[yt][:]:
                                sy.dma_start(out=o, in_=i_),
                                reads=[y0t_b[yt]], writes=[b], inc=True, tag="D_dn")
                else:
                    for sub in range(8):
                        par2 = sub % 4
                        EM.emit("vector",
                                lambda v, o=ysb[:, sub * D + cbk * 512: sub * D + (cbk + 1) * 512],
                                a=y0r[par2][:], b=PB8[sub][:, 0:512]:
                                v.tensor_tensor(o, a, b, OP.add),
                                reads=[y0r_b[par2], Pb8[sub]], writes=[ysb_b[sub]], inc=True, tag="D_dn")
                        if sub < 4:
                            y0r_load(sub + 4, cbk)
            free("y67_%d_1" % fh, "y67_%d_0" % fh)
            if fh == 0:
                free("y0t3", "y0t2", "y0t1", "y0t0", "wd_sl0_1", "wd_sl0_0")

        if dbg:
            EM.emit("sync", lambda sy: sy.dma_start(out=d_dbg["d_ysb"], in_=ysb[:]),
                    reads=ysb_b, inc=True, tag="dbg")

        EM.fence()
        free("y0r3", "y0r2", "y0r1", "y0r0", "wd_sl1_1", "wd_sl1_0", "hsb")
        x2f = [alloc(f"x2f{i}", [128, D], BF16) for i in range(2)]
        x2f_b = bufs(2, "x2f")
        outf = [alloc(f"outf{i}", [128, D], F32) for i in range(2)]
        outf_b = bufs(2, "outf")
        out_done = bufs(8, "out")

        for sub in range(8):
            par = sub % 2
            EM.emit("sync",
                    lambda sy, o=x2f[par][:], i_=d_x2[sub * 128:(sub + 1) * 128, :]:
                    sy.dma_start(out=o, in_=i_),
                    reads=[x2_b[sub]], writes=[x2f_b[par]], inc=True, tag="E")
            st = stn[par]
            ysl = ysb[:, sub * D:(sub + 1) * D]
            ln_stats(st, ysl, ysb_b[sub], "E")
            ln_norm(st, outf[par][:], outf_b[par], ysl, ysb_b[sub], "E")
            EM.emit("vector", lambda v, o=outf[par][:], x_=x2f[par][:]:
                    v.tensor_tensor(o, o, x_, OP.add),
                    reads=[outf_b[par], x2f_b[par]], writes=[outf_b[par]], inc=True, tag="E")
            EM.emit("sync",
                    lambda sy, o=d_out[sub * 128:(sub + 1) * 128, :], i_=outf[par][:]:
                    sy.dma_start(out=o, in_=i_),
                    reads=[outf_b[par]], writes=[out_done[sub]], inc=True, tag="E")
        EM.emit("sync", None, reads=out_done, tag="E")

        blk = outer.enter_context(nc.Block())
        EM.replay(nc, blk, sems)
    return nc


def _prep(L0_attn_x, L0_attn_Wqkv, L0_attn_bqkv, L0_attn_Wo, L0_attn_bo,
          L0_ffn_W_gate, L0_ffn_W_up, L0_ffn_W_down):
    x = np.ascontiguousarray(np.asarray(L0_attn_x, dtype=np.float32).reshape(S, D))
    Wqkv = np.asarray(L0_attn_Wqkv, dtype=np.float32)
    wq_t = np.ascontiguousarray(
        Wqkv.reshape(32, 128, 24, 128).transpose(2, 1, 0, 3)).reshape(24 * 128, D)
    bqkv_t = np.ascontiguousarray(
        np.asarray(L0_attn_bqkv, dtype=np.float32).reshape(24, 128).T)
    wo = np.ascontiguousarray(np.asarray(L0_attn_Wo, dtype=np.float32))
    bo_b = np.ascontiguousarray(
        np.broadcast_to(np.asarray(L0_attn_bo, dtype=np.float32), (128, D)))
    wg_t = np.ascontiguousarray(
        np.asarray(L0_ffn_W_gate, dtype=np.float32)
        .reshape(NFC, 128, 32, 128).transpose(3, 0, 2, 1)).reshape(128, NFC * D)
    wu_t = np.ascontiguousarray(
        np.asarray(L0_ffn_W_up, dtype=np.float32)
        .reshape(NFC, 128, 32, 128).transpose(3, 0, 2, 1)).reshape(128, NFC * D)
    wd_t = np.ascontiguousarray(
        np.asarray(L0_ffn_W_down, dtype=np.float32)
        .reshape(8, 512, NFC, 128).transpose(0, 3, 2, 1)).reshape(8 * 128, NFC * 512)
    ident = np.eye(128, dtype=np.float32)
    common = {"wq": wq_t, "bqkv": bqkv_t, "wo": wo, "bo": bo_b,
              "wg": wg_t, "wu": wu_t, "wd": wd_t, "ident": ident}
    return x, common


def kernel(L0_attn_x, L0_attn_Wqkv, L0_attn_bqkv, L0_attn_Wo, L0_attn_bo,
           L0_ffn_W_gate, L0_ffn_W_up, L0_ffn_W_down, _dbg=False):
    x, common = _prep(L0_attn_x, L0_attn_Wqkv, L0_attn_bqkv, L0_attn_Wo,
                      L0_attn_bo, L0_ffn_W_gate, L0_ffn_W_up, L0_ffn_W_down)
    nc = build_program(dbg=_dbg)
    in_maps = [dict(common, x=np.ascontiguousarray(x[c * T:(c + 1) * T]))
               for c in range(8)]
    res = run_bass_kernel_spmd(nc, in_maps, core_ids=list(range(8)))
    out = np.concatenate([res.results[c]["out"] for c in range(8)], axis=0)
    out = out.reshape(1, S, D).astype(np.float32)
    if _dbg:
        return out, res
    return out


# version 16 (final)
# speedup vs baseline: 1.2989x; 1.0187x over previous
import numpy as np
from contextlib import ExitStack

import concourse.bass as bass
import concourse.mybir as mybir
from concourse.bass_utils import run_bass_kernel_spmd

F32 = mybir.dt.float32
BF16 = mybir.dt.bfloat16
AF = mybir.ActivationFunctionType
OP = mybir.AluOpType
AF_COPY = getattr(AF, "Copy", None)
if AF_COPY is None:
    AF_COPY = getattr(AF, "Identity")

ENGINES = ("tensor", "vector", "scalar", "sync", "gpsimd")
ENG_SEM = {"tensor": "mt", "vector": "mv", "scalar": "ms", "sync": "dsy", "gpsimd": "dgp"}
ALL_SEMS = ("mt", "mv", "ms", "dsy", "dgp", "cc")
COMPUTE = ("tensor", "vector", "scalar")


class Buf:
    __slots__ = ("name", "ready", "readers")

    def __init__(self, name=""):
        self.name = name
        self.ready = None
        self.readers = {}


def bufs(n, name):
    return [Buf(f"{name}{i}") for i in range(n)]


class Em:
    def __init__(self):
        self.ops = {e: [] for e in ENGINES}
        self.cnt = {s: 0 for s in ALL_SEMS}
        self.cnt_sbuf = {s: 0 for s in ALL_SEMS}
        self.waited = {e: {} for e in ENGINES}
        self.pfence = {e: None for e in ENGINES}
        self.last_writes = {e: frozenset() for e in ENGINES}

    def emit(self, eng, build, reads=(), writes=(), inc=False, record=True,
             dram_only=False, use_sem=None, tag=""):
        own = ENG_SEM[eng]
        inc_sem = use_sem or own
        if eng == "vector" and any(id(b) in self.last_writes[eng] for b in reads):
            self.ops[eng].append(((), lambda v: v.engine_nop(), None, "nopg"))
            self.last_writes[eng] = frozenset()
        waits = {}

        def need(sem, val, why):
            if sem == own and eng in COMPUTE:
                return
            if val is None:
                raise RuntimeError(f"un-inc'd dependency: {why} [{tag}] on {eng}")
            if self.waited[eng].get(sem, 0) >= val:
                return
            if waits.get(sem, 0) < val:
                waits[sem] = val

        if eng in ("sync", "gpsimd"):
            need(own, self.cnt[own], "dma-serialize")
        pf = self.pfence[eng]
        if pf:
            self.pfence[eng] = None
            for s, v in pf.items():
                need(s, v, "fence")
        for b in reads:
            r = b.ready
            if r is None:
                continue
            if r[0] == "eng":
                if r[1] != eng:
                    raise RuntimeError(f"cross-engine read of un-inc'd {b.name} [{tag}]")
            else:
                need(r[0], r[1], f"read {b.name}")
        for b in writes:
            for s, v in b.readers.items():
                if s == own and eng in COMPUTE:
                    continue
                need(s, v, f"WAR {b.name}")
            r = b.ready
            if r is not None:
                if r[0] == "eng":
                    if r[1] != eng:
                        raise RuntimeError(f"WAW cross-engine un-inc'd {b.name} [{tag}]")
                else:
                    need(r[0], r[1], f"WAW {b.name}")
        increc = None
        if inc:
            amt = (16 if eng in ("sync", "gpsimd") else 1) if inc is True else int(inc)
            self.cnt[inc_sem] += amt
            if not dram_only:
                self.cnt_sbuf[inc_sem] = self.cnt[inc_sem]
            increc = (inc_sem, amt)
        self.ops[eng].append((tuple(sorted(waits.items())), build, increc, tag))
        self.last_writes[eng] = frozenset(id(b) for b in writes)
        for s, v in waits.items():
            self.waited[eng][s] = v
        mark = (inc_sem, self.cnt[inc_sem]) if inc else ("eng", eng)
        for b in writes:
            b.ready = mark
            b.readers = {}
        if record:
            for b in reads:
                b.readers[own] = self.cnt[own] if (inc and inc_sem == own) else None

    def note_readers(self, rbufs, eng):
        own = ENG_SEM[eng]
        for b in rbufs:
            b.readers[own] = self.cnt[own]

    def fence(self, skip=()):
        snap = {s: v for s, v in self.cnt_sbuf.items() if v > 0}
        for e in ENGINES:
            if e in skip:
                continue
            pf = self.pfence[e] or {}
            keys = set(pf) | set(snap)
            self.pfence[e] = {s: max(pf.get(s, 0), snap.get(s, 0)) for s in keys}

    def replay(self, nc, blk, sems):
        for eng in ENGINES:
            ops = self.ops[eng]

            def make(ops=ops):
                def body(e):
                    for waits, build, increc, tag in ops:
                        for s, v in waits:
                            e.wait_ge(sems[s], v)
                        ins = build(e) if build is not None else None
                        if increc is not None:
                            ins.then_inc(sems[increc[0]], increc[1])
                return body
            getattr(blk, eng)(make())


T = 1024
S = 8192
D = 4096
DH = 1024
F = 11008
NT = 8
NFC = 86
FHC = 43
C_SHIFT = -12.0
SCALE = 1.0 / 32.0


def build_program(dbg=False):
    nc = bass.Bass("TRN2", target_bir_lowering=False)

    d_x = nc.dram_tensor("x", [T, D], F32, kind="ExternalInput").ap()
    d_wq = nc.dram_tensor("wq", [24 * 128, D], F32, kind="ExternalInput").ap()
    d_bqkv = nc.dram_tensor("bqkv", [128, 24], F32, kind="ExternalInput").ap()
    d_wo = nc.dram_tensor("wo", [DH, D], F32, kind="ExternalInput").ap()
    d_bo = nc.dram_tensor("bo", [128, D], F32, kind="ExternalInput").ap()
    d_wg = nc.dram_tensor("wg", [128, NFC * D], F32, kind="ExternalInput").ap()
    d_wu = nc.dram_tensor("wu", [128, NFC * D], F32, kind="ExternalInput").ap()
    d_wd = nc.dram_tensor("wd", [8 * 128, NFC * 512], F32, kind="ExternalInput").ap()
    d_id = nc.dram_tensor("ident", [128, 128], F32, kind="ExternalInput").ap()
    d_out = nc.dram_tensor("out", [T, D], F32, kind="ExternalOutput").ap()

    xk = "ExternalOutput" if dbg else "Internal"
    d_x2 = nc.dram_tensor("x2d", [T, D], BF16, kind=xk).ap()
    d_y0 = nc.dram_tensor("y0d", [T, D], BF16, kind=xk).ap()
    d_cik = nc.dram_tensor("cc_in_k", [DH, T], BF16, kind="Internal").ap()
    d_civ = nc.dram_tensor("cc_in_v", [T, DH], BF16, kind="Internal").ap()
    d_cok = nc.dram_tensor("cc_out_k", [8 * DH, T], BF16, addr_space="Shared", kind="Internal").ap()
    d_cov = nc.dram_tensor("cc_out_v", [S, DH], BF16, addr_space="Shared", kind="Internal").ap()
    if dbg:
        d_dbg = {
            "d_qfm": nc.dram_tensor("d_qfm", [128, 8 * T], BF16, kind="ExternalOutput").ap(),
            "d_kfm": nc.dram_tensor("d_kfm", [128, 8 * T], BF16, kind="ExternalOutput").ap(),
            "d_vtm": nc.dram_tensor("d_vtm", [128, 8 * T], BF16, kind="ExternalOutput").ap(),
            "d_osb": nc.dram_tensor("d_osb", [128, 8 * T], BF16, kind="ExternalOutput").ap(),
            "d_den": nc.dram_tensor("d_den", [128, 8], F32, kind="ExternalOutput").ap(),
            "d_a0": nc.dram_tensor("d_a0", [128, D], F32, kind="ExternalOutput").ap(),
            "d_h0": nc.dram_tensor("d_h0", [128, FHC * T], BF16, kind="ExternalOutput").ap(),
            "d_ln3": nc.dram_tensor("d_ln3", [128, 32 * T], BF16, kind="ExternalOutput").ap(),
            "d_ysb": nc.dram_tensor("d_ysb", [128, 8 * D], BF16, kind="ExternalOutput").ap(),
            "d_xt0": nc.dram_tensor("d_xt0", [128, D], F32, kind="ExternalOutput").ap(),
            "d_ln1t0": nc.dram_tensor("d_ln1t0", [128, D], F32, kind="ExternalOutput").ap(),
            "d_ln1fm": nc.dram_tensor("d_ln1fm", [128, 32 * 512], BF16, kind="ExternalOutput").ap(),
            "d_wq0": nc.dram_tensor("d_wq0", [128, D], BF16, kind="ExternalOutput").ap(),
        }

    EM = Em()
    outer = ExitStack()
    live = {}

    def alloc(name, shape, dtype, kind="sbuf", side="left"):
        if kind == "sbuf":
            cm = nc.sbuf_tensor(name, shape, dtype, side=side)
        else:
            cm = nc.psum_tensor(name, shape, dtype)
        ap = cm.__enter__()
        live[name] = cm
        return ap

    def free(*names):
        for n in names:
            live.pop(n).__exit__(None, None, None)

    with outer:
        sems = {s: outer.enter_context(nc.semaphore(s)) for s in ALL_SEMS}

        P = [alloc(f"p{i}", [128, 512], F32, "psum") for i in range(6)]
        Pb = bufs(6, "psum")

        ident = alloc("ident_sb", [128, 128], F32)
        ident_b = Buf("ident")
        c_eps = alloc("c_eps", [128, 1], F32)
        c_s32 = alloc("c_s32", [128, 1], F32)
        c_nb12 = alloc("c_nb12", [128, 1], F32)
        c_i4096 = alloc("c_i4096", [128, 1], F32)
        ones_bf = alloc("ones_bf", [128, 1], BF16)
        consts_b = Buf("consts")
        bqkv_sb = alloc("bqkv_sb", [128, 24], F32)
        bqkv_b = Buf("bqkv")

        scr_bf = alloc("scr_bf", [128, D], BF16)
        scr_b = Buf("scr")
        stn = []
        for i in range(2):
            stn.append({
                "sums": alloc(f"sums_{i}", [128, 2], F32), "sumsb": Buf(f"sums_{i}"),
                "mean": alloc(f"mean_{i}", [128, 1], F32), "meanb": Buf(f"mean_{i}"),
                "msq": alloc(f"msq_{i}", [128, 1], F32), "msqb": Buf(f"msq_{i}"),
                "veps": alloc(f"veps_{i}", [128, 1], F32), "vepsb": Buf(f"veps_{i}"),
                "std": alloc(f"std_{i}", [128, 1], F32), "stdb": Buf(f"std_{i}"),
                "rcp": alloc(f"rcp_{i}", [128, 1], F32), "rcpb": Buf(f"rcp_{i}"),
            })

        def ln_stats(st, src_ap, src_buf, tag):
            EM.emit("vector", lambda v, o=st["sums"][:, 0:1], i_=src_ap:
                    v.tensor_reduce(o, i_, mybir.AxisListType.X, OP.add),
                    reads=[src_buf], writes=[st["sumsb"]], inc=True, tag=tag)
            EM.emit("scalar", lambda s, o=scr_bf[:], i_=src_ap, a=st["sums"][:, 1:2]:
                    s.activation(o, i_, AF.Square, accum_out=a),
                    reads=[src_buf], writes=[scr_b, st["sumsb"]], inc=True, tag=tag)
            EM.emit("scalar", lambda s, o=st["mean"][:], i_=st["sums"][:, 0:1], sc=c_i4096[:, 0:1]:
                    s.activation(o, i_, AF_COPY, scale=sc),
                    reads=[st["sumsb"], consts_b], writes=[st["meanb"]], inc=True, tag=tag)
            EM.emit("scalar", lambda s, o=st["msq"][:], i_=st["mean"][:]:
                    s.activation(o, i_, AF.Square),
                    reads=[st["meanb"]], writes=[st["msqb"]], inc=True, tag=tag)
            EM.emit("vector", lambda v, o=st["veps"][:], i_=st["sums"][:, 1:2], m2=st["msq"][:, 0:1]:
                    v.tensor_scalar(o, i_, 1.0 / D, m2, OP.mult, OP.subtract),
                    reads=[st["sumsb"], st["msqb"]], writes=[st["vepsb"]], inc=True, tag=tag)
            EM.emit("scalar", lambda s, o=st["std"][:], i_=st["veps"][:], b=c_eps[:, 0:1]:
                    s.activation(o, i_, AF.Sqrt, bias=b),
                    reads=[st["vepsb"], consts_b], writes=[st["stdb"]], inc=True, tag=tag)
            EM.emit("vector", lambda v, o=st["rcp"][:], i_=st["std"][:]: v.reciprocal(o, i_),
                    reads=[st["stdb"]], writes=[st["rcpb"]], inc=True, tag=tag)

        def ln_norm(st, out_ap, out_buf, src_ap, src_buf, tag):
            EM.emit("vector",
                    lambda v, o=out_ap, i_=src_ap, b=st["mean"][:, 0:1], sc=st["rcp"][:, 0:1]:
                    v.tensor_scalar(o, i_, b, sc, OP.subtract, OP.mult),
                    reads=[src_buf, st["meanb"], st["rcpb"]], writes=[out_buf], inc=True, tag=tag)

        EM.emit("vector", lambda v: v.memset(c_eps[:], 1e-5), writes=[consts_b], tag="init")
        EM.emit("vector", lambda v: v.memset(c_s32[:], SCALE), writes=[consts_b], tag="init")
        EM.emit("vector", lambda v: v.memset(c_nb12[:], C_SHIFT), writes=[consts_b], tag="init")
        EM.emit("vector", lambda v: v.memset(c_i4096[:], 1.0 / D), writes=[consts_b], tag="init")
        EM.emit("vector", lambda v: v.memset(ones_bf[:], 1.0), writes=[consts_b], inc=True, tag="init")
        EM.emit("sync", lambda sy: sy.dma_start(out=ident[:], in_=d_id),
                writes=[ident_b], inc=True, tag="init")
        identb = alloc("identb_sb", [128, 128], BF16)
        identb_b = Buf("identb")
        EM.emit("gpsimd", lambda g: g.dma_start(out=identb[:], in_=d_id),
                writes=[identb_b], inc=True, tag="init")
        EM.emit("sync", lambda sy: sy.dma_start(out=bqkv_sb[:], in_=d_bqkv),
                writes=[bqkv_b], inc=True, tag="init")

        ptr = [alloc(f"ptr{i}", [128, 1024], BF16, "psum") for i in range(2)]
        ptrb = bufs(2, "ptr")
        q_fm = alloc("q_fm", [128, 8 * T], BF16)
        q_fm_b = Buf("q_fm")
        k_fm = alloc("k_fm", [128, 8 * T], BF16)
        k_fm_b = Buf("k_fm")
        v_fm = alloc("v_fm", [128, 8 * T], BF16)
        v_fm_b = Buf("v_fm")
        v_tm = alloc("v_tm", [128, 8 * T], BF16)
        v_tm_b = Buf("v_tm")
        x_t = [alloc(f"x_t{i}", [128, D], F32) for i in range(2)]
        x_t_b = bufs(2, "x_t")
        ln1_t = [alloc(f"ln1_t{i}", [128, D], BF16) for i in range(2)]
        ln1_t_b = bufs(2, "ln1_t")
        ln1_fm = alloc("ln1_fm", [128, 32 * T], BF16)
        ln1_fm_b = Buf("ln1_fm")
        wq_sl = [alloc(f"wq_sl{i}", [128, D], BF16) for i in range(2)]
        wq_sl_b = bufs(2, "wq_sl")

        def transpose_block(trp, trb, src_ap, src_buf, n_dc, dst_fn, tag):
            for g in range(n_dc // 4):
                bank = g % 2
                for j in range(4):
                    dc = g * 4 + j
                    EM.emit("tensor",
                            lambda t, o=trp[bank][:, j * 128:(j + 1) * 128],
                            i_=src_ap[:, dc * 128:(dc + 1) * 128], idn=identb[:]:
                            t.transpose(o, i_, idn),
                            reads=[src_buf, identb_b], writes=[trb[bank]],
                            inc=(j == 3), record=(j == 3), tag=tag)
                for j in range(4):
                    dc = g * 4 + j
                    dst_ap, dst_buf = dst_fn(dc)
                    EM.emit("vector",
                            lambda v, o=dst_ap, i_=trp[bank][:, j * 128:(j + 1) * 128]:
                            v.tensor_copy(o, i_),
                            reads=[trb[bank]], writes=[dst_buf],
                            inc=(j == 3), record=(j == 3), tag=tag)

        for s_g in range(8):
            par = s_g % 2
            EM.emit("sync",
                    lambda sy, o=x_t[par][:], i_=d_x[s_g * 128:(s_g + 1) * 128, :]:
                    sy.dma_start(out=o, in_=i_),
                    writes=[x_t_b[par]], inc=True, tag="A_ln")
            st = stn[par]
            ln_stats(st, x_t[par][:], x_t_b[par], "A_ln")
            ln_norm(st, ln1_t[par][:], ln1_t_b[par], x_t[par][:], x_t_b[par], "A_ln")
            if dbg and s_g == 0:
                EM.emit("sync", lambda sy: sy.dma_start(out=d_dbg["d_xt0"], in_=x_t[0][:]),
                        reads=[x_t_b[0]], inc=True, tag="dbg")
                EM.emit("gpsimd", lambda g: g.dma_start(out=d_dbg["d_ln1t0"], in_=ln1_t[0][:]),
                        reads=[ln1_t_b[0]], inc=True, tag="dbg")
            transpose_block(
                ptr, ptrb, ln1_t[par][:], ln1_t_b[par], 32,
                lambda dc, s_g=s_g: (ln1_fm[:, dc * T + s_g * 128: dc * T + (s_g + 1) * 128], ln1_fm_b),
                "A_tr")
        if dbg:
            EM.emit("sync", lambda sy: sy.dma_start(out=d_dbg["d_ln1fm"], in_=ln1_fm[:, 0:32 * 512]),
                    reads=[ln1_fm_b], inc=True, tag="dbg")
        cik_b = bufs(8, "cik")
        civ_b = bufs(8, "civ")
        cok_b = Buf("cok")
        cov_b = Buf("cov")
        for fc in list(range(8, 24)) + list(range(0, 8)):
            par = fc % 2
            EM.emit("gpsimd",
                    lambda g, o=wq_sl[par][:], i_=d_wq[fc * 128:(fc + 1) * 128, :]:
                    g.dma_start(out=o, in_=i_),
                    writes=[wq_sl_b[par]], inc=True, tag="A_qkv")
            if dbg and fc == 0:
                EM.emit("sync", lambda sy: sy.dma_start(out=d_dbg["d_wq0"], in_=wq_sl[0][:]),
                        reads=[wq_sl_b[0]], inc=True, tag="dbg")
            if fc < 8:
                dst_ap, dst_buf = q_fm, q_fm_b
                col0 = fc * T
            elif fc < 16:
                dst_ap, dst_buf = k_fm, k_fm_b
                col0 = (fc - 8) * T
            else:
                dst_ap, dst_buf = v_fm, v_fm_b
                col0 = (fc - 16) * T
            for h in range(2):
                bank = (fc * 2 + h) % 2
                for dc in range(32):
                    EM.emit("tensor",
                            lambda t, o=P[bank][:, 0:512],
                            l=wq_sl[par][:, dc * 128:(dc + 1) * 128],
                            r=ln1_fm[:, dc * T + h * 512: dc * T + (h + 1) * 512],
                            st_=(dc == 0), sp=(dc == 31):
                            t.matmul(o, l, r, start=st_, stop=sp, skip_group_check=True),
                            reads=[wq_sl_b[par], ln1_fm_b], writes=[Pb[bank]],
                            inc=(dc == 31), record=(dc == 31), tag="A_qkv")
                EM.emit("vector",
                        lambda v, o=dst_ap[:, col0 + h * 512: col0 + (h + 1) * 512],
                        i_=P[bank][:, 0:512], b=bqkv_sb[:, fc:fc + 1]:
                        v.tensor_scalar_add(o, i_, b),
                        reads=[Pb[bank], bqkv_b], writes=[dst_buf], inc=True, tag="A_qkv")
            if fc == 15:
                for i in range(8):
                    EM.emit("sync",
                            lambda sy, o=d_cik[i * 128:(i + 1) * 128, :], i_=k_fm[:, i * T:(i + 1) * T]:
                            sy.dma_start(out=o, in_=i_),
                            reads=[k_fm_b], writes=[cik_b[i]], inc=True, tag="A_cc")
                EM.emit("gpsimd",
                        lambda g: g.collective_compute(
                            "AllGather", OP.bypass,
                            replica_groups=[[0, 1, 2, 3, 4, 5, 6, 7]],
                            ins=[d_cik], outs=[d_cok]),
                        reads=cik_b, writes=[cok_b], inc=1, dram_only=True,
                        use_sem="cc", tag="A_cc")
            if fc == 23:
                for kt in range(8):
                    for g2 in range(2):
                        bank = (kt * 2 + g2) % 2
                        for j in range(4):
                            dhc = g2 * 4 + j
                            EM.emit("tensor",
                                    lambda t, o=ptr[bank][:, j * 128:(j + 1) * 128],
                                    i_=v_fm[:, dhc * T + kt * 128: dhc * T + (kt + 1) * 128], idn=identb[:]:
                                    t.transpose(o, i_, idn),
                                    reads=[v_fm_b, identb_b], writes=[ptrb[bank]],
                                    inc=(j == 3), record=(j == 3), tag="A_vtr")
                        EM.emit("vector",
                                lambda v, o=v_tm[:, kt * T + g2 * 512: kt * T + (g2 + 1) * 512],
                                i_=ptr[bank][:, 0:512]:
                                v.tensor_copy(o, i_),
                                reads=[ptrb[bank]], writes=[v_tm_b], inc=True, tag="A_vtr")
                for i in range(8):
                    EM.emit("sync",
                            lambda sy, o=d_civ[i * 128:(i + 1) * 128, :], i_=v_tm[:, i * T:(i + 1) * T]:
                            sy.dma_start(out=o, in_=i_),
                            reads=[v_tm_b], writes=[civ_b[i]], inc=True, tag="A_cc")
                EM.emit("gpsimd",
                        lambda g: g.collective_compute(
                            "AllGather", OP.bypass,
                            replica_groups=[[0, 1, 2, 3, 4, 5, 6, 7]],
                            ins=[d_civ], outs=[d_cov]),
                        reads=civ_b, writes=[cov_b], inc=1, dram_only=True,
                        use_sem="cc", tag="A_cc")

        if dbg:
            EM.emit("sync", lambda sy: sy.dma_start(out=d_dbg["d_qfm"], in_=q_fm[:]),
                    reads=[q_fm_b], inc=True, tag="dbg")
            EM.emit("sync", lambda sy: sy.dma_start(out=d_dbg["d_kfm"], in_=k_fm[:]),
                    reads=[k_fm_b], inc=True, tag="dbg")
            EM.emit("sync", lambda sy: sy.dma_start(out=d_dbg["d_vtm"], in_=v_tm[:]),
                    reads=[v_tm_b], inc=True, tag="dbg")

        EM.fence()
        free("wq_sl1", "wq_sl0", "ln1_fm", "ln1_t1", "ln1_t0", "x_t1", "x_t0",
             "v_tm", "v_fm", "k_fm")
        free("ptr1", "ptr0")
        pden = alloc("pden", [128, 512], F32, "psum")
        pden_b = Buf("pden")

        wo_bf = alloc("wo_bf", [128, 8 * D], BF16, side="right")
        wo_b = bufs(8, "wo")
        o_sb = alloc("o_sb", [128, 8 * T], BF16, side="right")
        o_sb_b = Buf("o_sb")
        bo_bc = alloc("bo_bc", [128, D], BF16, side="right")
        bo_b = Buf("bo")
        rden = alloc("rden", [128, 8], F32, side="right")
        rden_b = Buf("rden")
        k_sl = [alloc(f"k_sl{i}", [128, 8 * T], BF16) for i in range(2)]
        k_sl_b = [bufs(8, f"k_sl{i}_") for i in range(2)]
        v_sl = [alloc(f"v_sl{i}", [128, 8 * T], BF16) for i in range(2)]
        v_sl_b = [bufs(8, f"v_sl{i}_") for i in range(2)]
        p_t8 = alloc("p_t8", [128, 8 * 512], BF16)
        p_t8_b = bufs(8, "p_t8")

        for i in range(8):
            EM.emit("gpsimd",
                    lambda g, o=wo_bf[:, i * D:(i + 1) * D], i_=d_wo[i * 128:(i + 1) * 128, :]:
                    g.dma_start(out=o, in_=i_),
                    writes=[wo_b[i]], inc=True, tag="B")
        EM.emit("gpsimd", lambda g: g.dma_start(out=bo_bc[:], in_=d_bo),
                writes=[bo_b], inc=True, tag="B")

        for cb in range(8):
            par = cb % 2
            for i in range(8):
                EM.emit("sync",
                        lambda sy, o=k_sl[par][:, i * T:(i + 1) * T],
                        i_=d_cok[cb * DH + i * 128: cb * DH + (i + 1) * 128, :]:
                        sy.dma_start(out=o, in_=i_),
                        reads=[cok_b], writes=[k_sl_b[par][i]], inc=True, tag="B")
            for i in range(8):
                EM.emit("sync",
                        lambda sy, o=v_sl[par][:, i * T:(i + 1) * T],
                        i_=d_cov[cb * T + i * 128: cb * T + (i + 1) * 128, :]:
                        sy.dma_start(out=o, in_=i_),
                        reads=[cov_b], writes=[v_sl_b[par][i]], inc=True, tag="B")

            for qp in range(2):
                for kt in range(8):
                    bank = kt % 4
                    for dhc in range(8):
                        EM.emit("tensor",
                                lambda t, o=P[bank][:, 0:512],
                                l=k_sl[par][:, dhc * T + kt * 128: dhc * T + (kt + 1) * 128],
                                r=q_fm[:, dhc * T + qp * 512: dhc * T + (qp + 1) * 512],
                                st_=(dhc == 0), sp=(dhc == 7):
                                t.matmul(o, l, r, start=st_, stop=sp, skip_group_check=True),
                                reads=[k_sl_b[par][dhc], q_fm_b], writes=[Pb[bank]],
                                inc=(dhc == 7), record=False, tag="B_s")
                    EM.note_readers([k_sl_b[par][d_] for d_ in range(8)] + [q_fm_b], "tensor")
                    EM.emit("scalar",
                            lambda s, o=p_t8[:, kt * 512:(kt + 1) * 512], i_=P[bank][:, 0:512],
                            b=c_nb12[:, 0:1], sc=c_s32[:, 0:1]:
                            s.activation(o, i_, AF.Exp, bias=b, scale=sc),
                            reads=[Pb[bank], consts_b], writes=[p_t8_b[kt]], inc=True, tag="B_s")
                    for sub in range(4):
                        EM.emit("tensor",
                                lambda t, o=pden[:, qp * 4 + sub: qp * 4 + sub + 1],
                                l=p_t8[:, kt * 512 + sub * 128: kt * 512 + (sub + 1) * 128],
                                r=ones_bf[:],
                                st_=(cb == 0 and qp == 0 and kt == 0 and sub == 0),
                                sp=(cb == 7 and qp == 1 and kt == 7 and sub == 3):
                                t.matmul(o, l, r, start=st_, stop=sp, skip_group_check=True),
                                reads=[p_t8_b[kt], consts_b], writes=[pden_b],
                                inc=(cb == 7 and qp == 1 and kt == 7 and sub == 3),
                                record=False, tag="B_d")
                for qh in range(2):
                    qg = qp * 2 + qh
                    for kt in range(8):
                        for dhc in range(8):
                            ob = 2 + dhc // 2
                            EM.emit("tensor",
                                    lambda t, o=P[ob][:, (dhc % 2) * 256:(dhc % 2) * 256 + 256],
                                    l=v_sl[par][:, kt * T + dhc * 128: kt * T + (dhc + 1) * 128],
                                    r=p_t8[:, kt * 512 + qh * 256: kt * 512 + qh * 256 + 256],
                                    st_=(kt == 0 and dhc % 2 == 0),
                                    sp=(kt == 7 and dhc % 2 == 1):
                                    t.matmul(o, l, r, start=st_, stop=sp, skip_group_check=True),
                                    reads=[v_sl_b[par][kt], p_t8_b[kt]], writes=[Pb[ob]],
                                    inc=(dhc == 7 or (kt == 7 and dhc % 2 == 1)),
                                    record=False, tag="B_pv")
                    EM.note_readers([v_sl_b[par][k_] for k_ in range(8)]
                                    + [p_t8_b[k_] for k_ in range(8)] + [consts_b], "tensor")
                    for dhc in range(8):
                        ob = 2 + dhc // 2
                        osl = o_sb[:, dhc * T + qg * 256: dhc * T + (qg + 1) * 256]
                        psl = P[ob][:, (dhc % 2) * 256:(dhc % 2) * 256 + 256]
                        if cb == 0:
                            EM.emit("vector", lambda v, o=osl, i_=psl: v.tensor_copy(o, i_),
                                    reads=[Pb[ob]], writes=[o_sb_b], inc=True, tag="B_ev")
                        else:
                            EM.emit("vector", lambda v, o=osl, i_=psl: v.tensor_tensor(o, o, i_, OP.add),
                                    reads=[Pb[ob], o_sb_b], writes=[o_sb_b], inc=True, tag="B_ev")

        EM.emit("vector", lambda v: v.reciprocal(rden[:], pden[:, 0:8]),
                reads=[pden_b], writes=[rden_b], inc=True, tag="B")

        if dbg:
            EM.emit("sync", lambda sy: sy.dma_start(out=d_dbg["d_osb"], in_=o_sb[:]),
                    reads=[o_sb_b], inc=True, tag="dbg")
            EM.emit("sync", lambda sy: sy.dma_start(out=d_dbg["d_den"], in_=rden[:]),
                    reads=[rden_b], inc=True, tag="dbg")

        EM.fence(skip=("tensor",))
        free("p_t8", "v_sl1", "v_sl0", "k_sl1", "k_sl0", "q_fm")
        free("pden")

        a_sb = [alloc(f"a_sb{i}", [128, D], F32) for i in range(3)]
        a_sb_b = bufs(3, "a_sb")
        lnb = [alloc("lnb0", [128, D], F32)]
        lnb_b = bufs(1, "lnb")
        xr_t = [alloc(f"xr_t{i}", [128, D], F32) for i in range(2)]
        xr_t_b = bufs(2, "xr")
        x2_b = bufs(8, "x2")

        for sg in range(8):
            par = sg % 3
            spar = sg % 2
            EM.emit("sync",
                    lambda sy, o=xr_t[spar][:], i_=d_x[sg * 128:(sg + 1) * 128, :]:
                    sy.dma_start(out=o, in_=i_),
                    writes=[xr_t_b[spar]], inc=True, tag="C")
            for cc in range(8):
                bank = cc % 2
                for dhc in range(8):
                    EM.emit("tensor",
                            lambda t, o=P[bank][:, 0:512],
                            l=o_sb[:, dhc * T + sg * 128: dhc * T + (sg + 1) * 128],
                            r=wo_bf[:, dhc * D + cc * 512: dhc * D + (cc + 1) * 512],
                            st_=(dhc == 0), sp=(dhc == 7):
                            t.matmul(o, l, r, start=st_, stop=sp, skip_group_check=True),
                            reads=[o_sb_b, wo_b[dhc]], writes=[Pb[bank]],
                            inc=(dhc == 7), record=False, tag="C_mm")
                EM.note_readers([o_sb_b] + wo_b, "tensor")
                EM.emit("scalar",
                        lambda s, o=a_sb[par][:, cc * 512:(cc + 1) * 512],
                        i_=P[bank][:, 0:512], sc=rden[:, sg:sg + 1]:
                        s.activation(o, i_, AF_COPY, scale=sc),
                        reads=[Pb[bank], rden_b], writes=[a_sb_b[par]], inc=True, tag="C_mm")
            EM.emit("vector", lambda v, o=a_sb[par][:], b=bo_bc[:]:
                    v.tensor_tensor(o, o, b, OP.add),
                    reads=[a_sb_b[par], bo_b], writes=[a_sb_b[par]], inc=True, tag="C")
            if dbg and sg == 0:
                EM.emit("sync", lambda sy: sy.dma_start(out=d_dbg["d_a0"], in_=a_sb[0][:]),
                        reads=[a_sb_b[0]], inc=True, tag="dbg")
            st = stn[sg % 2]
            ln_stats(st, a_sb[par][:], a_sb_b[par], "C")
            ln_norm(st, lnb[0][:], lnb_b[0], a_sb[par][:], a_sb_b[par], "C")
            EM.emit("vector", lambda v, o=lnb[0][:], x_=xr_t[spar][:]:
                    v.tensor_tensor(o, o, x_, OP.add),
                    reads=[lnb_b[0], xr_t_b[spar]], writes=[lnb_b[0]], inc=True, tag="C")
            EM.emit("gpsimd",
                    lambda g, o=d_x2[sg * 128:(sg + 1) * 128, :], i_=lnb[0][:]:
                    g.dma_start(out=o, in_=i_),
                    reads=[lnb_b[0]], writes=[x2_b[sg]], inc=True, tag="C")

        EM.fence()
        free("xr_t1", "xr_t0", "lnb0", "a_sb2", "a_sb1", "a_sb0",
             "rden", "bo_bc", "o_sb", "wo_bf")

        ln3_fm = alloc("ln3_fm", [128, 32 * T], BF16, side="right")
        ln3_fm_b = Buf("ln3_fm")
        hsb = alloc("hsb", [128, FHC * T], BF16)
        hsb_b = Buf("hsb")
        x2t = [alloc(f"x2t{i}", [128, D], BF16) for i in range(2)]
        x2t_b = bufs(2, "x2t")
        ln3t = [alloc(f"ln3t{i}", [128, D], BF16) for i in range(2)]
        ln3t_b = bufs(2, "ln3t")
        ptr2 = [alloc(f"ptr2_{i}", [128, 1024], BF16, "psum") for i in range(2)]
        ptr2b = bufs(2, "ptr2")

        for sg in range(8):
            par = sg % 2
            EM.emit("sync",
                    lambda sy, o=x2t[par][:], i_=d_x2[sg * 128:(sg + 1) * 128, :]:
                    sy.dma_start(out=o, in_=i_),
                    reads=[x2_b[sg]], writes=[x2t_b[par]], inc=True, tag="Dp")
            st = stn[par]
            ln_stats(st, x2t[par][:], x2t_b[par], "Dp")
            ln_norm(st, ln3t[par][:], ln3t_b[par], x2t[par][:], x2t_b[par], "Dp")
            transpose_block(
                ptr2, ptr2b, ln3t[par][:], ln3t_b[par], 32,
                lambda dc, sg=sg: (ln3_fm[:, dc * T + sg * 128: dc * T + (sg + 1) * 128], ln3_fm_b),
                "Dp_tr")
        if dbg:
            EM.emit("sync", lambda sy: sy.dma_start(out=d_dbg["d_ln3"], in_=ln3_fm[:]),
                    reads=[ln3_fm_b], inc=True, tag="dbg")

        EM.fence()
        free("ptr2_1", "ptr2_0")
        free("ln3t1", "ln3t0", "x2t1", "x2t0")
        y0_b = {}
        ysb_b = bufs(8, "ysb")
        WSEG = 22 * 512

        for fh in range(2):
            if fh == 1:
                EM.fence()
            wg_sl = [alloc(f"wg_sl{fh}_{i}", [128, D], BF16) for i in range(2)]
            wg_sl_b = bufs(2, "wg_sl")
            wu_sl = [alloc(f"wu_sl{fh}_{i}", [128, D], BF16) for i in range(2)]
            wu_sl_b = bufs(2, "wu_sl")
            sg_t = [alloc(f"sg_t{fh}_{i}", [128, 512], F32) for i in range(2)]
            sg_t_b = bufs(2, "sg_t")
            for fc in range(FHC):
                fg = fh * FHC + fc
                par = fc % 2
                EM.emit("gpsimd",
                        lambda g, o=wg_sl[par][:], i_=d_wg[:, fg * D:(fg + 1) * D]:
                        g.dma_start(out=o, in_=i_),
                        writes=[wg_sl_b[par]], inc=True, tag="D_gu")
                EM.emit("gpsimd",
                        lambda g, o=wu_sl[par][:], i_=d_wu[:, fg * D:(fg + 1) * D]:
                        g.dma_start(out=o, in_=i_),
                        writes=[wu_sl_b[par]], inc=True, tag="D_gu")
                for grp in range(2):
                    gb, ub = grp, 2 + grp
                    for dc in range(32):
                        EM.emit("tensor",
                                lambda t, o=P[gb][:, 0:512],
                                l=wg_sl[par][:, dc * 128:(dc + 1) * 128],
                                r=ln3_fm[:, dc * T + grp * 512: dc * T + grp * 512 + 512],
                                st_=(dc == 0), sp=(dc == 31):
                                t.matmul(o, l, r, start=st_, stop=sp, skip_group_check=True),
                                reads=[wg_sl_b[par], ln3_fm_b], writes=[Pb[gb]],
                                inc=(dc == 31), record=False, tag="D_gu")
                    for dc in range(32):
                        EM.emit("tensor",
                                lambda t, o=P[ub][:, 0:512],
                                l=wu_sl[par][:, dc * 128:(dc + 1) * 128],
                                r=ln3_fm[:, dc * T + grp * 512: dc * T + grp * 512 + 512],
                                st_=(dc == 0), sp=(dc == 31):
                                t.matmul(o, l, r, start=st_, stop=sp, skip_group_check=True),
                                reads=[wu_sl_b[par], ln3_fm_b], writes=[Pb[ub]],
                                inc=(dc == 31), record=False, tag="D_gu")
                    EM.note_readers([wg_sl_b[par], wu_sl_b[par], ln3_fm_b], "tensor")
                    EM.emit("scalar",
                            lambda s, o=sg_t[grp][:], i_=P[gb][:, 0:512]:
                            s.activation(o, i_, AF.Silu),
                            reads=[Pb[gb]], writes=[sg_t_b[grp]], inc=True, tag="D_gu")
                    EM.emit("vector",
                            lambda v, o=hsb[:, fc * T + grp * 512: fc * T + grp * 512 + 512],
                            a=sg_t[grp][:], b=P[ub][:, 0:512]:
                            v.tensor_tensor(o, a, b, OP.mult),
                            reads=[sg_t_b[grp], Pb[ub]], writes=[hsb_b], inc=True, tag="D_gu")
            if dbg and fh == 0:
                EM.emit("sync", lambda sy: sy.dma_start(out=d_dbg["d_h0"], in_=hsb[:]),
                        reads=[hsb_b], inc=True, tag="dbg")
            EM.fence()
            free(f"sg_t{fh}_1", f"sg_t{fh}_0", f"wu_sl{fh}_1", f"wu_sl{fh}_0",
                 f"wg_sl{fh}_1", f"wg_sl{fh}_0")
            if fh == 0:
                wd_sl = [alloc(f"wd_sl0_{i}", [128, WSEG], BF16) for i in range(2)]
                wd_sl_b = bufs(2, "wd_sl")
                y0t = [alloc(f"y0t{i}", [128, 512], BF16) for i in range(4)]
                y0t_b = bufs(4, "y0t")
            else:
                free("ln3_fm")
                ysb = alloc("ysb", [128, 8 * D], BF16, side="right")
                wd_sl = [alloc(f"wd_sl1_{i}", [128, WSEG], BF16) for i in range(2)]
                wd_sl_b = bufs(2, "wd_sl")
                y0r = [alloc(f"y0r{i}", [128, 512], BF16) for i in range(4)]
                y0r_b = bufs(4, "y0r")
            y67 = [alloc(f"y67_{fh}_{i}", [128, 512], F32, "psum") for i in range(2)]
            PB8 = P + y67
            Pb8 = Pb + bufs(2, f"y67_{fh}_")

            def y0r_load(sub, cbk):
                par2 = sub % 4
                EM.emit("sync",
                        lambda sy, o=y0r[par2][:],
                        i_=d_y0[sub * 128:(sub + 1) * 128, cbk * 512:(cbk + 1) * 512]:
                        sy.dma_start(out=o, in_=i_),
                        reads=[y0_b[(sub, cbk)]], writes=[y0r_b[par2]], inc=True, tag="D_dn")

            for cbk in range(8):
                if fh == 1:
                    for sub in range(4):
                        y0r_load(sub, cbk)
                for h2 in range(2):
                    seg = 22 if h2 == 0 else 21
                    spar = (cbk * 2 + h2) % 2
                    col0 = (fh * FHC + h2 * 22) * 512
                    EM.emit("gpsimd",
                            lambda g, o=wd_sl[spar][:, 0:seg * 512],
                            i_=d_wd[cbk * 128:(cbk + 1) * 128, col0:col0 + seg * 512]:
                            g.dma_start(out=o, in_=i_),
                            writes=[wd_sl_b[spar]], inc=True, tag="D_dn")
                    for fcl in range(seg):
                        fc = h2 * 22 + fcl
                        for sub in range(8):
                            EM.emit("tensor",
                                    lambda t, o=PB8[sub][:, 0:512],
                                    l=hsb[:, fc * T + sub * 128: fc * T + (sub + 1) * 128],
                                    r=wd_sl[spar][:, fcl * 512:(fcl + 1) * 512],
                                    st_=(fc == 0), sp=(fc == FHC - 1):
                                    t.matmul(o, l, r, start=st_, stop=sp, skip_group_check=True),
                                    reads=[hsb_b, wd_sl_b[spar]], writes=[Pb8[sub]],
                                    inc=(sub == 7 or fc == FHC - 1), record=False, tag="D_dn")
                    EM.note_readers([hsb_b, wd_sl_b[spar]], "tensor")
                if fh == 0:
                    for sub in range(8):
                        yt = sub % 4
                        eng = "scalar" if sub % 2 == 0 else "vector"
                        if eng == "scalar":
                            EM.emit("scalar",
                                    lambda s, o=y0t[yt][:], i_=PB8[sub][:, 0:512]:
                                    s.activation(o, i_, AF_COPY),
                                    reads=[Pb8[sub]], writes=[y0t_b[yt]], inc=True, tag="D_dn")
                        else:
                            EM.emit("vector",
                                    lambda v, o=y0t[yt][:], i_=PB8[sub][:, 0:512]:
                                    v.tensor_copy(o, i_),
                                    reads=[Pb8[sub]], writes=[y0t_b[yt]], inc=True, tag="D_dn")
                        b = Buf(f"y0_{sub}_{cbk}")
                        y0_b[(sub, cbk)] = b
                        EM.emit("sync",
                                lambda sy, o=d_y0[sub * 128:(sub + 1) * 128, cbk * 512:(cbk + 1) * 512],
                                i_=y0t[yt][:]:
                                sy.dma_start(out=o, in_=i_),
                                reads=[y0t_b[yt]], writes=[b], inc=True, tag="D_dn")
                else:
                    for sub in range(8):
                        ysl = ysb[:, sub * D + cbk * 512: sub * D + (cbk + 1) * 512]
                        if sub % 2 == 0:
                            EM.emit("scalar",
                                    lambda s, o=ysl, i_=PB8[sub][:, 0:512]:
                                    s.activation(o, i_, AF_COPY),
                                    reads=[Pb8[sub]], writes=[ysb_b[sub]], inc=True, tag="D_dn")
                        else:
                            EM.emit("vector",
                                    lambda v, o=ysl, i_=PB8[sub][:, 0:512]:
                                    v.tensor_copy(o, i_),
                                    reads=[Pb8[sub]], writes=[ysb_b[sub]], inc=True, tag="D_dn")
                    for sub in range(8):
                        par2 = sub % 4
                        ysl = ysb[:, sub * D + cbk * 512: sub * D + (cbk + 1) * 512]
                        EM.emit("vector",
                                lambda v, o=ysl, a=y0r[par2][:]:
                                v.tensor_tensor(o, o, a, OP.add),
                                reads=[ysb_b[sub], y0r_b[par2]], writes=[ysb_b[sub]], inc=True, tag="D_dn")
                        if sub < 4:
                            y0r_load(sub + 4, cbk)
            free("y67_%d_1" % fh, "y67_%d_0" % fh)
            if fh == 0:
                free("y0t3", "y0t2", "y0t1", "y0t0", "wd_sl0_1", "wd_sl0_0")

        if dbg:
            EM.emit("sync", lambda sy: sy.dma_start(out=d_dbg["d_ysb"], in_=ysb[:]),
                    reads=ysb_b, inc=True, tag="dbg")

        EM.fence()
        free("y0r3", "y0r2", "y0r1", "y0r0", "wd_sl1_1", "wd_sl1_0", "hsb")
        x2f = [alloc(f"x2f{i}", [128, D], BF16) for i in range(4)]
        x2f_b = bufs(4, "x2f")
        outf = [alloc(f"outf{i}", [128, D], F32) for i in range(2)]
        outf_b = bufs(2, "outf")
        out_done = bufs(8, "out")

        def x2f_load(sub):
            EM.emit("sync",
                    lambda sy, o=x2f[sub % 4][:], i_=d_x2[sub * 128:(sub + 1) * 128, :]:
                    sy.dma_start(out=o, in_=i_),
                    reads=[x2_b[sub]], writes=[x2f_b[sub % 4]], inc=True, tag="E")

        for sub in range(4):
            x2f_load(sub)
        for sub in range(8):
            par = sub % 4
            st = stn[sub % 2]
            ysl = ysb[:, sub * D:(sub + 1) * D]
            ln_stats(st, ysl, ysb_b[sub], "E")
            ln_norm(st, outf[sub % 2][:], outf_b[sub % 2], ysl, ysb_b[sub], "E")
            EM.emit("vector", lambda v, o=outf[sub % 2][:], x_=x2f[par][:]:
                    v.tensor_tensor(o, o, x_, OP.add),
                    reads=[outf_b[sub % 2], x2f_b[par]], writes=[outf_b[sub % 2]], inc=True, tag="E")
            if sub < 4:
                x2f_load(sub + 4)
            EM.emit("sync",
                    lambda sy, o=d_out[sub * 128:(sub + 1) * 128, :], i_=outf[sub % 2][:]:
                    sy.dma_start(out=o, in_=i_),
                    reads=[outf_b[sub % 2]], writes=[out_done[sub]], inc=True, tag="E")
        EM.emit("sync", None, reads=out_done, tag="E")

        blk = outer.enter_context(nc.Block())
        EM.replay(nc, blk, sems)
    return nc


def _prep(L0_attn_x, L0_attn_Wqkv, L0_attn_bqkv, L0_attn_Wo, L0_attn_bo,
          L0_ffn_W_gate, L0_ffn_W_up, L0_ffn_W_down):
    x = np.ascontiguousarray(np.asarray(L0_attn_x, dtype=np.float32).reshape(S, D))
    Wqkv = np.asarray(L0_attn_Wqkv, dtype=np.float32)
    wq_t = np.ascontiguousarray(
        Wqkv.reshape(32, 128, 24, 128).transpose(2, 1, 0, 3)).reshape(24 * 128, D)
    bqkv_t = np.ascontiguousarray(
        np.asarray(L0_attn_bqkv, dtype=np.float32).reshape(24, 128).T)
    wo = np.ascontiguousarray(np.asarray(L0_attn_Wo, dtype=np.float32))
    bo_b = np.ascontiguousarray(
        np.broadcast_to(np.asarray(L0_attn_bo, dtype=np.float32), (128, D)))
    wg_t = np.ascontiguousarray(
        np.asarray(L0_ffn_W_gate, dtype=np.float32)
        .reshape(NFC, 128, 32, 128).transpose(3, 0, 2, 1)).reshape(128, NFC * D)
    wu_t = np.ascontiguousarray(
        np.asarray(L0_ffn_W_up, dtype=np.float32)
        .reshape(NFC, 128, 32, 128).transpose(3, 0, 2, 1)).reshape(128, NFC * D)
    wd_t = np.ascontiguousarray(
        np.asarray(L0_ffn_W_down, dtype=np.float32)
        .reshape(8, 512, NFC, 128).transpose(0, 3, 2, 1)).reshape(8 * 128, NFC * 512)
    ident = np.eye(128, dtype=np.float32)
    common = {"wq": wq_t, "bqkv": bqkv_t, "wo": wo, "bo": bo_b,
              "wg": wg_t, "wu": wu_t, "wd": wd_t, "ident": ident}
    return x, common


def kernel(L0_attn_x, L0_attn_Wqkv, L0_attn_bqkv, L0_attn_Wo, L0_attn_bo,
           L0_ffn_W_gate, L0_ffn_W_up, L0_ffn_W_down, _dbg=False):
    x, common = _prep(L0_attn_x, L0_attn_Wqkv, L0_attn_bqkv, L0_attn_Wo,
                      L0_attn_bo, L0_ffn_W_gate, L0_ffn_W_up, L0_ffn_W_down)
    nc = build_program(dbg=_dbg)
    in_maps = [dict(common, x=np.ascontiguousarray(x[c * T:(c + 1) * T]))
               for c in range(8)]
    res = run_bass_kernel_spmd(nc, in_maps, core_ids=list(range(8)))
    out = np.concatenate([res.results[c]["out"] for c in range(8)], axis=0)
    out = out.reshape(1, S, D).astype(np.float32)
    if _dbg:
        return out, res
    return out


# version 17
# speedup vs baseline: 1.3086x; 1.0074x over previous
import numpy as np
from contextlib import ExitStack

import concourse.bass as bass
import concourse.mybir as mybir
from concourse.bass_utils import run_bass_kernel_spmd

F32 = mybir.dt.float32
BF16 = mybir.dt.bfloat16
AF = mybir.ActivationFunctionType
OP = mybir.AluOpType
AF_COPY = getattr(AF, "Copy", None)
if AF_COPY is None:
    AF_COPY = getattr(AF, "Identity")

ENGINES = ("tensor", "vector", "scalar", "sync", "gpsimd")
ENG_SEM = {"tensor": "mt", "vector": "mv", "scalar": "ms", "sync": "dsy", "gpsimd": "dgp"}
ALL_SEMS = ("mt", "mv", "ms", "dsy", "dgp", "cc")
COMPUTE = ("tensor", "vector", "scalar")


class Buf:
    __slots__ = ("name", "ready", "readers")

    def __init__(self, name=""):
        self.name = name
        self.ready = None
        self.readers = {}


def bufs(n, name):
    return [Buf(f"{name}{i}") for i in range(n)]


class Em:
    def __init__(self):
        self.ops = {e: [] for e in ENGINES}
        self.cnt = {s: 0 for s in ALL_SEMS}
        self.cnt_sbuf = {s: 0 for s in ALL_SEMS}
        self.waited = {e: {} for e in ENGINES}
        self.pfence = {e: None for e in ENGINES}
        self.last_writes = {e: frozenset() for e in ENGINES}

    def emit(self, eng, build, reads=(), writes=(), inc=False, record=True,
             dram_only=False, use_sem=None, tag=""):
        own = ENG_SEM[eng]
        inc_sem = use_sem or own
        if eng == "vector" and any(id(b) in self.last_writes[eng] for b in reads):
            self.ops[eng].append(((), lambda v: v.engine_nop(), None, "nopg"))
            self.last_writes[eng] = frozenset()
        waits = {}

        def need(sem, val, why):
            if sem == own and eng in COMPUTE:
                return
            if val is None:
                raise RuntimeError(f"un-inc'd dependency: {why} [{tag}] on {eng}")
            if self.waited[eng].get(sem, 0) >= val:
                return
            if waits.get(sem, 0) < val:
                waits[sem] = val

        if eng in ("sync", "gpsimd"):
            need(own, self.cnt[own], "dma-serialize")
        pf = self.pfence[eng]
        if pf:
            self.pfence[eng] = None
            for s, v in pf.items():
                need(s, v, "fence")
        for b in reads:
            r = b.ready
            if r is None:
                continue
            if r[0] == "eng":
                if r[1] != eng:
                    raise RuntimeError(f"cross-engine read of un-inc'd {b.name} [{tag}]")
            else:
                need(r[0], r[1], f"read {b.name}")
        for b in writes:
            for s, v in b.readers.items():
                if s == own and eng in COMPUTE:
                    continue
                need(s, v, f"WAR {b.name}")
            r = b.ready
            if r is not None:
                if r[0] == "eng":
                    if r[1] != eng:
                        raise RuntimeError(f"WAW cross-engine un-inc'd {b.name} [{tag}]")
                else:
                    need(r[0], r[1], f"WAW {b.name}")
        increc = None
        if inc:
            amt = (16 if eng in ("sync", "gpsimd") else 1) if inc is True else int(inc)
            self.cnt[inc_sem] += amt
            if not dram_only:
                self.cnt_sbuf[inc_sem] = self.cnt[inc_sem]
            increc = (inc_sem, amt)
        self.ops[eng].append((tuple(sorted(waits.items())), build, increc, tag))
        self.last_writes[eng] = frozenset(id(b) for b in writes)
        for s, v in waits.items():
            self.waited[eng][s] = v
        mark = (inc_sem, self.cnt[inc_sem]) if inc else ("eng", eng)
        for b in writes:
            b.ready = mark
            b.readers = {}
        if record:
            for b in reads:
                b.readers[own] = self.cnt[own] if (inc and inc_sem == own) else None

    def note_readers(self, rbufs, eng):
        own = ENG_SEM[eng]
        for b in rbufs:
            b.readers[own] = self.cnt[own]

    def fence(self, skip=()):
        snap = {s: v for s, v in self.cnt_sbuf.items() if v > 0}
        for e in ENGINES:
            if e in skip:
                continue
            pf = self.pfence[e] or {}
            keys = set(pf) | set(snap)
            self.pfence[e] = {s: max(pf.get(s, 0), snap.get(s, 0)) for s in keys}

    def replay(self, nc, blk, sems):
        for eng in ENGINES:
            ops = self.ops[eng]

            def make(ops=ops):
                def body(e):
                    for waits, build, increc, tag in ops:
                        for s, v in waits:
                            e.wait_ge(sems[s], v)
                        ins = build(e) if build is not None else None
                        if increc is not None:
                            ins.then_inc(sems[increc[0]], increc[1])
                return body
            getattr(blk, eng)(make())


T = 1024
S = 8192
D = 4096
DH = 1024
F = 11008
NT = 8
NFC = 86
FHC = 43
C_SHIFT = -12.0
SCALE = 1.0 / 32.0


def build_program(dbg=False):
    nc = bass.Bass("TRN2", target_bir_lowering=False)

    d_x = nc.dram_tensor("x", [T, D], F32, kind="ExternalInput").ap()
    d_wq = nc.dram_tensor("wq", [24 * 128, D], F32, kind="ExternalInput").ap()
    d_bqkv = nc.dram_tensor("bqkv", [128, 24], F32, kind="ExternalInput").ap()
    d_wo = nc.dram_tensor("wo", [DH, D], F32, kind="ExternalInput").ap()
    d_bo = nc.dram_tensor("bo", [128, D], F32, kind="ExternalInput").ap()
    d_wg = nc.dram_tensor("wg", [128, NFC * D], F32, kind="ExternalInput").ap()
    d_wu = nc.dram_tensor("wu", [128, NFC * D], F32, kind="ExternalInput").ap()
    d_wd = nc.dram_tensor("wd", [8 * 128, NFC * 512], F32, kind="ExternalInput").ap()
    d_id = nc.dram_tensor("ident", [128, 128], F32, kind="ExternalInput").ap()
    d_out = nc.dram_tensor("out", [T, D], F32, kind="ExternalOutput").ap()

    xk = "ExternalOutput" if dbg else "Internal"
    d_x2 = nc.dram_tensor("x2d", [T, D], BF16, kind=xk).ap()
    d_y0 = nc.dram_tensor("y0d", [T, D], BF16, kind=xk).ap()
    d_cik = nc.dram_tensor("cc_in_k", [DH, T], BF16, kind="Internal").ap()
    d_civ = nc.dram_tensor("cc_in_v", [T, DH], BF16, kind="Internal").ap()
    d_cok = nc.dram_tensor("cc_out_k", [8 * DH, T], BF16, addr_space="Shared", kind="Internal").ap()
    d_cov = nc.dram_tensor("cc_out_v", [S, DH], BF16, addr_space="Shared", kind="Internal").ap()
    if dbg:
        d_dbg = {
            "d_qfm": nc.dram_tensor("d_qfm", [128, 8 * T], BF16, kind="ExternalOutput").ap(),
            "d_kfm": nc.dram_tensor("d_kfm", [128, 8 * T], BF16, kind="ExternalOutput").ap(),
            "d_vtm": nc.dram_tensor("d_vtm", [128, 8 * T], BF16, kind="ExternalOutput").ap(),
            "d_osb": nc.dram_tensor("d_osb", [128, 8 * T], BF16, kind="ExternalOutput").ap(),
            "d_den": nc.dram_tensor("d_den", [128, 8], F32, kind="ExternalOutput").ap(),
            "d_a0": nc.dram_tensor("d_a0", [128, D], F32, kind="ExternalOutput").ap(),
            "d_h0": nc.dram_tensor("d_h0", [128, FHC * T], BF16, kind="ExternalOutput").ap(),
            "d_ln3": nc.dram_tensor("d_ln3", [128, 32 * T], BF16, kind="ExternalOutput").ap(),
            "d_ysb": nc.dram_tensor("d_ysb", [128, 8 * D], BF16, kind="ExternalOutput").ap(),
            "d_xt0": nc.dram_tensor("d_xt0", [128, D], F32, kind="ExternalOutput").ap(),
            "d_ln1t0": nc.dram_tensor("d_ln1t0", [128, D], F32, kind="ExternalOutput").ap(),
            "d_ln1fm": nc.dram_tensor("d_ln1fm", [128, 32 * 512], BF16, kind="ExternalOutput").ap(),
            "d_wq0": nc.dram_tensor("d_wq0", [128, D], BF16, kind="ExternalOutput").ap(),
        }

    EM = Em()
    outer = ExitStack()
    live = {}

    def alloc(name, shape, dtype, kind="sbuf", side="left"):
        if kind == "sbuf":
            cm = nc.sbuf_tensor(name, shape, dtype, side=side)
        else:
            cm = nc.psum_tensor(name, shape, dtype)
        ap = cm.__enter__()
        live[name] = cm
        return ap

    def free(*names):
        for n in names:
            live.pop(n).__exit__(None, None, None)

    with outer:
        sems = {s: outer.enter_context(nc.semaphore(s)) for s in ALL_SEMS}

        P = [alloc(f"p{i}", [128, 512], F32, "psum") for i in range(6)]
        Pb = bufs(6, "psum")

        ident = alloc("ident_sb", [128, 128], F32)
        ident_b = Buf("ident")
        c_eps = alloc("c_eps", [128, 1], F32)
        c_s32 = alloc("c_s32", [128, 1], F32)
        c_nb12 = alloc("c_nb12", [128, 1], F32)
        c_i4096 = alloc("c_i4096", [128, 1], F32)
        ones_bf = alloc("ones_bf", [128, 1], BF16)
        consts_b = Buf("consts")
        bqkv_sb = alloc("bqkv_sb", [128, 24], F32)
        bqkv_b = Buf("bqkv")

        scr_bf = alloc("scr_bf", [128, D], BF16)
        scr_b = Buf("scr")
        stn = []
        for i in range(2):
            stn.append({
                "sums": alloc(f"sums_{i}", [128, 2], F32), "sumsb": Buf(f"sums_{i}"),
                "mean": alloc(f"mean_{i}", [128, 1], F32), "meanb": Buf(f"mean_{i}"),
                "msq": alloc(f"msq_{i}", [128, 1], F32), "msqb": Buf(f"msq_{i}"),
                "veps": alloc(f"veps_{i}", [128, 1], F32), "vepsb": Buf(f"veps_{i}"),
                "std": alloc(f"std_{i}", [128, 1], F32), "stdb": Buf(f"std_{i}"),
                "rcp": alloc(f"rcp_{i}", [128, 1], F32), "rcpb": Buf(f"rcp_{i}"),
            })

        def ln_stats(st, src_ap, src_buf, tag):
            EM.emit("vector", lambda v, o=st["sums"][:, 0:1], i_=src_ap:
                    v.tensor_reduce(o, i_, mybir.AxisListType.X, OP.add),
                    reads=[src_buf], writes=[st["sumsb"]], inc=True, tag=tag)
            EM.emit("scalar", lambda s, o=scr_bf[:], i_=src_ap, a=st["sums"][:, 1:2]:
                    s.activation(o, i_, AF.Square, accum_out=a),
                    reads=[src_buf], writes=[scr_b, st["sumsb"]], inc=True, tag=tag)
            EM.emit("scalar", lambda s, o=st["mean"][:], i_=st["sums"][:, 0:1], sc=c_i4096[:, 0:1]:
                    s.activation(o, i_, AF_COPY, scale=sc),
                    reads=[st["sumsb"], consts_b], writes=[st["meanb"]], inc=True, tag=tag)
            EM.emit("scalar", lambda s, o=st["msq"][:], i_=st["mean"][:]:
                    s.activation(o, i_, AF.Square),
                    reads=[st["meanb"]], writes=[st["msqb"]], inc=True, tag=tag)
            EM.emit("vector", lambda v, o=st["veps"][:], i_=st["sums"][:, 1:2], m2=st["msq"][:, 0:1]:
                    v.tensor_scalar(o, i_, 1.0 / D, m2, OP.mult, OP.subtract),
                    reads=[st["sumsb"], st["msqb"]], writes=[st["vepsb"]], inc=True, tag=tag)
            EM.emit("scalar", lambda s, o=st["std"][:], i_=st["veps"][:], b=c_eps[:, 0:1]:
                    s.activation(o, i_, AF.Sqrt, bias=b),
                    reads=[st["vepsb"], consts_b], writes=[st["stdb"]], inc=True, tag=tag)
            EM.emit("vector", lambda v, o=st["rcp"][:], i_=st["std"][:]: v.reciprocal(o, i_),
                    reads=[st["stdb"]], writes=[st["rcpb"]], inc=True, tag=tag)

        def ln_norm(st, out_ap, out_buf, src_ap, src_buf, tag):
            EM.emit("vector",
                    lambda v, o=out_ap, i_=src_ap, b=st["mean"][:, 0:1], sc=st["rcp"][:, 0:1]:
                    v.tensor_scalar(o, i_, b, sc, OP.subtract, OP.mult),
                    reads=[src_buf, st["meanb"], st["rcpb"]], writes=[out_buf], inc=True, tag=tag)

        EM.emit("vector", lambda v: v.memset(c_eps[:], 1e-5), writes=[consts_b], tag="init")
        EM.emit("vector", lambda v: v.memset(c_s32[:], SCALE), writes=[consts_b], tag="init")
        EM.emit("vector", lambda v: v.memset(c_nb12[:], C_SHIFT), writes=[consts_b], tag="init")
        EM.emit("vector", lambda v: v.memset(c_i4096[:], 1.0 / D), writes=[consts_b], tag="init")
        EM.emit("vector", lambda v: v.memset(ones_bf[:], 1.0), writes=[consts_b], inc=True, tag="init")
        EM.emit("sync", lambda sy: sy.dma_start(out=ident[:], in_=d_id),
                writes=[ident_b], inc=True, tag="init")
        identb = alloc("identb_sb", [128, 128], BF16)
        identb_b = Buf("identb")
        EM.emit("gpsimd", lambda g: g.dma_start(out=identb[:], in_=d_id),
                writes=[identb_b], inc=True, tag="init")
        EM.emit("sync", lambda sy: sy.dma_start(out=bqkv_sb[:], in_=d_bqkv),
                writes=[bqkv_b], inc=True, tag="init")

        ptr = [alloc(f"ptr{i}", [128, 1024], BF16, "psum") for i in range(2)]
        ptrb = bufs(2, "ptr")
        q_fm = alloc("q_fm", [128, 8 * T], BF16)
        q_fm_b = Buf("q_fm")
        k_fm = alloc("k_fm", [128, 8 * T], BF16)
        k_fm_b = Buf("k_fm")
        v_fm = alloc("v_fm", [128, 8 * T], BF16)
        v_fm_b = Buf("v_fm")
        v_tm = alloc("v_tm", [128, 8 * T], BF16)
        v_tm_b = Buf("v_tm")
        x_t = [alloc(f"x_t{i}", [128, D], F32) for i in range(2)]
        x_t_b = bufs(2, "x_t")
        ln1_t = [alloc(f"ln1_t{i}", [128, D], BF16) for i in range(2)]
        ln1_t_b = bufs(2, "ln1_t")
        ln1_fm = alloc("ln1_fm", [128, 32 * T], BF16)
        ln1_fm_b = Buf("ln1_fm")
        wq_sl = [alloc(f"wq_sl{i}", [128, D], BF16) for i in range(2)]
        wq_sl_b = bufs(2, "wq_sl")

        def transpose_block(trp, trb, src_ap, src_buf, n_dc, dst_fn, tag):
            for g in range(n_dc // 4):
                bank = g % 2
                for j in range(4):
                    dc = g * 4 + j
                    EM.emit("tensor",
                            lambda t, o=trp[bank][:, j * 128:(j + 1) * 128],
                            i_=src_ap[:, dc * 128:(dc + 1) * 128], idn=identb[:]:
                            t.transpose(o, i_, idn),
                            reads=[src_buf, identb_b], writes=[trb[bank]],
                            inc=(j == 3), record=(j == 3), tag=tag)
                for j in range(4):
                    dc = g * 4 + j
                    dst_ap, dst_buf = dst_fn(dc)
                    EM.emit("vector",
                            lambda v, o=dst_ap, i_=trp[bank][:, j * 128:(j + 1) * 128]:
                            v.tensor_copy(o, i_),
                            reads=[trb[bank]], writes=[dst_buf],
                            inc=(j == 3), record=(j == 3), tag=tag)

        for s_g in range(8):
            par = s_g % 2
            EM.emit("sync",
                    lambda sy, o=x_t[par][:], i_=d_x[s_g * 128:(s_g + 1) * 128, :]:
                    sy.dma_start(out=o, in_=i_),
                    writes=[x_t_b[par]], inc=True, tag="A_ln")
            st = stn[par]
            ln_stats(st, x_t[par][:], x_t_b[par], "A_ln")
            ln_norm(st, ln1_t[par][:], ln1_t_b[par], x_t[par][:], x_t_b[par], "A_ln")
            if dbg and s_g == 0:
                EM.emit("sync", lambda sy: sy.dma_start(out=d_dbg["d_xt0"], in_=x_t[0][:]),
                        reads=[x_t_b[0]], inc=True, tag="dbg")
                EM.emit("gpsimd", lambda g: g.dma_start(out=d_dbg["d_ln1t0"], in_=ln1_t[0][:]),
                        reads=[ln1_t_b[0]], inc=True, tag="dbg")
            transpose_block(
                ptr, ptrb, ln1_t[par][:], ln1_t_b[par], 32,
                lambda dc, s_g=s_g: (ln1_fm[:, dc * T + s_g * 128: dc * T + (s_g + 1) * 128], ln1_fm_b),
                "A_tr")
        if dbg:
            EM.emit("sync", lambda sy: sy.dma_start(out=d_dbg["d_ln1fm"], in_=ln1_fm[:, 0:32 * 512]),
                    reads=[ln1_fm_b], inc=True, tag="dbg")
        cik_b = bufs(8, "cik")
        civ_b = bufs(8, "civ")
        cok_b = Buf("cok")
        cov_b = Buf("cov")
        for fc in list(range(8, 24)) + list(range(0, 8)):
            par = fc % 2
            EM.emit("gpsimd",
                    lambda g, o=wq_sl[par][:], i_=d_wq[fc * 128:(fc + 1) * 128, :]:
                    g.dma_start(out=o, in_=i_),
                    writes=[wq_sl_b[par]], inc=True, tag="A_qkv")
            if dbg and fc == 0:
                EM.emit("sync", lambda sy: sy.dma_start(out=d_dbg["d_wq0"], in_=wq_sl[0][:]),
                        reads=[wq_sl_b[0]], inc=True, tag="dbg")
            if fc < 8:
                dst_ap, dst_buf = q_fm, q_fm_b
                col0 = fc * T
            elif fc < 16:
                dst_ap, dst_buf = k_fm, k_fm_b
                col0 = (fc - 8) * T
            else:
                dst_ap, dst_buf = v_fm, v_fm_b
                col0 = (fc - 16) * T
            for h in range(2):
                bank = (fc * 2 + h) % 2
                for dc in range(32):
                    EM.emit("tensor",
                            lambda t, o=P[bank][:, 0:512],
                            l=wq_sl[par][:, dc * 128:(dc + 1) * 128],
                            r=ln1_fm[:, dc * T + h * 512: dc * T + (h + 1) * 512],
                            st_=(dc == 0), sp=(dc == 31):
                            t.matmul(o, l, r, start=st_, stop=sp, skip_group_check=True),
                            reads=[wq_sl_b[par], ln1_fm_b], writes=[Pb[bank]],
                            inc=(dc == 31), record=(dc == 31), tag="A_qkv")
                EM.emit("vector",
                        lambda v, o=dst_ap[:, col0 + h * 512: col0 + (h + 1) * 512],
                        i_=P[bank][:, 0:512], b=bqkv_sb[:, fc:fc + 1]:
                        v.tensor_scalar_add(o, i_, b),
                        reads=[Pb[bank], bqkv_b], writes=[dst_buf], inc=True, tag="A_qkv")
            if fc == 15:
                for i in range(8):
                    EM.emit("sync",
                            lambda sy, o=d_cik[i * 128:(i + 1) * 128, :], i_=k_fm[:, i * T:(i + 1) * T]:
                            sy.dma_start(out=o, in_=i_),
                            reads=[k_fm_b], writes=[cik_b[i]], inc=True, tag="A_cc")
                EM.emit("gpsimd",
                        lambda g: g.collective_compute(
                            "AllGather", OP.bypass,
                            replica_groups=[[0, 1, 2, 3, 4, 5, 6, 7]],
                            ins=[d_cik], outs=[d_cok]),
                        reads=cik_b, writes=[cok_b], inc=1, dram_only=True,
                        use_sem="cc", tag="A_cc")
            if fc == 23:
                for kt in range(8):
                    for g2 in range(2):
                        bank = (kt * 2 + g2) % 2
                        for j in range(4):
                            dhc = g2 * 4 + j
                            EM.emit("tensor",
                                    lambda t, o=ptr[bank][:, j * 128:(j + 1) * 128],
                                    i_=v_fm[:, dhc * T + kt * 128: dhc * T + (kt + 1) * 128], idn=identb[:]:
                                    t.transpose(o, i_, idn),
                                    reads=[v_fm_b, identb_b], writes=[ptrb[bank]],
                                    inc=(j == 3), record=(j == 3), tag="A_vtr")
                        EM.emit("vector",
                                lambda v, o=v_tm[:, kt * T + g2 * 512: kt * T + (g2 + 1) * 512],
                                i_=ptr[bank][:, 0:512]:
                                v.tensor_copy(o, i_),
                                reads=[ptrb[bank]], writes=[v_tm_b], inc=True, tag="A_vtr")
                for i in range(8):
                    EM.emit("sync",
                            lambda sy, o=d_civ[i * 128:(i + 1) * 128, :], i_=v_tm[:, i * T:(i + 1) * T]:
                            sy.dma_start(out=o, in_=i_),
                            reads=[v_tm_b], writes=[civ_b[i]], inc=True, tag="A_cc")
                EM.emit("gpsimd",
                        lambda g: g.collective_compute(
                            "AllGather", OP.bypass,
                            replica_groups=[[0, 1, 2, 3, 4, 5, 6, 7]],
                            ins=[d_civ], outs=[d_cov]),
                        reads=civ_b, writes=[cov_b], inc=1, dram_only=True,
                        use_sem="cc", tag="A_cc")

        if dbg:
            EM.emit("sync", lambda sy: sy.dma_start(out=d_dbg["d_qfm"], in_=q_fm[:]),
                    reads=[q_fm_b], inc=True, tag="dbg")
            EM.emit("sync", lambda sy: sy.dma_start(out=d_dbg["d_kfm"], in_=k_fm[:]),
                    reads=[k_fm_b], inc=True, tag="dbg")
            EM.emit("sync", lambda sy: sy.dma_start(out=d_dbg["d_vtm"], in_=v_tm[:]),
                    reads=[v_tm_b], inc=True, tag="dbg")

        EM.fence()
        free("wq_sl1", "wq_sl0", "ln1_fm", "ln1_t1", "ln1_t0", "x_t1", "x_t0",
             "v_tm", "v_fm", "k_fm")
        free("ptr1", "ptr0")
        pden = alloc("pden", [128, 512], F32, "psum")
        pden_b = Buf("pden")

        wo_bf = alloc("wo_bf", [128, 8 * D], BF16, side="right")
        wo_b = bufs(8, "wo")
        o_sb = alloc("o_sb", [128, 8 * T], BF16, side="right")
        o_sb_b = Buf("o_sb")
        bo_bc = alloc("bo_bc", [128, D], BF16, side="right")
        bo_b = Buf("bo")
        rden = alloc("rden", [128, 8], F32, side="right")
        rden_b = Buf("rden")
        k_sl = [alloc(f"k_sl{i}", [128, 8 * T], BF16) for i in range(2)]
        k_sl_b = [bufs(8, f"k_sl{i}_") for i in range(2)]
        v_sl = [alloc(f"v_sl{i}", [128, 8 * T], BF16) for i in range(2)]
        v_sl_b = [bufs(8, f"v_sl{i}_") for i in range(2)]
        p_t8 = alloc("p_t8", [128, 8 * 512], BF16)
        p_t8_b = bufs(8, "p_t8")

        for i in range(8):
            EM.emit("gpsimd",
                    lambda g, o=wo_bf[:, i * D:(i + 1) * D], i_=d_wo[i * 128:(i + 1) * 128, :]:
                    g.dma_start(out=o, in_=i_),
                    writes=[wo_b[i]], inc=True, tag="B")
        EM.emit("gpsimd", lambda g: g.dma_start(out=bo_bc[:], in_=d_bo),
                writes=[bo_b], inc=True, tag="B")

        for cb in range(8):
            par = cb % 2
            for i in range(8):
                EM.emit("sync",
                        lambda sy, o=k_sl[par][:, i * T:(i + 1) * T],
                        i_=d_cok[cb * DH + i * 128: cb * DH + (i + 1) * 128, :]:
                        sy.dma_start(out=o, in_=i_),
                        reads=[cok_b], writes=[k_sl_b[par][i]], inc=True, tag="B")
            for i in range(8):
                EM.emit("sync",
                        lambda sy, o=v_sl[par][:, i * T:(i + 1) * T],
                        i_=d_cov[cb * T + i * 128: cb * T + (i + 1) * 128, :]:
                        sy.dma_start(out=o, in_=i_),
                        reads=[cov_b], writes=[v_sl_b[par][i]], inc=True, tag="B")

            for qp in range(2):
                for kt in range(8):
                    bank = kt % 4
                    for dhc in range(8):
                        EM.emit("tensor",
                                lambda t, o=P[bank][:, 0:512],
                                l=k_sl[par][:, dhc * T + kt * 128: dhc * T + (kt + 1) * 128],
                                r=q_fm[:, dhc * T + qp * 512: dhc * T + (qp + 1) * 512],
                                st_=(dhc == 0), sp=(dhc == 7):
                                t.matmul(o, l, r, start=st_, stop=sp, skip_group_check=True),
                                reads=[k_sl_b[par][dhc], q_fm_b], writes=[Pb[bank]],
                                inc=(dhc == 7), record=False, tag="B_s")
                    EM.note_readers([k_sl_b[par][d_] for d_ in range(8)] + [q_fm_b], "tensor")
                    EM.emit("scalar",
                            lambda s, o=p_t8[:, kt * 512:(kt + 1) * 512], i_=P[bank][:, 0:512],
                            b=c_nb12[:, 0:1], sc=c_s32[:, 0:1]:
                            s.activation(o, i_, AF.Exp, bias=b, scale=sc),
                            reads=[Pb[bank], consts_b], writes=[p_t8_b[kt]], inc=True, tag="B_s")
                    for sub in range(4):
                        EM.emit("tensor",
                                lambda t, o=pden[:, qp * 4 + sub: qp * 4 + sub + 1],
                                l=p_t8[:, kt * 512 + sub * 128: kt * 512 + (sub + 1) * 128],
                                r=ones_bf[:],
                                st_=(cb == 0 and qp == 0 and kt == 0 and sub == 0),
                                sp=(cb == 7 and qp == 1 and kt == 7 and sub == 3):
                                t.matmul(o, l, r, start=st_, stop=sp, skip_group_check=True),
                                reads=[p_t8_b[kt], consts_b], writes=[pden_b],
                                inc=(cb == 7 and qp == 1 and kt == 7 and sub == 3),
                                record=False, tag="B_d")
                for qh in range(2):
                    qg = qp * 2 + qh
                    for kt in range(8):
                        for dhc in range(8):
                            ob = 2 + dhc // 2
                            EM.emit("tensor",
                                    lambda t, o=P[ob][:, (dhc % 2) * 256:(dhc % 2) * 256 + 256],
                                    l=v_sl[par][:, kt * T + dhc * 128: kt * T + (dhc + 1) * 128],
                                    r=p_t8[:, kt * 512 + qh * 256: kt * 512 + qh * 256 + 256],
                                    st_=(kt == 0 and dhc % 2 == 0),
                                    sp=(kt == 7 and dhc % 2 == 1):
                                    t.matmul(o, l, r, start=st_, stop=sp, skip_group_check=True),
                                    reads=[v_sl_b[par][kt], p_t8_b[kt]], writes=[Pb[ob]],
                                    inc=(dhc == 7 or (kt == 7 and dhc % 2 == 1)),
                                    record=False, tag="B_pv")
                    EM.note_readers([v_sl_b[par][k_] for k_ in range(8)]
                                    + [p_t8_b[k_] for k_ in range(8)] + [consts_b], "tensor")
                    for dhc in range(8):
                        ob = 2 + dhc // 2
                        osl = o_sb[:, dhc * T + qg * 256: dhc * T + (qg + 1) * 256]
                        psl = P[ob][:, (dhc % 2) * 256:(dhc % 2) * 256 + 256]
                        if cb == 0:
                            EM.emit("vector", lambda v, o=osl, i_=psl: v.tensor_copy(o, i_),
                                    reads=[Pb[ob]], writes=[o_sb_b], inc=True, tag="B_ev")
                        else:
                            EM.emit("vector", lambda v, o=osl, i_=psl: v.tensor_tensor(o, o, i_, OP.add),
                                    reads=[Pb[ob], o_sb_b], writes=[o_sb_b], inc=True, tag="B_ev")

        EM.emit("vector", lambda v: v.reciprocal(rden[:], pden[:, 0:8]),
                reads=[pden_b], writes=[rden_b], inc=True, tag="B")

        if dbg:
            EM.emit("sync", lambda sy: sy.dma_start(out=d_dbg["d_osb"], in_=o_sb[:]),
                    reads=[o_sb_b], inc=True, tag="dbg")
            EM.emit("sync", lambda sy: sy.dma_start(out=d_dbg["d_den"], in_=rden[:]),
                    reads=[rden_b], inc=True, tag="dbg")

        EM.fence(skip=("tensor",))
        free("p_t8", "v_sl1", "v_sl0", "k_sl1", "k_sl0", "q_fm")
        free("pden")

        a_sb = [alloc(f"a_sb{i}", [128, D], F32) for i in range(3)]
        a_sb_b = bufs(3, "a_sb")
        lnb = [alloc("lnb0", [128, D], F32)]
        lnb_b = bufs(1, "lnb")
        xr_t = [alloc(f"xr_t{i}", [128, D], F32) for i in range(2)]
        xr_t_b = bufs(2, "xr")
        x2_b = bufs(8, "x2")

        for sg in range(8):
            par = sg % 3
            spar = sg % 2
            EM.emit("sync",
                    lambda sy, o=xr_t[spar][:], i_=d_x[sg * 128:(sg + 1) * 128, :]:
                    sy.dma_start(out=o, in_=i_),
                    writes=[xr_t_b[spar]], inc=True, tag="C")
            for cc in range(8):
                bank = cc % 2
                for dhc in range(8):
                    EM.emit("tensor",
                            lambda t, o=P[bank][:, 0:512],
                            l=o_sb[:, dhc * T + sg * 128: dhc * T + (sg + 1) * 128],
                            r=wo_bf[:, dhc * D + cc * 512: dhc * D + (cc + 1) * 512],
                            st_=(dhc == 0), sp=(dhc == 7):
                            t.matmul(o, l, r, start=st_, stop=sp, skip_group_check=True),
                            reads=[o_sb_b, wo_b[dhc]], writes=[Pb[bank]],
                            inc=(dhc == 7), record=False, tag="C_mm")
                EM.note_readers([o_sb_b] + wo_b, "tensor")
                EM.emit("scalar",
                        lambda s, o=a_sb[par][:, cc * 512:(cc + 1) * 512],
                        i_=P[bank][:, 0:512], sc=rden[:, sg:sg + 1]:
                        s.activation(o, i_, AF_COPY, scale=sc),
                        reads=[Pb[bank], rden_b], writes=[a_sb_b[par]], inc=True, tag="C_mm")
            EM.emit("vector", lambda v, o=a_sb[par][:], b=bo_bc[:]:
                    v.tensor_tensor(o, o, b, OP.add),
                    reads=[a_sb_b[par], bo_b], writes=[a_sb_b[par]], inc=True, tag="C")
            if dbg and sg == 0:
                EM.emit("sync", lambda sy: sy.dma_start(out=d_dbg["d_a0"], in_=a_sb[0][:]),
                        reads=[a_sb_b[0]], inc=True, tag="dbg")
            st = stn[sg % 2]
            ln_stats(st, a_sb[par][:], a_sb_b[par], "C")
            ln_norm(st, lnb[0][:], lnb_b[0], a_sb[par][:], a_sb_b[par], "C")
            EM.emit("vector", lambda v, o=lnb[0][:], x_=xr_t[spar][:]:
                    v.tensor_tensor(o, o, x_, OP.add),
                    reads=[lnb_b[0], xr_t_b[spar]], writes=[lnb_b[0]], inc=True, tag="C")
            EM.emit("gpsimd",
                    lambda g, o=d_x2[sg * 128:(sg + 1) * 128, :], i_=lnb[0][:]:
                    g.dma_start(out=o, in_=i_),
                    reads=[lnb_b[0]], writes=[x2_b[sg]], inc=True, tag="C")

        EM.fence()
        free("xr_t1", "xr_t0", "lnb0", "a_sb2", "a_sb1", "a_sb0",
             "rden", "bo_bc", "o_sb", "wo_bf")

        ln3_fm = alloc("ln3_fm", [128, 32 * T], BF16, side="right")
        ln3_fm_b = Buf("ln3_fm")
        hsb = alloc("hsb", [128, FHC * T], BF16)
        hsb_b = Buf("hsb")
        x2t = [alloc(f"x2t{i}", [128, D], BF16) for i in range(2)]
        x2t_b = bufs(2, "x2t")
        ln3t = [alloc(f"ln3t{i}", [128, D], BF16) for i in range(2)]
        ln3t_b = bufs(2, "ln3t")
        ptr2 = [alloc(f"ptr2_{i}", [128, 1024], BF16, "psum") for i in range(2)]
        ptr2b = bufs(2, "ptr2")

        for sg in range(8):
            par = sg % 2
            EM.emit("sync",
                    lambda sy, o=x2t[par][:], i_=d_x2[sg * 128:(sg + 1) * 128, :]:
                    sy.dma_start(out=o, in_=i_),
                    reads=[x2_b[sg]], writes=[x2t_b[par]], inc=True, tag="Dp")
            st = stn[par]
            ln_stats(st, x2t[par][:], x2t_b[par], "Dp")
            ln_norm(st, ln3t[par][:], ln3t_b[par], x2t[par][:], x2t_b[par], "Dp")
            transpose_block(
                ptr2, ptr2b, ln3t[par][:], ln3t_b[par], 32,
                lambda dc, sg=sg: (ln3_fm[:, dc * T + sg * 128: dc * T + (sg + 1) * 128], ln3_fm_b),
                "Dp_tr")
        if dbg:
            EM.emit("sync", lambda sy: sy.dma_start(out=d_dbg["d_ln3"], in_=ln3_fm[:]),
                    reads=[ln3_fm_b], inc=True, tag="dbg")

        EM.fence()
        free("ptr2_1", "ptr2_0")
        free("ln3t1", "ln3t0", "x2t1", "x2t0")
        y0_b = {}
        ysb_b = bufs(8, "ysb")
        WSEG = 22 * 512

        for fh in range(2):
            if fh == 1:
                EM.fence()
            wg_sl = [alloc(f"wg_sl{fh}_{i}", [128, D], BF16) for i in range(2)]
            wg_sl_b = bufs(2, "wg_sl")
            wu_sl = [alloc(f"wu_sl{fh}_{i}", [128, D], BF16) for i in range(2)]
            wu_sl_b = bufs(2, "wu_sl")
            sg_t = [alloc(f"sg_t{fh}_{i}", [128, 512], F32) for i in range(2)]
            sg_t_b = bufs(2, "sg_t")
            for fc in range(FHC):
                fg = fh * FHC + fc
                par = fc % 2
                EM.emit("gpsimd",
                        lambda g, o=wg_sl[par][:], i_=d_wg[:, fg * D:(fg + 1) * D]:
                        g.dma_start(out=o, in_=i_),
                        writes=[wg_sl_b[par]], inc=True, tag="D_gu")
                EM.emit("gpsimd",
                        lambda g, o=wu_sl[par][:], i_=d_wu[:, fg * D:(fg + 1) * D]:
                        g.dma_start(out=o, in_=i_),
                        writes=[wu_sl_b[par]], inc=True, tag="D_gu")
                for grp in range(2):
                    slot = (fc * 2 + grp) % 3
                    gb = (0, 1, 4)[slot]
                    ub = (2, 3, 5)[slot]
                    for dc in range(32):
                        EM.emit("tensor",
                                lambda t, o=P[gb][:, 0:512],
                                l=wg_sl[par][:, dc * 128:(dc + 1) * 128],
                                r=ln3_fm[:, dc * T + grp * 512: dc * T + grp * 512 + 512],
                                st_=(dc == 0), sp=(dc == 31):
                                t.matmul(o, l, r, start=st_, stop=sp, skip_group_check=True),
                                reads=[wg_sl_b[par], ln3_fm_b], writes=[Pb[gb]],
                                inc=(dc == 31), record=False, tag="D_gu")
                    for dc in range(32):
                        EM.emit("tensor",
                                lambda t, o=P[ub][:, 0:512],
                                l=wu_sl[par][:, dc * 128:(dc + 1) * 128],
                                r=ln3_fm[:, dc * T + grp * 512: dc * T + grp * 512 + 512],
                                st_=(dc == 0), sp=(dc == 31):
                                t.matmul(o, l, r, start=st_, stop=sp, skip_group_check=True),
                                reads=[wu_sl_b[par], ln3_fm_b], writes=[Pb[ub]],
                                inc=(dc == 31), record=False, tag="D_gu")
                    EM.note_readers([wg_sl_b[par], wu_sl_b[par], ln3_fm_b], "tensor")
                    EM.emit("scalar",
                            lambda s, o=sg_t[grp][:], i_=P[gb][:, 0:512]:
                            s.activation(o, i_, AF.Silu),
                            reads=[Pb[gb]], writes=[sg_t_b[grp]], inc=True, tag="D_gu")
                    EM.emit("vector",
                            lambda v, o=hsb[:, fc * T + grp * 512: fc * T + grp * 512 + 512],
                            a=sg_t[grp][:], b=P[ub][:, 0:512]:
                            v.tensor_tensor(o, a, b, OP.mult),
                            reads=[sg_t_b[grp], Pb[ub]], writes=[hsb_b], inc=True, tag="D_gu")
            if dbg and fh == 0:
                EM.emit("sync", lambda sy: sy.dma_start(out=d_dbg["d_h0"], in_=hsb[:]),
                        reads=[hsb_b], inc=True, tag="dbg")
            EM.fence()
            free(f"sg_t{fh}_1", f"sg_t{fh}_0", f"wu_sl{fh}_1", f"wu_sl{fh}_0",
                 f"wg_sl{fh}_1", f"wg_sl{fh}_0")
            if fh == 0:
                wd_sl = [alloc(f"wd_sl0_{i}", [128, WSEG], BF16) for i in range(2)]
                wd_sl_b = bufs(2, "wd_sl")
                y0t = [alloc(f"y0t{i}", [128, 512], BF16) for i in range(4)]
                y0t_b = bufs(4, "y0t")
            else:
                free("ln3_fm")
                ysb = alloc("ysb", [128, 8 * D], BF16, side="right")
                wd_sl = [alloc(f"wd_sl1_{i}", [128, WSEG], BF16) for i in range(2)]
                wd_sl_b = bufs(2, "wd_sl")
                y0r = [alloc(f"y0r{i}", [128, 512], BF16) for i in range(4)]
                y0r_b = bufs(4, "y0r")
            y67 = [alloc(f"y67_{fh}_{i}", [128, 512], F32, "psum") for i in range(2)]
            PB8 = P + y67
            Pb8 = Pb + bufs(2, f"y67_{fh}_")

            def y0r_load(sub, cbk):
                par2 = sub % 4
                EM.emit("sync",
                        lambda sy, o=y0r[par2][:],
                        i_=d_y0[sub * 128:(sub + 1) * 128, cbk * 512:(cbk + 1) * 512]:
                        sy.dma_start(out=o, in_=i_),
                        reads=[y0_b[(sub, cbk)]], writes=[y0r_b[par2]], inc=True, tag="D_dn")

            for cbk in range(8):
                if fh == 1:
                    for sub in range(4):
                        y0r_load(sub, cbk)
                for h2 in range(2):
                    seg = 22 if h2 == 0 else 21
                    spar = (cbk * 2 + h2) % 2
                    col0 = (fh * FHC + h2 * 22) * 512
                    EM.emit("gpsimd",
                            lambda g, o=wd_sl[spar][:, 0:seg * 512],
                            i_=d_wd[cbk * 128:(cbk + 1) * 128, col0:col0 + seg * 512]:
                            g.dma_start(out=o, in_=i_),
                            writes=[wd_sl_b[spar]], inc=True, tag="D_dn")
                    for fcl in range(seg):
                        fc = h2 * 22 + fcl
                        for sub in range(8):
                            EM.emit("tensor",
                                    lambda t, o=PB8[sub][:, 0:512],
                                    l=hsb[:, fc * T + sub * 128: fc * T + (sub + 1) * 128],
                                    r=wd_sl[spar][:, fcl * 512:(fcl + 1) * 512],
                                    st_=(fc == 0), sp=(fc == FHC - 1):
                                    t.matmul(o, l, r, start=st_, stop=sp, skip_group_check=True),
                                    reads=[hsb_b, wd_sl_b[spar]], writes=[Pb8[sub]],
                                    inc=(sub == 7 or fc == FHC - 1), record=False, tag="D_dn")
                    EM.note_readers([hsb_b, wd_sl_b[spar]], "tensor")
                if fh == 0:
                    for sub in range(8):
                        yt = sub % 4
                        eng = "scalar" if sub % 2 == 0 else "vector"
                        if eng == "scalar":
                            EM.emit("scalar",
                                    lambda s, o=y0t[yt][:], i_=PB8[sub][:, 0:512]:
                                    s.activation(o, i_, AF_COPY),
                                    reads=[Pb8[sub]], writes=[y0t_b[yt]], inc=True, tag="D_dn")
                        else:
                            EM.emit("vector",
                                    lambda v, o=y0t[yt][:], i_=PB8[sub][:, 0:512]:
                                    v.tensor_copy(o, i_),
                                    reads=[Pb8[sub]], writes=[y0t_b[yt]], inc=True, tag="D_dn")
                        b = Buf(f"y0_{sub}_{cbk}")
                        y0_b[(sub, cbk)] = b
                        EM.emit("sync",
                                lambda sy, o=d_y0[sub * 128:(sub + 1) * 128, cbk * 512:(cbk + 1) * 512],
                                i_=y0t[yt][:]:
                                sy.dma_start(out=o, in_=i_),
                                reads=[y0t_b[yt]], writes=[b], inc=True, tag="D_dn")
                else:
                    for sub in range(8):
                        ysl = ysb[:, sub * D + cbk * 512: sub * D + (cbk + 1) * 512]
                        if sub % 2 == 0:
                            EM.emit("scalar",
                                    lambda s, o=ysl, i_=PB8[sub][:, 0:512]:
                                    s.activation(o, i_, AF_COPY),
                                    reads=[Pb8[sub]], writes=[ysb_b[sub]], inc=True, tag="D_dn")
                        else:
                            EM.emit("vector",
                                    lambda v, o=ysl, i_=PB8[sub][:, 0:512]:
                                    v.tensor_copy(o, i_),
                                    reads=[Pb8[sub]], writes=[ysb_b[sub]], inc=True, tag="D_dn")
                    for sub in range(8):
                        par2 = sub % 4
                        ysl = ysb[:, sub * D + cbk * 512: sub * D + (cbk + 1) * 512]
                        EM.emit("vector",
                                lambda v, o=ysl, a=y0r[par2][:]:
                                v.tensor_tensor(o, o, a, OP.add),
                                reads=[ysb_b[sub], y0r_b[par2]], writes=[ysb_b[sub]], inc=True, tag="D_dn")
                        if sub < 4:
                            y0r_load(sub + 4, cbk)
            free("y67_%d_1" % fh, "y67_%d_0" % fh)
            if fh == 0:
                free("y0t3", "y0t2", "y0t1", "y0t0", "wd_sl0_1", "wd_sl0_0")

        if dbg:
            EM.emit("sync", lambda sy: sy.dma_start(out=d_dbg["d_ysb"], in_=ysb[:]),
                    reads=ysb_b, inc=True, tag="dbg")

        EM.fence()
        free("y0r3", "y0r2", "y0r1", "y0r0", "wd_sl1_1", "wd_sl1_0", "hsb")
        x2f = [alloc(f"x2f{i}", [128, D], BF16) for i in range(4)]
        x2f_b = bufs(4, "x2f")
        outf = [alloc(f"outf{i}", [128, D], F32) for i in range(2)]
        outf_b = bufs(2, "outf")
        out_done = bufs(8, "out")

        def x2f_load(sub):
            EM.emit("sync",
                    lambda sy, o=x2f[sub % 4][:], i_=d_x2[sub * 128:(sub + 1) * 128, :]:
                    sy.dma_start(out=o, in_=i_),
                    reads=[x2_b[sub]], writes=[x2f_b[sub % 4]], inc=True, tag="E")

        for sub in range(4):
            x2f_load(sub)
        for sub in range(8):
            par = sub % 4
            st = stn[sub % 2]
            ysl = ysb[:, sub * D:(sub + 1) * D]
            ln_stats(st, ysl, ysb_b[sub], "E")
            ln_norm(st, outf[sub % 2][:], outf_b[sub % 2], ysl, ysb_b[sub], "E")
            EM.emit("vector", lambda v, o=outf[sub % 2][:], x_=x2f[par][:]:
                    v.tensor_tensor(o, o, x_, OP.add),
                    reads=[outf_b[sub % 2], x2f_b[par]], writes=[outf_b[sub % 2]], inc=True, tag="E")
            if sub < 4:
                x2f_load(sub + 4)
            EM.emit("sync",
                    lambda sy, o=d_out[sub * 128:(sub + 1) * 128, :], i_=outf[sub % 2][:]:
                    sy.dma_start(out=o, in_=i_),
                    reads=[outf_b[sub % 2]], writes=[out_done[sub]], inc=True, tag="E")
        EM.emit("sync", None, reads=out_done, tag="E")

        blk = outer.enter_context(nc.Block())
        EM.replay(nc, blk, sems)
    return nc


def _prep(L0_attn_x, L0_attn_Wqkv, L0_attn_bqkv, L0_attn_Wo, L0_attn_bo,
          L0_ffn_W_gate, L0_ffn_W_up, L0_ffn_W_down):
    x = np.ascontiguousarray(np.asarray(L0_attn_x, dtype=np.float32).reshape(S, D))
    Wqkv = np.asarray(L0_attn_Wqkv, dtype=np.float32)
    wq_t = np.ascontiguousarray(
        Wqkv.reshape(32, 128, 24, 128).transpose(2, 1, 0, 3)).reshape(24 * 128, D)
    bqkv_t = np.ascontiguousarray(
        np.asarray(L0_attn_bqkv, dtype=np.float32).reshape(24, 128).T)
    wo = np.ascontiguousarray(np.asarray(L0_attn_Wo, dtype=np.float32))
    bo_b = np.ascontiguousarray(
        np.broadcast_to(np.asarray(L0_attn_bo, dtype=np.float32), (128, D)))
    wg_t = np.ascontiguousarray(
        np.asarray(L0_ffn_W_gate, dtype=np.float32)
        .reshape(NFC, 128, 32, 128).transpose(3, 0, 2, 1)).reshape(128, NFC * D)
    wu_t = np.ascontiguousarray(
        np.asarray(L0_ffn_W_up, dtype=np.float32)
        .reshape(NFC, 128, 32, 128).transpose(3, 0, 2, 1)).reshape(128, NFC * D)
    wd_t = np.ascontiguousarray(
        np.asarray(L0_ffn_W_down, dtype=np.float32)
        .reshape(8, 512, NFC, 128).transpose(0, 3, 2, 1)).reshape(8 * 128, NFC * 512)
    ident = np.eye(128, dtype=np.float32)
    common = {"wq": wq_t, "bqkv": bqkv_t, "wo": wo, "bo": bo_b,
              "wg": wg_t, "wu": wu_t, "wd": wd_t, "ident": ident}
    return x, common


def kernel(L0_attn_x, L0_attn_Wqkv, L0_attn_bqkv, L0_attn_Wo, L0_attn_bo,
           L0_ffn_W_gate, L0_ffn_W_up, L0_ffn_W_down, _dbg=False):
    x, common = _prep(L0_attn_x, L0_attn_Wqkv, L0_attn_bqkv, L0_attn_Wo,
                      L0_attn_bo, L0_ffn_W_gate, L0_ffn_W_up, L0_ffn_W_down)
    nc = build_program(dbg=_dbg)
    in_maps = [dict(common, x=np.ascontiguousarray(x[c * T:(c + 1) * T]))
               for c in range(8)]
    res = run_bass_kernel_spmd(nc, in_maps, core_ids=list(range(8)))
    out = np.concatenate([res.results[c]["out"] for c in range(8)], axis=0)
    out = out.reshape(1, S, D).astype(np.float32)
    if _dbg:
        return out, res
    return out


# version 18
# speedup vs baseline: 1.3160x; 1.0056x over previous
import numpy as np
from contextlib import ExitStack

import concourse.bass as bass
import concourse.mybir as mybir
from concourse.bass_utils import run_bass_kernel_spmd

F32 = mybir.dt.float32
BF16 = mybir.dt.bfloat16
AF = mybir.ActivationFunctionType
OP = mybir.AluOpType
AF_COPY = getattr(AF, "Copy", None)
if AF_COPY is None:
    AF_COPY = getattr(AF, "Identity")

ENGINES = ("tensor", "vector", "scalar", "sync", "gpsimd")
ENG_SEM = {"tensor": "mt", "vector": "mv", "scalar": "ms", "sync": "dsy", "gpsimd": "dgp"}
ALL_SEMS = ("mt", "mv", "ms", "dsy", "dgp", "cc")
COMPUTE = ("tensor", "vector", "scalar")


class Buf:
    __slots__ = ("name", "ready", "readers")

    def __init__(self, name=""):
        self.name = name
        self.ready = None
        self.readers = {}


def bufs(n, name):
    return [Buf(f"{name}{i}") for i in range(n)]


class Em:
    def __init__(self):
        self.ops = {e: [] for e in ENGINES}
        self.cnt = {s: 0 for s in ALL_SEMS}
        self.cnt_sbuf = {s: 0 for s in ALL_SEMS}
        self.waited = {e: {} for e in ENGINES}
        self.pfence = {e: None for e in ENGINES}
        self.last_writes = {e: frozenset() for e in ENGINES}

    def emit(self, eng, build, reads=(), writes=(), inc=False, record=True,
             dram_only=False, use_sem=None, tag=""):
        own = ENG_SEM[eng]
        inc_sem = use_sem or own
        if eng == "vector" and any(id(b) in self.last_writes[eng] for b in reads):
            self.ops[eng].append(((), lambda v: v.engine_nop(), None, "nopg"))
            self.last_writes[eng] = frozenset()
        waits = {}

        def need(sem, val, why):
            if sem == own and eng in COMPUTE:
                return
            if val is None:
                raise RuntimeError(f"un-inc'd dependency: {why} [{tag}] on {eng}")
            if self.waited[eng].get(sem, 0) >= val:
                return
            if waits.get(sem, 0) < val:
                waits[sem] = val

        if eng in ("sync", "gpsimd"):
            need(own, self.cnt[own], "dma-serialize")
        pf = self.pfence[eng]
        if pf:
            self.pfence[eng] = None
            for s, v in pf.items():
                need(s, v, "fence")
        for b in reads:
            r = b.ready
            if r is None:
                continue
            if r[0] == "eng":
                if r[1] != eng:
                    raise RuntimeError(f"cross-engine read of un-inc'd {b.name} [{tag}]")
            else:
                need(r[0], r[1], f"read {b.name}")
        for b in writes:
            for s, v in b.readers.items():
                if s == own and eng in COMPUTE:
                    continue
                need(s, v, f"WAR {b.name}")
            r = b.ready
            if r is not None:
                if r[0] == "eng":
                    if r[1] != eng:
                        raise RuntimeError(f"WAW cross-engine un-inc'd {b.name} [{tag}]")
                else:
                    need(r[0], r[1], f"WAW {b.name}")
        increc = None
        if inc:
            amt = (16 if eng in ("sync", "gpsimd") else 1) if inc is True else int(inc)
            self.cnt[inc_sem] += amt
            if not dram_only:
                self.cnt_sbuf[inc_sem] = self.cnt[inc_sem]
            increc = (inc_sem, amt)
        self.ops[eng].append((tuple(sorted(waits.items())), build, increc, tag))
        self.last_writes[eng] = frozenset(id(b) for b in writes)
        for s, v in waits.items():
            self.waited[eng][s] = v
        mark = (inc_sem, self.cnt[inc_sem]) if inc else ("eng", eng)
        for b in writes:
            b.ready = mark
            b.readers = {}
        if record:
            for b in reads:
                b.readers[own] = self.cnt[own] if (inc and inc_sem == own) else None

    def note_readers(self, rbufs, eng):
        own = ENG_SEM[eng]
        for b in rbufs:
            b.readers[own] = self.cnt[own]

    def fence(self, skip=()):
        snap = {s: v for s, v in self.cnt_sbuf.items() if v > 0}
        for e in ENGINES:
            if e in skip:
                continue
            pf = self.pfence[e] or {}
            keys = set(pf) | set(snap)
            self.pfence[e] = {s: max(pf.get(s, 0), snap.get(s, 0)) for s in keys}

    def replay(self, nc, blk, sems):
        for eng in ENGINES:
            ops = self.ops[eng]

            def make(ops=ops):
                def body(e):
                    for waits, build, increc, tag in ops:
                        for s, v in waits:
                            e.wait_ge(sems[s], v)
                        ins = build(e) if build is not None else None
                        if increc is not None:
                            ins.then_inc(sems[increc[0]], increc[1])
                return body
            getattr(blk, eng)(make())


T = 1024
S = 8192
D = 4096
DH = 1024
F = 11008
NT = 8
NFC = 86
FHC = 43
C_SHIFT = -12.0
SCALE = 1.0 / 32.0


def build_program(dbg=False):
    nc = bass.Bass("TRN2", target_bir_lowering=False)

    d_x = nc.dram_tensor("x", [T, D], F32, kind="ExternalInput").ap()
    d_wq = nc.dram_tensor("wq", [24 * 128, D], F32, kind="ExternalInput").ap()
    d_bqkv = nc.dram_tensor("bqkv", [128, 24], F32, kind="ExternalInput").ap()
    d_wo = nc.dram_tensor("wo", [DH, D], F32, kind="ExternalInput").ap()
    d_bo = nc.dram_tensor("bo", [128, D], F32, kind="ExternalInput").ap()
    d_wg = nc.dram_tensor("wg", [128, NFC * D], F32, kind="ExternalInput").ap()
    d_wu = nc.dram_tensor("wu", [128, NFC * D], F32, kind="ExternalInput").ap()
    d_wd = nc.dram_tensor("wd", [8 * 128, NFC * 512], F32, kind="ExternalInput").ap()
    d_id = nc.dram_tensor("ident", [128, 128], F32, kind="ExternalInput").ap()
    d_out = nc.dram_tensor("out", [T, D], F32, kind="ExternalOutput").ap()

    xk = "ExternalOutput" if dbg else "Internal"
    d_x2 = nc.dram_tensor("x2d", [T, D], BF16, kind=xk).ap()
    d_y0 = nc.dram_tensor("y0d", [T, D], BF16, kind=xk).ap()
    d_cik = nc.dram_tensor("cc_in_k", [DH, T], BF16, kind="Internal").ap()
    d_civ = nc.dram_tensor("cc_in_v", [T, DH], BF16, kind="Internal").ap()
    d_cok = nc.dram_tensor("cc_out_k", [8 * DH, T], BF16, addr_space="Shared", kind="Internal").ap()
    d_cov = nc.dram_tensor("cc_out_v", [S, DH], BF16, addr_space="Shared", kind="Internal").ap()
    if dbg:
        d_dbg = {
            "d_qfm": nc.dram_tensor("d_qfm", [128, 8 * T], BF16, kind="ExternalOutput").ap(),
            "d_kfm": nc.dram_tensor("d_kfm", [128, 8 * T], BF16, kind="ExternalOutput").ap(),
            "d_vtm": nc.dram_tensor("d_vtm", [128, 8 * T], BF16, kind="ExternalOutput").ap(),
            "d_osb": nc.dram_tensor("d_osb", [128, 8 * T], BF16, kind="ExternalOutput").ap(),
            "d_den": nc.dram_tensor("d_den", [128, 8], F32, kind="ExternalOutput").ap(),
            "d_a0": nc.dram_tensor("d_a0", [128, D], F32, kind="ExternalOutput").ap(),
            "d_h0": nc.dram_tensor("d_h0", [128, FHC * T], BF16, kind="ExternalOutput").ap(),
            "d_ln3": nc.dram_tensor("d_ln3", [128, 32 * T], BF16, kind="ExternalOutput").ap(),
            "d_ysb": nc.dram_tensor("d_ysb", [128, 8 * D], BF16, kind="ExternalOutput").ap(),
            "d_xt0": nc.dram_tensor("d_xt0", [128, D], F32, kind="ExternalOutput").ap(),
            "d_ln1t0": nc.dram_tensor("d_ln1t0", [128, D], F32, kind="ExternalOutput").ap(),
            "d_ln1fm": nc.dram_tensor("d_ln1fm", [128, 32 * 512], BF16, kind="ExternalOutput").ap(),
            "d_wq0": nc.dram_tensor("d_wq0", [128, D], BF16, kind="ExternalOutput").ap(),
        }

    EM = Em()
    outer = ExitStack()
    live = {}

    def alloc(name, shape, dtype, kind="sbuf", side="left"):
        if kind == "sbuf":
            cm = nc.sbuf_tensor(name, shape, dtype, side=side)
        else:
            cm = nc.psum_tensor(name, shape, dtype)
        ap = cm.__enter__()
        live[name] = cm
        return ap

    def free(*names):
        for n in names:
            live.pop(n).__exit__(None, None, None)

    with outer:
        sems = {s: outer.enter_context(nc.semaphore(s)) for s in ALL_SEMS}

        P = [alloc(f"p{i}", [128, 512], F32, "psum") for i in range(6)]
        Pb = bufs(6, "psum")

        ident = alloc("ident_sb", [128, 128], F32)
        ident_b = Buf("ident")
        c_eps = alloc("c_eps", [128, 1], F32)
        c_s32 = alloc("c_s32", [128, 1], F32)
        c_nb12 = alloc("c_nb12", [128, 1], F32)
        c_i4096 = alloc("c_i4096", [128, 1], F32)
        ones_bf = alloc("ones_bf", [128, 1], BF16)
        consts_b = Buf("consts")
        bqkv_sb = alloc("bqkv_sb", [128, 24], F32)
        bqkv_b = Buf("bqkv")

        scr_bf = alloc("scr_bf", [128, D], BF16)
        scr_b = Buf("scr")
        stn = []
        for i in range(2):
            stn.append({
                "sums": alloc(f"sums_{i}", [128, 2], F32), "sumsb": Buf(f"sums_{i}"),
                "mean": alloc(f"mean_{i}", [128, 1], F32), "meanb": Buf(f"mean_{i}"),
                "msq": alloc(f"msq_{i}", [128, 1], F32), "msqb": Buf(f"msq_{i}"),
                "veps": alloc(f"veps_{i}", [128, 1], F32), "vepsb": Buf(f"veps_{i}"),
                "std": alloc(f"std_{i}", [128, 1], F32), "stdb": Buf(f"std_{i}"),
                "rcp": alloc(f"rcp_{i}", [128, 1], F32), "rcpb": Buf(f"rcp_{i}"),
            })

        def ln_stats(st, src_ap, src_buf, tag):
            EM.emit("vector", lambda v, o=st["sums"][:, 0:1], i_=src_ap:
                    v.tensor_reduce(o, i_, mybir.AxisListType.X, OP.add),
                    reads=[src_buf], writes=[st["sumsb"]], inc=True, tag=tag)
            EM.emit("scalar", lambda s, o=scr_bf[:], i_=src_ap, a=st["sums"][:, 1:2]:
                    s.activation(o, i_, AF.Square, accum_out=a),
                    reads=[src_buf], writes=[scr_b, st["sumsb"]], inc=True, tag=tag)
            EM.emit("scalar", lambda s, o=st["mean"][:], i_=st["sums"][:, 0:1], sc=c_i4096[:, 0:1]:
                    s.activation(o, i_, AF_COPY, scale=sc),
                    reads=[st["sumsb"], consts_b], writes=[st["meanb"]], inc=True, tag=tag)
            EM.emit("scalar", lambda s, o=st["msq"][:], i_=st["mean"][:]:
                    s.activation(o, i_, AF.Square),
                    reads=[st["meanb"]], writes=[st["msqb"]], inc=True, tag=tag)
            EM.emit("vector", lambda v, o=st["veps"][:], i_=st["sums"][:, 1:2], m2=st["msq"][:, 0:1]:
                    v.tensor_scalar(o, i_, 1.0 / D, m2, OP.mult, OP.subtract),
                    reads=[st["sumsb"], st["msqb"]], writes=[st["vepsb"]], inc=True, tag=tag)
            EM.emit("scalar", lambda s, o=st["std"][:], i_=st["veps"][:], b=c_eps[:, 0:1]:
                    s.activation(o, i_, AF.Sqrt, bias=b),
                    reads=[st["vepsb"], consts_b], writes=[st["stdb"]], inc=True, tag=tag)
            EM.emit("vector", lambda v, o=st["rcp"][:], i_=st["std"][:]: v.reciprocal(o, i_),
                    reads=[st["stdb"]], writes=[st["rcpb"]], inc=True, tag=tag)

        def ln_norm(st, out_ap, out_buf, src_ap, src_buf, tag):
            EM.emit("vector",
                    lambda v, o=out_ap, i_=src_ap, b=st["mean"][:, 0:1], sc=st["rcp"][:, 0:1]:
                    v.tensor_scalar(o, i_, b, sc, OP.subtract, OP.mult),
                    reads=[src_buf, st["meanb"], st["rcpb"]], writes=[out_buf], inc=True, tag=tag)

        EM.emit("vector", lambda v: v.memset(c_eps[:], 1e-5), writes=[consts_b], tag="init")
        EM.emit("vector", lambda v: v.memset(c_s32[:], SCALE), writes=[consts_b], tag="init")
        EM.emit("vector", lambda v: v.memset(c_nb12[:], C_SHIFT), writes=[consts_b], tag="init")
        EM.emit("vector", lambda v: v.memset(c_i4096[:], 1.0 / D), writes=[consts_b], tag="init")
        EM.emit("vector", lambda v: v.memset(ones_bf[:], 1.0), writes=[consts_b], inc=True, tag="init")
        EM.emit("sync", lambda sy: sy.dma_start(out=ident[:], in_=d_id),
                writes=[ident_b], inc=True, tag="init")
        identb = alloc("identb_sb", [128, 128], BF16)
        identb_b = Buf("identb")
        EM.emit("gpsimd", lambda g: g.dma_start(out=identb[:], in_=d_id),
                writes=[identb_b], inc=True, tag="init")
        EM.emit("sync", lambda sy: sy.dma_start(out=bqkv_sb[:], in_=d_bqkv),
                writes=[bqkv_b], inc=True, tag="init")

        ptr = [alloc(f"ptr{i}", [128, 1024], BF16, "psum") for i in range(2)]
        ptrb = bufs(2, "ptr")
        q_fm = alloc("q_fm", [128, 8 * T], BF16)
        q_fm_b = Buf("q_fm")
        k_fm = alloc("k_fm", [128, 8 * T], BF16)
        k_fm_b = Buf("k_fm")
        v_fm = alloc("v_fm", [128, 8 * T], BF16)
        v_fm_b = Buf("v_fm")
        v_tm = alloc("v_tm", [128, 8 * T], BF16)
        v_tm_b = Buf("v_tm")
        x_t = [alloc(f"x_t{i}", [128, D], F32) for i in range(2)]
        x_t_b = bufs(2, "x_t")
        ln1_t = [alloc(f"ln1_t{i}", [128, D], BF16) for i in range(2)]
        ln1_t_b = bufs(2, "ln1_t")
        ln1_fm = alloc("ln1_fm", [128, 32 * T], BF16)
        ln1_fm_b = Buf("ln1_fm")
        wq_sl = [alloc(f"wq_sl{i}", [128, D], BF16) for i in range(2)]
        wq_sl_b = bufs(2, "wq_sl")

        def transpose_block(trp, trb, src_ap, src_buf, n_dc, dst_fn, tag):
            for g in range(n_dc // 4):
                bank = g % 2
                for j in range(4):
                    dc = g * 4 + j
                    EM.emit("tensor",
                            lambda t, o=trp[bank][:, j * 128:(j + 1) * 128],
                            i_=src_ap[:, dc * 128:(dc + 1) * 128], idn=identb[:]:
                            t.transpose(o, i_, idn),
                            reads=[src_buf, identb_b], writes=[trb[bank]],
                            inc=(j == 3), record=(j == 3), tag=tag)
                for j in range(4):
                    dc = g * 4 + j
                    dst_ap, dst_buf = dst_fn(dc)
                    EM.emit("vector",
                            lambda v, o=dst_ap, i_=trp[bank][:, j * 128:(j + 1) * 128]:
                            v.tensor_copy(o, i_),
                            reads=[trb[bank]], writes=[dst_buf],
                            inc=(j == 3), record=(j == 3), tag=tag)

        for s_g in range(8):
            par = s_g % 2
            EM.emit("sync",
                    lambda sy, o=x_t[par][:], i_=d_x[s_g * 128:(s_g + 1) * 128, :]:
                    sy.dma_start(out=o, in_=i_),
                    writes=[x_t_b[par]], inc=True, tag="A_ln")
            st = stn[par]
            ln_stats(st, x_t[par][:], x_t_b[par], "A_ln")
            ln_norm(st, ln1_t[par][:], ln1_t_b[par], x_t[par][:], x_t_b[par], "A_ln")
            if dbg and s_g == 0:
                EM.emit("sync", lambda sy: sy.dma_start(out=d_dbg["d_xt0"], in_=x_t[0][:]),
                        reads=[x_t_b[0]], inc=True, tag="dbg")
                EM.emit("gpsimd", lambda g: g.dma_start(out=d_dbg["d_ln1t0"], in_=ln1_t[0][:]),
                        reads=[ln1_t_b[0]], inc=True, tag="dbg")
            transpose_block(
                ptr, ptrb, ln1_t[par][:], ln1_t_b[par], 32,
                lambda dc, s_g=s_g: (ln1_fm[:, dc * T + s_g * 128: dc * T + (s_g + 1) * 128], ln1_fm_b),
                "A_tr")
        if dbg:
            EM.emit("sync", lambda sy: sy.dma_start(out=d_dbg["d_ln1fm"], in_=ln1_fm[:, 0:32 * 512]),
                    reads=[ln1_fm_b], inc=True, tag="dbg")
        cik_b = bufs(8, "cik")
        civ_b = bufs(8, "civ")
        cok_b = Buf("cok")
        cov_b = Buf("cov")
        for fc in list(range(8, 24)) + list(range(0, 8)):
            par = fc % 2
            EM.emit("gpsimd",
                    lambda g, o=wq_sl[par][:], i_=d_wq[fc * 128:(fc + 1) * 128, :]:
                    g.dma_start(out=o, in_=i_),
                    writes=[wq_sl_b[par]], inc=True, tag="A_qkv")
            if dbg and fc == 0:
                EM.emit("sync", lambda sy: sy.dma_start(out=d_dbg["d_wq0"], in_=wq_sl[0][:]),
                        reads=[wq_sl_b[0]], inc=True, tag="dbg")
            if fc < 8:
                dst_ap, dst_buf = q_fm, q_fm_b
                col0 = fc * T
            elif fc < 16:
                dst_ap, dst_buf = k_fm, k_fm_b
                col0 = (fc - 8) * T
            else:
                dst_ap, dst_buf = v_fm, v_fm_b
                col0 = (fc - 16) * T
            for h in range(2):
                bank = (fc * 2 + h) % 2
                for dc in range(32):
                    EM.emit("tensor",
                            lambda t, o=P[bank][:, 0:512],
                            l=wq_sl[par][:, dc * 128:(dc + 1) * 128],
                            r=ln1_fm[:, dc * T + h * 512: dc * T + (h + 1) * 512],
                            st_=(dc == 0), sp=(dc == 31):
                            t.matmul(o, l, r, start=st_, stop=sp, skip_group_check=True),
                            reads=[wq_sl_b[par], ln1_fm_b], writes=[Pb[bank]],
                            inc=(dc == 31), record=(dc == 31), tag="A_qkv")
                EM.emit("vector",
                        lambda v, o=dst_ap[:, col0 + h * 512: col0 + (h + 1) * 512],
                        i_=P[bank][:, 0:512], b=bqkv_sb[:, fc:fc + 1]:
                        v.tensor_scalar_add(o, i_, b),
                        reads=[Pb[bank], bqkv_b], writes=[dst_buf], inc=True, tag="A_qkv")
            if fc == 15:
                for i in range(8):
                    EM.emit("sync",
                            lambda sy, o=d_cik[i * 128:(i + 1) * 128, :], i_=k_fm[:, i * T:(i + 1) * T]:
                            sy.dma_start(out=o, in_=i_),
                            reads=[k_fm_b], writes=[cik_b[i]], inc=True, tag="A_cc")
                EM.emit("gpsimd",
                        lambda g: g.collective_compute(
                            "AllGather", OP.bypass,
                            replica_groups=[[0, 1, 2, 3, 4, 5, 6, 7]],
                            ins=[d_cik], outs=[d_cok]),
                        reads=cik_b, writes=[cok_b], inc=1, dram_only=True,
                        use_sem="cc", tag="A_cc")
            if fc == 23:
                for kt in range(8):
                    for g2 in range(2):
                        bank = (kt * 2 + g2) % 2
                        for j in range(4):
                            dhc = g2 * 4 + j
                            EM.emit("tensor",
                                    lambda t, o=ptr[bank][:, j * 128:(j + 1) * 128],
                                    i_=v_fm[:, dhc * T + kt * 128: dhc * T + (kt + 1) * 128], idn=identb[:]:
                                    t.transpose(o, i_, idn),
                                    reads=[v_fm_b, identb_b], writes=[ptrb[bank]],
                                    inc=(j == 3), record=(j == 3), tag="A_vtr")
                        EM.emit("vector",
                                lambda v, o=v_tm[:, kt * T + g2 * 512: kt * T + (g2 + 1) * 512],
                                i_=ptr[bank][:, 0:512]:
                                v.tensor_copy(o, i_),
                                reads=[ptrb[bank]], writes=[v_tm_b], inc=True, tag="A_vtr")
                for i in range(8):
                    EM.emit("sync",
                            lambda sy, o=d_civ[i * 128:(i + 1) * 128, :], i_=v_tm[:, i * T:(i + 1) * T]:
                            sy.dma_start(out=o, in_=i_),
                            reads=[v_tm_b], writes=[civ_b[i]], inc=True, tag="A_cc")
                EM.emit("gpsimd",
                        lambda g: g.collective_compute(
                            "AllGather", OP.bypass,
                            replica_groups=[[0, 1, 2, 3, 4, 5, 6, 7]],
                            ins=[d_civ], outs=[d_cov]),
                        reads=civ_b, writes=[cov_b], inc=1, dram_only=True,
                        use_sem="cc", tag="A_cc")

        if dbg:
            EM.emit("sync", lambda sy: sy.dma_start(out=d_dbg["d_qfm"], in_=q_fm[:]),
                    reads=[q_fm_b], inc=True, tag="dbg")
            EM.emit("sync", lambda sy: sy.dma_start(out=d_dbg["d_kfm"], in_=k_fm[:]),
                    reads=[k_fm_b], inc=True, tag="dbg")
            EM.emit("sync", lambda sy: sy.dma_start(out=d_dbg["d_vtm"], in_=v_tm[:]),
                    reads=[v_tm_b], inc=True, tag="dbg")

        EM.fence()
        free("wq_sl1", "wq_sl0", "ln1_fm", "ln1_t1", "ln1_t0", "x_t1", "x_t0",
             "v_tm", "v_fm", "k_fm")
        free("ptr1", "ptr0")
        pden = alloc("pden", [128, 512], F32, "psum")
        pden_b = Buf("pden")

        wo_bf = alloc("wo_bf", [128, 8 * D], BF16, side="right")
        wo_b = bufs(8, "wo")
        o_sb = alloc("o_sb", [128, 8 * T], BF16, side="right")
        o_sb_b = Buf("o_sb")
        bo_bc = alloc("bo_bc", [128, D], BF16, side="right")
        bo_b = Buf("bo")
        rden = alloc("rden", [128, 8], F32, side="right")
        rden_b = Buf("rden")
        k_sl = [alloc(f"k_sl{i}", [128, 8 * T], BF16) for i in range(2)]
        k_sl_b = [bufs(8, f"k_sl{i}_") for i in range(2)]
        v_sl = [alloc(f"v_sl{i}", [128, 8 * T], BF16) for i in range(2)]
        v_sl_b = [bufs(8, f"v_sl{i}_") for i in range(2)]
        p_t8 = alloc("p_t8", [128, 8 * 512], BF16)
        p_t8_b = bufs(8, "p_t8")

        for i in range(8):
            EM.emit("gpsimd",
                    lambda g, o=wo_bf[:, i * D:(i + 1) * D], i_=d_wo[i * 128:(i + 1) * 128, :]:
                    g.dma_start(out=o, in_=i_),
                    writes=[wo_b[i]], inc=True, tag="B")
        EM.emit("gpsimd", lambda g: g.dma_start(out=bo_bc[:], in_=d_bo),
                writes=[bo_b], inc=True, tag="B")

        for cb in range(8):
            par = cb % 2
            for i in range(8):
                EM.emit("sync",
                        lambda sy, o=k_sl[par][:, i * T:(i + 1) * T],
                        i_=d_cok[cb * DH + i * 128: cb * DH + (i + 1) * 128, :]:
                        sy.dma_start(out=o, in_=i_),
                        reads=[cok_b], writes=[k_sl_b[par][i]], inc=True, tag="B")
            for i in range(8):
                EM.emit("sync",
                        lambda sy, o=v_sl[par][:, i * T:(i + 1) * T],
                        i_=d_cov[cb * T + i * 128: cb * T + (i + 1) * 128, :]:
                        sy.dma_start(out=o, in_=i_),
                        reads=[cov_b], writes=[v_sl_b[par][i]], inc=True, tag="B")

            for qp in range(2):
                for kt in range(8):
                    bank = kt % 4
                    for dhc in range(8):
                        EM.emit("tensor",
                                lambda t, o=P[bank][:, 0:512],
                                l=k_sl[par][:, dhc * T + kt * 128: dhc * T + (kt + 1) * 128],
                                r=q_fm[:, dhc * T + qp * 512: dhc * T + (qp + 1) * 512],
                                st_=(dhc == 0), sp=(dhc == 7):
                                t.matmul(o, l, r, start=st_, stop=sp, skip_group_check=True),
                                reads=[k_sl_b[par][dhc], q_fm_b], writes=[Pb[bank]],
                                inc=(dhc == 7), record=False, tag="B_s")
                    EM.note_readers([k_sl_b[par][d_] for d_ in range(8)] + [q_fm_b], "tensor")
                    EM.emit("scalar",
                            lambda s, o=p_t8[:, kt * 512:(kt + 1) * 512], i_=P[bank][:, 0:512],
                            b=c_nb12[:, 0:1], sc=c_s32[:, 0:1]:
                            s.activation(o, i_, AF.Exp, bias=b, scale=sc),
                            reads=[Pb[bank], consts_b], writes=[p_t8_b[kt]], inc=True, tag="B_s")
                    for sub in range(4):
                        EM.emit("tensor",
                                lambda t, o=pden[:, qp * 4 + sub: qp * 4 + sub + 1],
                                l=p_t8[:, kt * 512 + sub * 128: kt * 512 + (sub + 1) * 128],
                                r=ones_bf[:],
                                st_=(cb == 0 and qp == 0 and kt == 0 and sub == 0),
                                sp=(cb == 7 and qp == 1 and kt == 7 and sub == 3):
                                t.matmul(o, l, r, start=st_, stop=sp, skip_group_check=True),
                                reads=[p_t8_b[kt], consts_b], writes=[pden_b],
                                inc=(cb == 7 and qp == 1 and kt == 7 and sub == 3),
                                record=False, tag="B_d")
                for qh in range(2):
                    qg = qp * 2 + qh
                    for kt in range(8):
                        for dhc in range(8):
                            ob = 2 + dhc // 2
                            EM.emit("tensor",
                                    lambda t, o=P[ob][:, (dhc % 2) * 256:(dhc % 2) * 256 + 256],
                                    l=v_sl[par][:, kt * T + dhc * 128: kt * T + (dhc + 1) * 128],
                                    r=p_t8[:, kt * 512 + qh * 256: kt * 512 + qh * 256 + 256],
                                    st_=(kt == 0 and dhc % 2 == 0),
                                    sp=(kt == 7 and dhc % 2 == 1):
                                    t.matmul(o, l, r, start=st_, stop=sp, skip_group_check=True),
                                    reads=[v_sl_b[par][kt], p_t8_b[kt]], writes=[Pb[ob]],
                                    inc=(dhc == 7 or (kt == 7 and dhc % 2 == 1)),
                                    record=False, tag="B_pv")
                    EM.note_readers([v_sl_b[par][k_] for k_ in range(8)]
                                    + [p_t8_b[k_] for k_ in range(8)] + [consts_b], "tensor")
                    for dhc in range(8):
                        ob = 2 + dhc // 2
                        osl = o_sb[:, dhc * T + qg * 256: dhc * T + (qg + 1) * 256]
                        psl = P[ob][:, (dhc % 2) * 256:(dhc % 2) * 256 + 256]
                        if cb == 0:
                            EM.emit("vector", lambda v, o=osl, i_=psl: v.tensor_copy(o, i_),
                                    reads=[Pb[ob]], writes=[o_sb_b], inc=True, tag="B_ev")
                        else:
                            EM.emit("vector", lambda v, o=osl, i_=psl: v.tensor_tensor(o, o, i_, OP.add),
                                    reads=[Pb[ob], o_sb_b], writes=[o_sb_b], inc=True, tag="B_ev")

        EM.emit("vector", lambda v: v.reciprocal(rden[:], pden[:, 0:8]),
                reads=[pden_b], writes=[rden_b], inc=True, tag="B")

        if dbg:
            EM.emit("sync", lambda sy: sy.dma_start(out=d_dbg["d_osb"], in_=o_sb[:]),
                    reads=[o_sb_b], inc=True, tag="dbg")
            EM.emit("sync", lambda sy: sy.dma_start(out=d_dbg["d_den"], in_=rden[:]),
                    reads=[rden_b], inc=True, tag="dbg")

        EM.fence(skip=("tensor",))
        free("p_t8", "v_sl1", "v_sl0", "k_sl1", "k_sl0", "q_fm")
        free("pden")

        a_sb = [alloc(f"a_sb{i}", [128, D], F32) for i in range(3)]
        a_sb_b = bufs(3, "a_sb")
        lnb = [alloc(f"lnb{i}", [128, D], F32) for i in range(2)]
        lnb_b = bufs(2, "lnb")
        xr_t = [alloc("xr_t0", [128, D], F32)]
        xr_t_b = bufs(1, "xr")
        x2_b = bufs(8, "x2")

        for sg in range(8):
            par = sg % 3
            spar = 0
            EM.emit("sync",
                    lambda sy, o=xr_t[spar][:], i_=d_x[sg * 128:(sg + 1) * 128, :]:
                    sy.dma_start(out=o, in_=i_),
                    writes=[xr_t_b[spar]], inc=True, tag="C")
            for cc in range(8):
                bank = cc % 2
                for dhc in range(8):
                    EM.emit("tensor",
                            lambda t, o=P[bank][:, 0:512],
                            l=o_sb[:, dhc * T + sg * 128: dhc * T + (sg + 1) * 128],
                            r=wo_bf[:, dhc * D + cc * 512: dhc * D + (cc + 1) * 512],
                            st_=(dhc == 0), sp=(dhc == 7):
                            t.matmul(o, l, r, start=st_, stop=sp, skip_group_check=True),
                            reads=[o_sb_b, wo_b[dhc]], writes=[Pb[bank]],
                            inc=(dhc == 7), record=False, tag="C_mm")
                EM.note_readers([o_sb_b] + wo_b, "tensor")
                EM.emit("scalar",
                        lambda s, o=a_sb[par][:, cc * 512:(cc + 1) * 512],
                        i_=P[bank][:, 0:512], sc=rden[:, sg:sg + 1]:
                        s.activation(o, i_, AF_COPY, scale=sc),
                        reads=[Pb[bank], rden_b], writes=[a_sb_b[par]], inc=True, tag="C_mm")
            EM.emit("vector", lambda v, o=a_sb[par][:], b=bo_bc[:]:
                    v.tensor_tensor(o, o, b, OP.add),
                    reads=[a_sb_b[par], bo_b], writes=[a_sb_b[par]], inc=True, tag="C")
            if dbg and sg == 0:
                EM.emit("sync", lambda sy: sy.dma_start(out=d_dbg["d_a0"], in_=a_sb[0][:]),
                        reads=[a_sb_b[0]], inc=True, tag="dbg")
            st = stn[sg % 2]
            lp = sg % 2
            ln_stats(st, a_sb[par][:], a_sb_b[par], "C")
            ln_norm(st, lnb[lp][:], lnb_b[lp], a_sb[par][:], a_sb_b[par], "C")
            EM.emit("vector", lambda v, o=lnb[lp][:], x_=xr_t[spar][:]:
                    v.tensor_tensor(o, o, x_, OP.add),
                    reads=[lnb_b[lp], xr_t_b[spar]], writes=[lnb_b[lp]], inc=True, tag="C")
            EM.emit("gpsimd",
                    lambda g, o=d_x2[sg * 128:(sg + 1) * 128, :], i_=lnb[lp][:]:
                    g.dma_start(out=o, in_=i_),
                    reads=[lnb_b[lp]], writes=[x2_b[sg]], inc=True, tag="C")

        EM.fence()
        free("xr_t0", "lnb1", "lnb0", "a_sb2", "a_sb1", "a_sb0",
             "rden", "bo_bc", "o_sb", "wo_bf")

        ln3_fm = alloc("ln3_fm", [128, 32 * T], BF16, side="right")
        ln3_fm_b = Buf("ln3_fm")
        hsb = alloc("hsb", [128, FHC * T], BF16)
        hsb_b = Buf("hsb")
        x2t = [alloc(f"x2t{i}", [128, D], BF16) for i in range(2)]
        x2t_b = bufs(2, "x2t")
        ln3t = [alloc(f"ln3t{i}", [128, D], BF16) for i in range(2)]
        ln3t_b = bufs(2, "ln3t")
        ptr2 = [alloc(f"ptr2_{i}", [128, 1024], BF16, "psum") for i in range(2)]
        ptr2b = bufs(2, "ptr2")

        for sg in range(8):
            par = sg % 2
            EM.emit("sync",
                    lambda sy, o=x2t[par][:], i_=d_x2[sg * 128:(sg + 1) * 128, :]:
                    sy.dma_start(out=o, in_=i_),
                    reads=[x2_b[sg]], writes=[x2t_b[par]], inc=True, tag="Dp")
            st = stn[par]
            ln_stats(st, x2t[par][:], x2t_b[par], "Dp")
            ln_norm(st, ln3t[par][:], ln3t_b[par], x2t[par][:], x2t_b[par], "Dp")
            transpose_block(
                ptr2, ptr2b, ln3t[par][:], ln3t_b[par], 32,
                lambda dc, sg=sg: (ln3_fm[:, dc * T + sg * 128: dc * T + (sg + 1) * 128], ln3_fm_b),
                "Dp_tr")
        if dbg:
            EM.emit("sync", lambda sy: sy.dma_start(out=d_dbg["d_ln3"], in_=ln3_fm[:]),
                    reads=[ln3_fm_b], inc=True, tag="dbg")

        EM.fence()
        free("ptr2_1", "ptr2_0")
        free("ln3t1", "ln3t0", "x2t1", "x2t0")
        y0_b = {}
        ysb_b = bufs(8, "ysb")
        WSEG = 22 * 512

        for fh in range(2):
            if fh == 1:
                EM.fence()
            wg_sl = [alloc(f"wg_sl{fh}_{i}", [128, D], BF16) for i in range(2)]
            wg_sl_b = bufs(2, "wg_sl")
            wu_sl = [alloc(f"wu_sl{fh}_{i}", [128, D], BF16) for i in range(2)]
            wu_sl_b = bufs(2, "wu_sl")
            sg_t = [alloc(f"sg_t{fh}_{i}", [128, 512], F32) for i in range(2)]
            sg_t_b = bufs(2, "sg_t")
            for fc in range(FHC):
                fg = fh * FHC + fc
                par = fc % 2
                EM.emit("gpsimd",
                        lambda g, o=wg_sl[par][:], i_=d_wg[:, fg * D:(fg + 1) * D]:
                        g.dma_start(out=o, in_=i_),
                        writes=[wg_sl_b[par]], inc=True, tag="D_gu")
                EM.emit("gpsimd",
                        lambda g, o=wu_sl[par][:], i_=d_wu[:, fg * D:(fg + 1) * D]:
                        g.dma_start(out=o, in_=i_),
                        writes=[wu_sl_b[par]], inc=True, tag="D_gu")
                for grp in range(2):
                    gb, ub = grp, 2 + grp
                    for dc in range(32):
                        EM.emit("tensor",
                                lambda t, o=P[gb][:, 0:512],
                                l=wg_sl[par][:, dc * 128:(dc + 1) * 128],
                                r=ln3_fm[:, dc * T + grp * 512: dc * T + grp * 512 + 512],
                                st_=(dc == 0), sp=(dc == 31):
                                t.matmul(o, l, r, start=st_, stop=sp, skip_group_check=True),
                                reads=[wg_sl_b[par], ln3_fm_b], writes=[Pb[gb]],
                                inc=(dc == 31), record=False, tag="D_gu")
                    for dc in range(32):
                        EM.emit("tensor",
                                lambda t, o=P[ub][:, 0:512],
                                l=wu_sl[par][:, dc * 128:(dc + 1) * 128],
                                r=ln3_fm[:, dc * T + grp * 512: dc * T + grp * 512 + 512],
                                st_=(dc == 0), sp=(dc == 31):
                                t.matmul(o, l, r, start=st_, stop=sp, skip_group_check=True),
                                reads=[wu_sl_b[par], ln3_fm_b], writes=[Pb[ub]],
                                inc=(dc == 31), record=False, tag="D_gu")
                    EM.note_readers([wg_sl_b[par], wu_sl_b[par], ln3_fm_b], "tensor")
                    EM.emit("scalar",
                            lambda s, o=sg_t[grp][:], i_=P[gb][:, 0:512]:
                            s.activation(o, i_, AF.Silu),
                            reads=[Pb[gb]], writes=[sg_t_b[grp]], inc=True, tag="D_gu")
                    EM.emit("vector",
                            lambda v, o=hsb[:, fc * T + grp * 512: fc * T + grp * 512 + 512],
                            a=sg_t[grp][:], b=P[ub][:, 0:512]:
                            v.tensor_tensor(o, a, b, OP.mult),
                            reads=[sg_t_b[grp], Pb[ub]], writes=[hsb_b], inc=True, tag="D_gu")
            if dbg and fh == 0:
                EM.emit("sync", lambda sy: sy.dma_start(out=d_dbg["d_h0"], in_=hsb[:]),
                        reads=[hsb_b], inc=True, tag="dbg")
            EM.fence()
            free(f"sg_t{fh}_1", f"sg_t{fh}_0", f"wu_sl{fh}_1", f"wu_sl{fh}_0",
                 f"wg_sl{fh}_1", f"wg_sl{fh}_0")
            if fh == 0:
                wd_sl = [alloc(f"wd_sl0_{i}", [128, WSEG], BF16) for i in range(2)]
                wd_sl_b = bufs(2, "wd_sl")
                y0t = [alloc(f"y0t{i}", [128, 512], BF16) for i in range(4)]
                y0t_b = bufs(4, "y0t")
            else:
                free("ln3_fm")
                ysb = alloc("ysb", [128, 8 * D], BF16, side="right")
                wd_sl = [alloc(f"wd_sl1_{i}", [128, WSEG], BF16) for i in range(2)]
                wd_sl_b = bufs(2, "wd_sl")
                y0r = [alloc(f"y0r{i}", [128, 512], BF16) for i in range(4)]
                y0r_b = bufs(4, "y0r")
            y67 = [alloc(f"y67_{fh}_{i}", [128, 512], F32, "psum") for i in range(2)]
            PB8 = P + y67
            Pb8 = Pb + bufs(2, f"y67_{fh}_")

            def y0r_load(sub, cbk):
                par2 = sub % 4
                EM.emit("sync",
                        lambda sy, o=y0r[par2][:],
                        i_=d_y0[sub * 128:(sub + 1) * 128, cbk * 512:(cbk + 1) * 512]:
                        sy.dma_start(out=o, in_=i_),
                        reads=[y0_b[(sub, cbk)]], writes=[y0r_b[par2]], inc=True, tag="D_dn")

            for cbk in range(8):
                if fh == 1:
                    for sub in range(4):
                        y0r_load(sub, cbk)
                for h2 in range(2):
                    seg = 22 if h2 == 0 else 21
                    spar = (cbk * 2 + h2) % 2
                    col0 = (fh * FHC + h2 * 22) * 512
                    EM.emit("gpsimd",
                            lambda g, o=wd_sl[spar][:, 0:seg * 512],
                            i_=d_wd[cbk * 128:(cbk + 1) * 128, col0:col0 + seg * 512]:
                            g.dma_start(out=o, in_=i_),
                            writes=[wd_sl_b[spar]], inc=True, tag="D_dn")
                    for fcl in range(seg):
                        fc = h2 * 22 + fcl
                        for sub in range(8):
                            EM.emit("tensor",
                                    lambda t, o=PB8[sub][:, 0:512],
                                    l=hsb[:, fc * T + sub * 128: fc * T + (sub + 1) * 128],
                                    r=wd_sl[spar][:, fcl * 512:(fcl + 1) * 512],
                                    st_=(fc == 0), sp=(fc == FHC - 1):
                                    t.matmul(o, l, r, start=st_, stop=sp, skip_group_check=True),
                                    reads=[hsb_b, wd_sl_b[spar]], writes=[Pb8[sub]],
                                    inc=(sub == 7 or fc == FHC - 1), record=False, tag="D_dn")
                    EM.note_readers([hsb_b, wd_sl_b[spar]], "tensor")
                if fh == 0:
                    for sub in range(8):
                        yt = sub % 4
                        eng = "scalar" if sub % 2 == 0 else "vector"
                        if eng == "scalar":
                            EM.emit("scalar",
                                    lambda s, o=y0t[yt][:], i_=PB8[sub][:, 0:512]:
                                    s.activation(o, i_, AF_COPY),
                                    reads=[Pb8[sub]], writes=[y0t_b[yt]], inc=True, tag="D_dn")
                        else:
                            EM.emit("vector",
                                    lambda v, o=y0t[yt][:], i_=PB8[sub][:, 0:512]:
                                    v.tensor_copy(o, i_),
                                    reads=[Pb8[sub]], writes=[y0t_b[yt]], inc=True, tag="D_dn")
                        b = Buf(f"y0_{sub}_{cbk}")
                        y0_b[(sub, cbk)] = b
                        EM.emit("sync",
                                lambda sy, o=d_y0[sub * 128:(sub + 1) * 128, cbk * 512:(cbk + 1) * 512],
                                i_=y0t[yt][:]:
                                sy.dma_start(out=o, in_=i_),
                                reads=[y0t_b[yt]], writes=[b], inc=True, tag="D_dn")
                else:
                    for sub in range(8):
                        ysl = ysb[:, sub * D + cbk * 512: sub * D + (cbk + 1) * 512]
                        if sub % 2 == 0:
                            EM.emit("scalar",
                                    lambda s, o=ysl, i_=PB8[sub][:, 0:512]:
                                    s.activation(o, i_, AF_COPY),
                                    reads=[Pb8[sub]], writes=[ysb_b[sub]], inc=True, tag="D_dn")
                        else:
                            EM.emit("vector",
                                    lambda v, o=ysl, i_=PB8[sub][:, 0:512]:
                                    v.tensor_copy(o, i_),
                                    reads=[Pb8[sub]], writes=[ysb_b[sub]], inc=True, tag="D_dn")
                    for sub in range(8):
                        par2 = sub % 4
                        ysl = ysb[:, sub * D + cbk * 512: sub * D + (cbk + 1) * 512]
                        EM.emit("vector",
                                lambda v, o=ysl, a=y0r[par2][:]:
                                v.tensor_tensor(o, o, a, OP.add),
                                reads=[ysb_b[sub], y0r_b[par2]], writes=[ysb_b[sub]], inc=True, tag="D_dn")
                        if sub < 4:
                            y0r_load(sub + 4, cbk)
            free("y67_%d_1" % fh, "y67_%d_0" % fh)
            if fh == 0:
                free("y0t3", "y0t2", "y0t1", "y0t0", "wd_sl0_1", "wd_sl0_0")

        if dbg:
            EM.emit("sync", lambda sy: sy.dma_start(out=d_dbg["d_ysb"], in_=ysb[:]),
                    reads=ysb_b, inc=True, tag="dbg")

        EM.fence()
        free("y0r3", "y0r2", "y0r1", "y0r0", "wd_sl1_1", "wd_sl1_0", "hsb")
        x2f = [alloc(f"x2f{i}", [128, D], BF16) for i in range(4)]
        x2f_b = bufs(4, "x2f")
        outf = [alloc(f"outf{i}", [128, D], F32) for i in range(2)]
        outf_b = bufs(2, "outf")
        out_done = bufs(8, "out")

        def x2f_load(sub):
            EM.emit("sync",
                    lambda sy, o=x2f[sub % 4][:], i_=d_x2[sub * 128:(sub + 1) * 128, :]:
                    sy.dma_start(out=o, in_=i_),
                    reads=[x2_b[sub]], writes=[x2f_b[sub % 4]], inc=True, tag="E")

        for sub in range(4):
            x2f_load(sub)
        for sub in range(8):
            par = sub % 4
            st = stn[sub % 2]
            ysl = ysb[:, sub * D:(sub + 1) * D]
            ln_stats(st, ysl, ysb_b[sub], "E")
            ln_norm(st, outf[sub % 2][:], outf_b[sub % 2], ysl, ysb_b[sub], "E")
            EM.emit("vector", lambda v, o=outf[sub % 2][:], x_=x2f[par][:]:
                    v.tensor_tensor(o, o, x_, OP.add),
                    reads=[outf_b[sub % 2], x2f_b[par]], writes=[outf_b[sub % 2]], inc=True, tag="E")
            if sub < 4:
                x2f_load(sub + 4)
            EM.emit("sync",
                    lambda sy, o=d_out[sub * 128:(sub + 1) * 128, :], i_=outf[sub % 2][:]:
                    sy.dma_start(out=o, in_=i_),
                    reads=[outf_b[sub % 2]], writes=[out_done[sub]], inc=True, tag="E")
        EM.emit("sync", None, reads=out_done, tag="E")

        blk = outer.enter_context(nc.Block())
        EM.replay(nc, blk, sems)
    return nc


def _prep(L0_attn_x, L0_attn_Wqkv, L0_attn_bqkv, L0_attn_Wo, L0_attn_bo,
          L0_ffn_W_gate, L0_ffn_W_up, L0_ffn_W_down):
    x = np.ascontiguousarray(np.asarray(L0_attn_x, dtype=np.float32).reshape(S, D))
    Wqkv = np.asarray(L0_attn_Wqkv, dtype=np.float32)
    wq_t = np.ascontiguousarray(
        Wqkv.reshape(32, 128, 24, 128).transpose(2, 1, 0, 3)).reshape(24 * 128, D)
    bqkv_t = np.ascontiguousarray(
        np.asarray(L0_attn_bqkv, dtype=np.float32).reshape(24, 128).T)
    wo = np.ascontiguousarray(np.asarray(L0_attn_Wo, dtype=np.float32))
    bo_b = np.ascontiguousarray(
        np.broadcast_to(np.asarray(L0_attn_bo, dtype=np.float32), (128, D)))
    wg_t = np.ascontiguousarray(
        np.asarray(L0_ffn_W_gate, dtype=np.float32)
        .reshape(NFC, 128, 32, 128).transpose(3, 0, 2, 1)).reshape(128, NFC * D)
    wu_t = np.ascontiguousarray(
        np.asarray(L0_ffn_W_up, dtype=np.float32)
        .reshape(NFC, 128, 32, 128).transpose(3, 0, 2, 1)).reshape(128, NFC * D)
    wd_t = np.ascontiguousarray(
        np.asarray(L0_ffn_W_down, dtype=np.float32)
        .reshape(8, 512, NFC, 128).transpose(0, 3, 2, 1)).reshape(8 * 128, NFC * 512)
    ident = np.eye(128, dtype=np.float32)
    common = {"wq": wq_t, "bqkv": bqkv_t, "wo": wo, "bo": bo_b,
              "wg": wg_t, "wu": wu_t, "wd": wd_t, "ident": ident}
    return x, common


def kernel(L0_attn_x, L0_attn_Wqkv, L0_attn_bqkv, L0_attn_Wo, L0_attn_bo,
           L0_ffn_W_gate, L0_ffn_W_up, L0_ffn_W_down, _dbg=False):
    x, common = _prep(L0_attn_x, L0_attn_Wqkv, L0_attn_bqkv, L0_attn_Wo,
                      L0_attn_bo, L0_ffn_W_gate, L0_ffn_W_up, L0_ffn_W_down)
    nc = build_program(dbg=_dbg)
    in_maps = [dict(common, x=np.ascontiguousarray(x[c * T:(c + 1) * T]))
               for c in range(8)]
    res = run_bass_kernel_spmd(nc, in_maps, core_ids=list(range(8)))
    out = np.concatenate([res.results[c]["out"] for c in range(8)], axis=0)
    out = out.reshape(1, S, D).astype(np.float32)
    if _dbg:
        return out, res
    return out
